# version 07
# baseline (speedup 1.0000x reference)
import math
import os
from contextlib import ExitStack

import numpy as np
import ml_dtypes

import concourse.bass as bass
import concourse.mybir as mybir
from concourse.bass_utils import run_bass_kernel_spmd

F32 = mybir.dt.float32
BF16 = mybir.dt.bfloat16
I32 = mybir.dt.int32
AF = mybir.ActivationFunctionType
ALU = mybir.AluOpType
AX = mybir.AxisListType

D = 2048
T = 2048
TT = 4096
NH = 8
HD = 128
NDC = D // 128
NTC = T // 512
NTTC = TT // 512
SCALE = 1.0 / math.sqrt(HD)
EPS = 1e-5
N_SUB = int(os.environ.get("MK_NSUB", "4"))

ENGINES = ("t", "a", "v", "g", "y")


class Plan:
    def __init__(self):
        self.q = {e: [] for e in ENGINES}
        self.semv = {}
        self.seen = {e: {} for e in ENGINES}

    def val(self, sem):
        return self.semv.get(sem.name, 0)

    def inc(self, sem, n):
        v = self.val(sem) + n
        self.semv[sem.name] = v
        return v

    def wait(self, e, sem, v=None):
        if v is None:
            v = self.val(sem)
        if v <= 0 or self.seen[e].get(sem.name, -1) >= v:
            return
        self.seen[e][sem.name] = v
        self.q[e].append(lambda eng, s=sem, vv=v: eng.wait_ge(s, vv))

    def op(self, e, fn, inc=None, n=1):
        if inc is not None:
            self.inc(inc, n)
        self.q[e].append(
            lambda eng, f=fn, s=inc, nn=n: (f(eng).then_inc(s, nn) if s is not None else f(eng))
        )

    def dma(self, e, sem, out, in_, accum=False):
        if accum:
            self.op(e, lambda eng, o=out, i=in_: eng.dma_start(out=o, in_=i, accum_op=ALU.add),
                    inc=sem, n=16)
        else:
            self.op(e, lambda eng, o=out, i=in_: eng.dma_start(out=o, in_=i), inc=sem, n=16)
        return self.val(sem)


def build_nc():
    nc = bass.Bass("TRN2", target_bir_lowering=False)

    din = {}
    def inp(name, shape, dt=F32):
        din[name] = nc.dram_tensor(name, list(shape), dt, kind="ExternalInput").ap()
        return din[name]

    x0 = inp("x0", (D, T))
    inp("wqa", (NH, 128, NDC * 128)); inp("wka", (NH, 128, NDC * 128))
    inp("wva", (128, NDC * NH * 128))
    inp("woa", (16, 128, NDC * 128))
    inp("bqa", (128, NH)); inp("bka", (128, NH))
    inp("bva", (1, NH * 128)); inp("boa", (128, 16))
    inp("wqb", (NH, 128, NDC * 128)); inp("wkb", (NH, 128, NDC * 128))
    inp("wvb", (128, NDC * NH * 128))
    inp("wob", (16, 128, NDC * 128))
    inp("wg", (64, 128, NDC * 128)); inp("wu", (64, 128, NDC * 128))
    inp("wd", (16, 128, 64 * 128))
    inp("alibi", (128, NH * 32))
    inp("csc", (128, TT), BF16); inp("css", (128, TT), BF16)
    inp("pswap", (128, 128), BF16)
    inp("maskc", (128, 896), BF16)
    inp("colsel", (1, 1), I32)
    inp("wal", (NH, 1, TT), BF16)

    out_t = nc.dram_tensor("out", [D, T], F32, kind="ExternalOutput").ap()

    y_scr = nc.dram_tensor("y_scr", [D, T], BF16).ap()
    y_gth = nc.dram_tensor("y_gth", [2 * D, T], BF16).ap()
    q_scr = nc.dram_tensor("q_scr", [NH * 128, TT], BF16).ap()
    k_scr = nc.dram_tensor("k_scr", [NH * 128, TT], BF16).ap()
    v_scr = nc.dram_tensor("v_scr", [TT, NH * 128], BF16).ap()
    c_scr = nc.dram_tensor("c_scr", [NH * 128, TT], BF16).ap()
    c_gth = nc.dram_tensor("c_gth", [D, TT], BF16).ap()
    x_scr = [nc.dram_tensor(f"x{i}_scr", [D, T], F32).ap() for i in (1, 2, 3)]

    es = ExitStack()
    sb = lambda name, shape, dt: es.enter_context(nc.sbuf_tensor(name, list(shape), dt))

    ar1 = sb("ar1", (128, 32768), BF16)
    ar2 = sb("ar2", (128, 32768), BF16)
    wst = sb("wst", (128, 6144), BF16)
    ones_c = sb("ones_c", (128, 1), BF16)
    ones_rb = sb("ones_rb", (1, 128), BF16)
    ones_rf = sb("ones_rf", (1, 128), F32)
    m_row = sb("m_row", (1, T), F32)
    q_row = sb("q_row", (1, T), F32)
    r_row = sb("r_row", (1, T), F32)
    _ = r_row
    mrep = sb("mrep", (128, 512), F32)
    rrep = sb("rrep", (128, 512), F32)
    xst = sb("xst", (128, 2048), F32)
    sqst = sb("sqst", (128, 1024), BF16)
    fst = sb("fst", (128, 1024), F32)
    fst2 = sb("fst2", (128, 1024), F32)
    btl = sb("btl", (128, 6 * 512), BF16)
    ptl = sb("ptl", (128, 2048), BF16)
    sgt = sb("sgt", (128, 2048), BF16)
    biassl = sb("biassl", (128, 64), F32)
    mcol = sb("mcol", (128, 2), F32)
    alib = sb("alib", (128, NH * 32), F32)
    maskb = sb("maskb", (128, 896), BF16)
    pswb = sb("pswb", (128, 128), BF16)
    bqb = sb("bqb", (128, NH), F32)
    bkb = sb("bkb", (128, NH), F32)
    bvb = sb("bvb", (1, NH * 128), BF16)
    bob = sb("bob", (128, 16), F32)
    colsb = sb("colsb", (1, 1), I32)
    cells = sb("cells", (1, 64), F32)
    rowt = r_row[0:1, 1536:2048]
    psA = es.enter_context(nc.psum_tensor("psA", [128, 2048], F32))
    psB = es.enter_context(nc.psum_tensor("psB", [128, 2048], F32))

    dW = es.enter_context(nc.semaphore("dW"))
    dY = es.enter_context(nc.semaphore("dY"))
    dS = es.enter_context(nc.semaphore("dS"))
    cT = es.enter_context(nc.semaphore("cT"))
    cA = es.enter_context(nc.semaphore("cA"))
    cV = es.enter_context(nc.semaphore("cV"))
    cc = es.enter_context(nc.semaphore("ccs"))
    ini = es.enter_context(nc.semaphore("ini"))

    P = Plan()

    rreg = nc.alloc_register(mybir.EngineType.SP, "rctx")
    rtmp = nc.alloc_register(mybir.EngineType.SP, "rtmp")

    def sl512(t_, i):
        return t_[:, i * 512:(i + 1) * 512]

    P.op("g", lambda g: g.memset(ones_c[:, :], 1.0), inc=ini)
    P.op("g", lambda g: g.memset(ones_rb[:, :], 1.0), inc=ini)
    P.op("g", lambda g: g.memset(ones_rf[:, :], 1.0), inc=ini)
    for e_ in ("t", "a", "v", "y"):
        P.wait(e_, ini)
    P.dma("g", dW, maskb[:, :], din["maskc"])
    P.dma("g", dW, pswb[:, :], din["pswap"])
    P.dma("g", dW, colsb[:, :], din["colsel"])
    P.dma("g", dW, bqb[:, :], din["bqa"])
    P.dma("g", dW, bkb[:, :], din["bka"])
    P.dma("g", dW, bvb[:, :], din["bva"])
    P.dma("g", dW, bob[:, :], din["boa"])
    v_colsel = P.val(dW)
    P.wait("y", dW, v_colsel)
    P.q["y"].append(lambda sy: sy.reg_load(rreg, colsb[0:1, 0:1]))

    def barrier_dma(e):
        for s in (dW, dY, dS):
            P.wait(e, s)
        P.wait(e, cc)

    a1v = ar1[:, :].rearrange("p (dc t) -> p dc t", dc=NDC)

    def stats(x_ap):
        sum_r = psA[0:1, 0:512]
        sq_r = psA[0:1, 512:1024]
        P.wait("v", cT)
        P.wait("y", dW)
        P.wait("a", dW)
        P.wait("a", dY)
        slot_cv = [0] * 4
        slot_ca = [0] * 4
        sq_ct = [0, 0]
        row_cv = 0
        for tc in range(NTC):
            if tc > 0:
                P.wait("t", cV, row_cv)
            for dc in range(NDC):
                k = tc * NDC + dc
                i = k % 4
                qe, qs = ("y", dY) if k % 2 == 0 else ("a", dS)
                P.wait(qe, cV, slot_cv[i])
                P.wait(qe, cA, slot_ca[i])
                vx = P.dma(qe, qs, sl512(xst, i),
                           x_ap[dc * 128:(dc + 1) * 128, tc * 512:(tc + 1) * 512])
                P.wait("v", qs, vx)
                P.op("v", lambda v, i=i, dc=dc, tc=tc: v.tensor_copy(
                    a1v[:, dc, tc * 512:(tc + 1) * 512], sl512(xst, i)), inc=cV)
                slot_cv[i] = P.val(cV)
                P.wait("a", qs, vx)
                P.wait("a", cT, sq_ct[k % 2])
                P.op("a", lambda a, i=i, k=k: a.activation(
                    sl512(sqst, k % 2), sl512(xst, i), AF.Square), inc=cA)
                slot_ca[i] = P.val(cA)
                P.wait("t", cV, slot_cv[i])
                P.wait("t", cA, slot_ca[i])
                P.op("t", lambda t, dc=dc, tc=tc: t.matmul(
                    sum_r, ones_c[:, :], a1v[:, dc, tc * 512:(tc + 1) * 512],
                    start=(dc == 0), stop=(dc == NDC - 1)), inc=cT)
                P.op("t", lambda t, k=k, dc=dc: t.matmul(
                    sq_r, ones_c[:, :], sl512(sqst, k % 2),
                    start=(dc == 0), stop=(dc == NDC - 1)), inc=cT)
                sq_ct[k % 2] = P.val(cT)
            P.wait("v", cT)
            P.op("v", lambda v, tc=tc: v.tensor_scalar(
                m_row[0:1, tc * 512:(tc + 1) * 512], sum_r, 1.0 / D, None, ALU.mult), inc=cV)
            P.op("v", lambda v, tc=tc: v.tensor_scalar(
                q_row[0:1, tc * 512:(tc + 1) * 512], sq_r, 1.0 / D, None, ALU.mult), inc=cV)
            row_cv = P.val(cV)
        P.op("v", lambda v: v.tensor_tensor(r_row[:, :], m_row[:, :], m_row[:, :], ALU.mult), inc=cV)
        P.wait("v", cV)
        P.op("v", lambda v: v.tensor_tensor(q_row[:, :], q_row[:, :], r_row[:, :], ALU.subtract), inc=cV)
        P.wait("v", cV)
        P.op("v", lambda v: v.tensor_scalar(q_row[:, :], q_row[:, :], float(EPS), None, ALU.add), inc=cV)
        P.wait("a", cV)
        P.op("a", lambda a: a.activation(q_row[:, :], q_row[:, :], AF.Ln), inc=cA)
        P.wait("a", cA)
        P.op("a", lambda a: a.activation(r_row[:, :], q_row[:, :], AF.Exp, scale=-0.5), inc=cA)
        return slot_cv, slot_ca

    def replicate_tc(tc):
        P.wait("t", cA)
        P.wait("t", cV)
        P.op("t", lambda t, tc=tc: t.matmul(
            psB[:, 0:512], ones_rf[:, :], m_row[0:1, tc * 512:(tc + 1) * 512],
            start=True, stop=True), inc=cT)
        P.op("t", lambda t, tc=tc: t.matmul(
            psB[:, 512:1024], ones_rf[:, :], r_row[0:1, tc * 512:(tc + 1) * 512],
            start=True, stop=True), inc=cT)
        P.wait("a", cT)
        P.wait("a", cV)
        P.op("a", lambda a: a.copy(mrep[:, :], psB[:, 0:512]), inc=cA)
        P.op("a", lambda a: a.copy(rrep[:, :], psB[:, 512:1024]), inc=cA)
        return P.val(cA)

    def y_produce(x_ap, dst_dram, st_cv, st_ca):
        slot_cv = list(st_cv)
        slot_ca = list(st_ca)
        for tc in range(NTC):
            v_rep = replicate_tc(tc)
            for dc in range(NDC):
                k = tc * NDC + dc
                i = k % 4
                qe, qs = ("y", dY) if k % 2 == 0 else ("a", dS)
                P.wait(qe, cV, slot_cv[i])
                P.wait(qe, cA, slot_ca[i])
                slot_ca[i] = 0
                vx = P.dma(qe, qs, sl512(xst, i),
                           x_ap[dc * 128:(dc + 1) * 128, tc * 512:(tc + 1) * 512])
                P.wait("v", qs, vx)
                P.wait("v", cA, v_rep)
                P.op("v", lambda v, i=i, k=k, dc=dc, tc=tc: v.tensor_tensor(
                    sl512(fst, k % 2), sl512(xst, i), mrep[:, :], ALU.subtract), inc=cV)
                P.op("v", lambda v, k=k, dc=dc, tc=tc: v.tensor_tensor(
                    a1v[:, dc, tc * 512:(tc + 1) * 512], sl512(fst, k % 2), rrep[:, :],
                    ALU.mult), inc=cV)
                slot_cv[i] = P.val(cV)
                if dst_dram is not None:
                    P.wait(qe, cV, slot_cv[i])
                    P.dma(qe, qs, dst_dram[dc * 128:(dc + 1) * 128, tc * 512:(tc + 1) * 512],
                          a1v[:, dc, tc * 512:(tc + 1) * 512])

    wslot = [wst[:, i * 2048:(i + 1) * 2048] for i in range(3)]

    def wload(slot, src_ap):
        return P.dma("g", dW, slot, src_ap)

    def attn_sublayer(x_prev, x_next, is_b):
        wq_i, wk_i, wv_i, wo_i = (
            (din["wqb"], din["wkb"], din["wvb"], din["wob"]) if is_b else
            (din["wqa"], din["wka"], din["wva"], din["woa"]))

        if is_b:
            P.op("g", lambda g: g.memset(alib[:, :], 0.0), inc=ini)
        else:
            P.dma("g", dW, alib[:, :], din["alibi"])

        st_cv, st_ca = stats(x_prev)
        y_produce(x_prev, y_scr, st_cv, st_ca)
        v_ywr = P.val(dY)
        v_ysr = P.val(dS)
        P.wait("g", dY, v_ywr)
        P.wait("g", dS, v_ysr)
        P.op("g", lambda g: g.collective_compute(
            "AllGather", ALU.bypass,
            replica_groups=[[0, 1], [2, 3], [4, 5], [6, 7]],
            ins=[y_scr], outs=[y_gth]), inc=cc)
        v_cc = P.val(cc)

        wv_all = ar2[:, 0:16384].rearrange("p (dc e) -> p dc e", dc=NDC)
        qraw_st = [ar2[:, 16384 + i * 512:16384 + (i + 1) * 512] for i in range(2)]
        csc_sb = ar2[:, 20480:24576]
        css_sb = ar2[:, 24576:28672]
        P.wait("g", cT)
        if is_b:
            P.dma("g", dW, csc_sb, din["csc"])
            P.dma("g", dW, css_sb, din["css"])
        v_wv = P.dma("g", dW, wv_all, wv_i.rearrange("p (dc e) -> p dc e", dc=NDC))
        P.op("g", lambda g: g.memset(cells[:, :], 0.0), inc=ini)
        v_cells0 = P.val(ini)
        P.wait("v", ini, v_cells0)

        yslab = ar1[:, :].rearrange("p (dc t) -> p dc t", dc=NDC)

        def proj_evac_qk(quad, oc, sl, tq, which, i):
            tile = sl512(btl, i % 6)
            col = slice(sl * 2048 + tq * 512, sl * 2048 + (tq + 1) * 512)
            scr = k_scr if which == "k" else q_scr
            bias_sb = bkb if which == "k" else bqb
            if not is_b:
                P.wait("a", cT)
                P.wait("a", dY)
                P.op("a", lambda a, quad=quad, tq=tq, tile=tile, oc=oc, bias_sb=bias_sb: a.add(
                    tile, sl512(quad, tq), bias_sb[:, oc:oc + 1]), inc=cA)
                va = P.val(cA)
                P.wait("y", cA, va)
                P.dma("y", dY, scr[oc * 128:(oc + 1) * 128, col], tile)
            else:
                qraw = qraw_st[i % 2]
                P.wait("a", cT)
                P.wait("a", cV)
                P.op("a", lambda a, quad=quad, tq=tq, qraw=qraw: a.copy(
                    qraw, sl512(quad, tq)), inc=cA)
                va = P.val(cA)
                P.wait("t", cA, va)
                P.op("t", lambda t, quad=quad, tq=tq, qraw=qraw: t.matmul(
                    sl512(quad, tq), pswb[:, :], qraw, start=True, stop=True), inc=cT)
                vt = P.val(cT)
                P.wait("v", cA, va)
                P.wait("v", cA)
                P.wait("v", dY)
                P.wait("v", dS)
                P.op("v", lambda v, qraw=qraw, c=col: v.tensor_tensor(
                    sl512(xst, 0), qraw, csc_sb[:, c], ALU.mult), inc=cV)
                P.wait("v", cT, vt)
                P.op("v", lambda v, quad=quad, tq=tq, c=col: v.tensor_tensor(
                    sl512(xst, 1), sl512(quad, tq), css_sb[:, c], ALU.mult), inc=cV)
                P.wait("v", cV)
                P.op("v", lambda v, tile=tile: v.tensor_tensor(
                    tile, sl512(xst, 0), sl512(xst, 1), ALU.add), inc=cV)
                vv = P.val(cV)
                P.wait("y", cV, vv)
                P.dma("y", dY, scr[oc * 128:(oc + 1) * 128, col], tile)
            cidx = (0 if which == "q" else 8) + oc
            P.wait("v", cA)
            P.wait("v", cV)
            P.op("v", lambda v, i=i, tile=tile: v.tensor_tensor(
                sl512(sqst, i % 2), tile, tile, ALU.mult), inc=cV)
            vsq = P.val(cV)
            P.wait("t", cV, vsq)
            P.op("t", lambda t, quad=quad, tq=tq, i=i: t.matmul(
                quad[0:1, tq * 512:(tq + 1) * 512], ones_c[:, :], sl512(sqst, i % 2),
                start=True, stop=True), inc=cT)
            vnm = P.val(cT)
            P.wait("v", cT, vnm)
            P.op("v", lambda v, quad=quad, tq=tq: v.reduce_max(
                rowt[0:1, 0:1], quad[0:1, tq * 512:(tq + 1) * 512], axis=AX.X), inc=cV)
            P.wait("v", cV)
            P.op("v", lambda v, cidx=cidx: v.tensor_tensor(
                cells[0:1, cidx:cidx + 1], cells[0:1, cidx:cidx + 1], rowt[0:1, 0:1],
                ALU.max), inc=cV)

        for sl in range(2):
            P.wait("y", cc, v_cc)
            P.wait("y", cT)
            src = y_gth[sl * D:(sl + 1) * D, :].rearrange("(dc p) t -> p dc t", p=128)
            for half in range(2):
                P.dma("y", dY, yslab[:, half * 8:(half + 1) * 8, :],
                      src[:, half * 8:(half + 1) * 8, :])
            vy = P.val(dY)
            P.wait("t", dY, vy)

            for which in ("q", "k"):
                w_in = wq_i if which == "q" else wk_i
                slots = (wslot[0], wslot[2]) if which == "q" else (wslot[1], wslot[2])
                for oc in range(NH):
                    s_ = slots[oc % 2]
                    P.wait("g", cT)
                    vw = wload(s_, w_in[oc])
                    quad = psA if oc % 2 == 0 else psB
                    P.wait("t", dW, vw)
                    P.wait("t", cA)
                    P.wait("t", cV)
                    for dc in range(NDC):
                        for tq in range(4):
                            P.op("t", lambda t, quad=quad, s_=s_, dc=dc, tq=tq: t.matmul(
                                sl512(quad, tq), s_[:, dc * 128:(dc + 1) * 128],
                                yslab[:, dc, tq * 512:(tq + 1) * 512],
                                start=(dc == 0), stop=(dc == NDC - 1)),
                                inc=(cT if (dc == NDC - 1 and tq == 3) else None))
                    for tq in range(4):
                        proj_evac_qk(quad, oc, sl, tq, which, oc * 4 + tq)

            P.wait("t", dW, v_wv)
            for tcb in range(16):
                quad = psA if tcb % 2 == 0 else psB
                P.wait("t", cA)
                last_dc_stop = is_b
                for dc in range(NDC):
                    for ec in range(2):
                        P.op("t", lambda t, quad=quad, dc=dc, ec=ec, tcb=tcb, lds=last_dc_stop: t.matmul(
                            sl512(quad, ec), yslab[:, dc, tcb * 128:(tcb + 1) * 128],
                            wv_all[:, dc, ec * 512:(ec + 1) * 512],
                            start=(dc == 0),
                            stop=(dc == NDC - 1 and lds)),
                            inc=(cT if (dc == NDC - 1 and ec == 1 and last_dc_stop) else None))
                if not is_b:
                    for ec in range(2):
                        P.op("t", lambda t, quad=quad, ec=ec: t.matmul(
                            sl512(quad, ec), ones_rb[:, :],
                            bvb[0:1, ec * 512:(ec + 1) * 512],
                            start=False, stop=True), inc=(cT if ec == 1 else None))
                vv_ = P.val(cT)
                P.wait("a", cT, vv_)
                for ec in range(2):
                    ti = (tcb * 2 + ec) % 6
                    P.wait("a", dY)
                    P.op("a", lambda a, quad=quad, ec=ec, ti=ti: a.copy(
                        sl512(btl, ti), sl512(quad, ec)), inc=cA)
                    va = P.val(cA)
                    P.wait("y", cA, va)
                    P.dma("y", dY,
                          v_scr[(sl * 2048 + tcb * 128):(sl * 2048 + (tcb + 1) * 128),
                                ec * 512:(ec + 1) * 512], sl512(btl, ti))

        P.wait("v", cT)
        P.wait("v", cV)
        P.op("v", lambda v: v.tensor_tensor(
            cells[0:1, 16:24], cells[0:1, 0:8], cells[0:1, 8:16], ALU.add), inc=cV)
        P.wait("v", cV)
        P.op("v", lambda v: v.tensor_scalar(
            cells[0:1, 16:24], cells[0:1, 16:24], SCALE / 2.0, 8.0, ALU.mult, ALU.add), inc=cV)
        v_M = P.val(cV)

        kh = [ar2[:, 0:4096], ar2[:, 4096:8192]]
        qh = [ar2[:, 8192:12288], ar2[:, 12288:16384]]
        vh = [ar2[:, 16384:20480], ar2[:, 20480:24576]]
        wal = [ar2[0:1, 24576:28672], ar2[0:1, 28672:32768]]

        def head_loads(h):
            i = h % 2
            P.wait("a", cT)
            P.dma("a", dS, kh[i], k_scr[h * 128:(h + 1) * 128, :])
            P.dma("a", dS, qh[i], q_scr[h * 128:(h + 1) * 128, :])
            if not is_b:
                P.wait("a", cV)
                P.dma("a", dS, wal[i], din["wal"][h])
            P.wait("y", cT)
            P.wait("y", cV)
            vsrc = v_scr[:, :].rearrange("(kb p) e -> p kb e", p=128)
            for piece in range(4):
                P.dma("y", dY,
                      vh[i].rearrange("p (kb e) -> p kb e", kb=32)[:, piece * 8:(piece + 1) * 8, :],
                      vsrc[:, piece * 8:(piece + 1) * 8, (h * 128):((h + 1) * 128)])
            return P.val(dS), P.val(dY)

        def build_bias_slab(h):
            P.wait("t", cV, v_M)
            P.wait("t", cA)
            P.op("t", lambda t, h=h: t.matmul(
                psB[:, 1024:1025], ones_rf[:, :], cells[0:1, 16 + h:17 + h],
                start=True, stop=True), inc=cT)
            vt = P.val(cT)
            P.wait("a", cT, vt)
            P.wait("a", cV)
            P.op("a", lambda a, h=h: a.copy(mcol[:, h % 2:h % 2 + 1], psB[:, 1024:1025]), inc=cA)
            va = P.val(cA)
            P.wait("v", cA, va)
            P.op("v", lambda v, h=h: v.tensor_scalar(
                biassl[:, (h % 2) * 32:(h % 2) * 32 + 32], alib[:, h * 32:(h + 1) * 32],
                mcol[:, h % 2:h % 2 + 1], None, ALU.subtract), inc=cV)
            return P.val(cV)

        sems = [None] * (NH + 1)
        bs_v = [None] * (NH + 1)
        sems[0] = head_loads(0)
        bs_v[0] = build_bias_slab(0)
        KBc = [0]
        exp_v = {}
        psb_free = [P.val(cV)]
        for h in range(NH):
            if h + 1 < NH:
                sems[h + 1] = head_loads(h + 1)
                bs_v[h + 1] = build_bias_slab(h + 1)
            vS, vY = sems[h]
            P.wait("t", dS, vS)
            P.wait("t", dY, vY)
            i = h % 2
            for qc in range(NTTC):
                nkb = 4 * (qc + 1)
                kg0 = KBc[0]

                def u_scores(kb, kg, i=i, h=h, qc=qc):
                    sl = sl512(psA, kg % 4)
                    P.wait("t", cA, exp_v.get(kg - 4, 0))
                    P.op("t", lambda t, sl=sl, kb=kb, i=i, qc=qc: t.matmul(
                        sl, kh[i][:, kb * 128:(kb + 1) * 128],
                        qh[i][:, qc * 512:(qc + 1) * 512],
                        start=True, stop=is_b), inc=(cT if is_b else None))
                    if not is_b:
                        P.op("t", lambda t, sl=sl, i=i, qc=qc: t.matmul(
                            sl, ones_rb[:, :], wal[i][0:1, qc * 512:(qc + 1) * 512],
                            start=False, stop=True), inc=cT)
                    vsc = P.val(cT)
                    dj = kb - 4 * qc
                    vmask = None
                    if dj >= 0:
                        P.wait("v", cT, vsc)
                        P.op("v", lambda v, sl=sl, dj=dj: v.tensor_tensor(
                            sl, sl, maskb[:, 384 - 128 * dj:896 - 128 * dj],
                            ALU.add), inc=cV)
                        vmask = P.val(cV)
                    P.wait("a", cT, vsc)
                    if vmask is not None:
                        P.wait("a", cV, vmask)
                    P.wait("a", cV, bs_v[h])
                    P.op("a", lambda a, sl=sl, kg=kg, kb=kb, h=h: a.activation(
                        sl512(ptl, kg % 4), sl,
                        AF.Exp, bias=biassl[:, (h % 2) * 32 + kb:(h % 2) * 32 + kb + 1],
                        scale=SCALE), inc=cA)
                    exp_v[kg] = P.val(cA)

                def u_sums_av(kb, kg, first, i=i):
                    P.wait("t", cA, exp_v[kg])
                    if first:
                        P.wait("t", cV, psb_free[0])
                    P.op("t", lambda t, kg=kg, first=first: t.matmul(
                        psB[0:1, 512:1024], ones_c[:, :], sl512(ptl, kg % 4),
                        start=first, stop=True), inc=None)
                    P.op("t", lambda t, kg=kg, kb=kb, i=i, first=first: t.matmul(
                        psB[:, 0:512], vh[i][:, kb * 128:(kb + 1) * 128],
                        sl512(ptl, kg % 4),
                        start=first, stop=True), inc=cT)

                for kb in range(nkb):
                    u_scores(kb, kg0 + kb)
                    if kb >= 2:
                        u_sums_av(kb - 2, kg0 + kb - 2, kb - 2 == 0)
                for kb in range(max(0, nkb - 2), nkb):
                    u_sums_av(kb, kg0 + kb, kb == 0)
                KBc[0] = kg0 + nkb

                v_avdone = P.val(cT)
                P.wait("v", cT, v_avdone)
                P.op("v", lambda v: v.reciprocal(rowt[:, :], psB[0:1, 512:1024]), inc=cV)
                vrec = P.val(cV)
                P.wait("t", cV, vrec)
                P.op("t", lambda t: t.matmul(
                    psB[:, 1024:1536], ones_rf[:, :], rowt[0:1, :], start=True, stop=True), inc=cT)
                vrep = P.val(cT)
                P.wait("a", cT, vrep)
                P.wait("a", cV)
                P.op("a", lambda a, qc=qc: a.copy(sl512(fst2, qc % 2), psB[:, 1024:1536]), inc=cA)
                vrepc = P.val(cA)
                ctile = sl512(btl, qc % 6)
                P.wait("v", cA, vrepc)
                P.wait("v", dY)
                P.op("v", lambda v, qc=qc, ctile=ctile: v.tensor_tensor(
                    ctile, psB[:, 0:512], sl512(fst2, qc % 2), ALU.mult), inc=cV)
                vctx = P.val(cV)
                psb_free[0] = vctx
                P.wait("y", cV, vctx)
                P.dma("y", dY, c_scr[h * 128:(h + 1) * 128, qc * 512:(qc + 1) * 512], ctile)

        v_cwr = P.val(dY)
        P.wait("g", dY, v_cwr)
        P.wait("g", dS)
        P.op("g", lambda g: g.collective_compute(
            "AllGather", ALU.bypass,
            replica_groups=[[0, 1], [2, 3], [4, 5], [6, 7]],
            ins=[c_scr], outs=[c_gth]), inc=cc)
        v_cc2 = P.val(cc)

        ctxb = ar1[:, :].rearrange("p (dc t) -> p dc t", dc=NDC)
        P.wait("y", cc, v_cc2)
        P.wait("y", cT)
        for piece in range(4):
            def dma_piece(sy, piece=piece):
                sy.reg_add(rtmp, rreg, piece * 4 * 128 * TT)
                return sy.dma_start(
                    out=ctxb[:, piece * 4:(piece + 1) * 4, :],
                    in_=bass.AP(c_gth.tensor, rtmp, [[TT, 128], [128 * TT, 4], [1, T]]))
            P.op("y", dma_piece, inc=dY, n=16)
        v_ctxld = P.val(dY)
        P.wait("t", dY, v_ctxld)

        for oc in range(16):
            s_ = wslot[oc % 2]
            P.wait("g", cT)
            vw = wload(s_, wo_i[oc])
            quad = psA if oc % 2 == 0 else psB
            P.wait("t", dW, vw)
            P.wait("t", cV)
            P.wait("t", cA)
            for dc in range(NDC):
                for tq in range(4):
                    P.op("t", lambda t, quad=quad, s_=s_, dc=dc, tq=tq: t.matmul(
                        sl512(quad, tq), s_[:, dc * 128:(dc + 1) * 128],
                        ctxb[:, dc, tq * 512:(tq + 1) * 512],
                        start=(dc == 0), stop=(dc == NDC - 1)),
                        inc=(cT if (dc == NDC - 1 and tq == 3) else None))
            vq = P.val(cT)
            for tq in range(4):
                i = (oc * 4 + tq) % 2
                P.wait("y", cV)
                vx = P.dma("y", dY, sl512(xst, i),
                           x_prev[oc * 128:(oc + 1) * 128, tq * 512:(tq + 1) * 512])
                P.wait("v", cT, vq)
                P.wait("v", dY, vx)
                if is_b:
                    P.op("v", lambda v, quad=quad, tq=tq, i=i: v.tensor_tensor(
                        sl512(fst, i), sl512(quad, tq), sl512(xst, i), ALU.add), inc=cV)
                else:
                    P.op("v", lambda v, quad=quad, tq=tq, i=i, oc=oc: v.scalar_tensor_tensor(
                        sl512(fst, i), sl512(quad, tq), bob[:, oc:oc + 1],
                        sl512(xst, i), ALU.add, ALU.add), inc=cV)
                vv = P.val(cV)
                P.wait("y", cV, vv)
                P.dma("y", dY, x_next[oc * 128:(oc + 1) * 128, tq * 512:(tq + 1) * 512],
                      sl512(fst, i))

    def ffn_sublayer(x_prev, x_next):
        st_cv, st_ca = stats(x_prev)
        y_produce(x_prev, None, st_cv, st_ca)
        y2 = ar1[:, :].rearrange("p (dc t) -> p dc t", dc=NDC)
        g_sl = ar2[:, :].rearrange("p (hc t) -> p hc t", hc=16)
        v_y2 = P.val(cV)
        P.wait("t", cV, v_y2)

        for slab in range(4):
            for hc in range(16):
                habs = slab * 16 + hc
                sg_ = wslot[hc % 2]
                su_ = wslot[2]
                P.wait("g", cT)
                vwg = wload(sg_, din["wg"][habs])
                vwu = wload(su_, din["wu"][habs])
                P.wait("t", dW, vwg)
                P.wait("t", cA)
                for dc in range(NDC):
                    for tq in range(4):
                        P.op("t", lambda t, s_=sg_, dc=dc, tq=tq: t.matmul(
                            sl512(psA, tq), s_[:, dc * 128:(dc + 1) * 128],
                            y2[:, dc, tq * 512:(tq + 1) * 512],
                            start=(dc == 0), stop=(dc == NDC - 1)),
                            inc=(cT if (dc == NDC - 1 and tq == 3) else None))
                vg = P.val(cT)
                P.wait("t", dW, vwu)
                P.wait("t", cV)
                for dc in range(NDC):
                    for tq in range(4):
                        P.op("t", lambda t, s_=su_, dc=dc, tq=tq: t.matmul(
                            sl512(psB, tq), s_[:, dc * 128:(dc + 1) * 128],
                            y2[:, dc, tq * 512:(tq + 1) * 512],
                            start=(dc == 0), stop=(dc == NDC - 1)),
                            inc=(cT if (dc == NDC - 1 and tq == 3) else None))
                vu = P.val(cT)
                P.wait("a", cT, vg)
                P.wait("a", cV)
                P.op("a", lambda a: a.activation(sgt[:, :], psA[:, :], AF.Silu), inc=cA)
                vsilu = P.val(cA)
                P.wait("v", cA, vsilu)
                P.wait("v", cT, vu)
                P.op("v", lambda v, hc=hc: v.tensor_tensor(
                    g_sl[:, hc, :], sgt[:, :], psB[:, :], ALU.mult), inc=cV)
            v_g = P.val(cV)

            for oc in range(16):
                s_ = wslot[oc % 2]
                P.wait("g", cT)
                vw = wload(s_, din["wd"][oc][:, slab * 2048:(slab + 1) * 2048])
                quad = psA if oc % 2 == 0 else psB
                P.wait("t", dW, vw)
                P.wait("t", cV, v_g)
                P.wait("t", cA)
                P.wait("t", cV)
                for hc in range(16):
                    for tq in range(4):
                        P.op("t", lambda t, quad=quad, s_=s_, hc=hc, tq=tq: t.matmul(
                            sl512(quad, tq), s_[:, hc * 128:(hc + 1) * 128],
                            g_sl[:, hc, tq * 512:(tq + 1) * 512],
                            start=(hc == 0), stop=(hc == 15)),
                            inc=(cT if (hc == 15 and tq == 3) else None))
                vdn = P.val(cT)
                for tq in range(4):
                    i = (oc * 4 + tq) % 2
                    dst = x_next[oc * 128:(oc + 1) * 128, tq * 512:(tq + 1) * 512]
                    if slab == 0:
                        P.wait("y", cV)
                        vx = P.dma("y", dY, sl512(fst2, i),
                                   x_prev[oc * 128:(oc + 1) * 128, tq * 512:(tq + 1) * 512])
                        P.wait("v", cT, vdn)
                        P.wait("v", dY, vx)
                        P.op("v", lambda v, quad=quad, tq=tq, i=i: v.tensor_tensor(
                            sl512(fst, i), sl512(quad, tq), sl512(fst2, i), ALU.add), inc=cV)
                        vv = P.val(cV)
                        P.wait("y", cV, vv)
                        P.dma("y", dY, dst, sl512(fst, i))
                    else:
                        P.wait("a", cT, vdn)
                        P.wait("a", dW)
                        P.wait("a", cV)
                        P.op("a", lambda a, quad=quad, tq=tq, i=i: a.copy(
                            sl512(fst, i), sl512(quad, tq)), inc=cA)
                        va = P.val(cA)
                        P.wait("g", cA, va)
                        P.wait("g", dY)
                        P.dma("g", dW, dst, sl512(fst, i), accum=True)

    seq = [
        ("attn", x0, x_scr[0], False),
        ("ffn", x_scr[0], x_scr[1], None),
        ("attn", x_scr[1], x_scr[2], True),
        ("ffn", x_scr[2], out_t, None),
    ][:N_SUB]
    if N_SUB < 4:
        kind, xp, _, fl = seq[-1]
        seq[-1] = (kind, xp, out_t, fl)

    for kind, xp, xn, fl in seq:
        for e in ("y", "a", "g"):
            barrier_dma(e)
        if kind == "attn":
            attn_sublayer(xp, xn, fl)
        else:
            ffn_sublayer(xp, xn)

    for s in (dW, dY, dS):
        P.wait("y", s)
    P.wait("y", cc)

    if os.environ.get("MK_DEBUG") == "1":
        dbg_specs = [
            ("dbg_y", y_scr, [D, T], BF16), ("dbg_ygth", y_gth, [2 * D, T], BF16),
            ("dbg_q", q_scr, [NH * 128, TT], BF16), ("dbg_k", k_scr, [NH * 128, TT], BF16),
            ("dbg_v", v_scr, [TT, NH * 128], BF16), ("dbg_c", c_scr, [NH * 128, TT], BF16),
            ("dbg_cgth", c_gth, [D, TT], BF16), ("dbg_x1", x_scr[0], [D, T], F32),
            ("dbg_x2", x_scr[1], [D, T], F32), ("dbg_x3", x_scr[2], [D, T], F32),
        ]
        for nm, src, shp, dt in dbg_specs:
            dap = nc.dram_tensor(nm, shp, dt, kind="ExternalOutput").ap()
            P.dma("y", dY, dap, src)
        ar1d = nc.dram_tensor("dbg_ar1", [128, 32768], BF16, kind="ExternalOutput").ap()
        P.dma("y", dY, ar1d, ar1[:, :])
        P.wait("y", dY)

    with nc.Block() as block:
        @block.tensor
        def _(t):
            for f in P.q["t"]:
                f(t)

        @block.scalar
        def _(a):
            for f in P.q["a"]:
                f(a)

        @block.vector
        def _(v):
            for f in P.q["v"]:
                f(v)

        @block.gpsimd
        def _(g):
            for f in P.q["g"]:
                f(g)

        @block.sync
        def _(sy):
            for f in P.q["y"]:
                f(sy)

    es.close()
    return nc


_NC_CACHE = {}


def _blocks_img(w):
    O = w.shape[1]
    nb = O // 128
    out = np.empty((nb, 128, NDC * 128), np.float32)
    for oc in range(nb):
        blk = w[:, oc * 128:(oc + 1) * 128].reshape(NDC, 128, 128)
        out[oc] = blk.transpose(1, 0, 2).reshape(128, NDC * 128)
    return np.ascontiguousarray(out)


def _img(w):
    K, N = w.shape
    return np.ascontiguousarray(
        w.reshape(K // 128, 128, N).transpose(1, 0, 2).reshape(128, (K // 128) * N))


def _prep_inputs(inputs):
    f = lambda k: np.asarray(inputs[k], np.float32)
    x = f("L0_attn_a_x")
    wqa, wka, wva, woa = f("L0_attn_a_Wq"), f("L0_attn_a_Wk"), f("L0_attn_a_Wv"), f("L0_attn_a_Wo")
    bqa, bka, bva, boa = f("L0_attn_a_bq"), f("L0_attn_a_bk"), f("L0_attn_a_bv"), f("L0_attn_a_bo")
    wqb, wkb, wvb, wob = f("L0_attn_b_Wq"), f("L0_attn_b_Wk"), f("L0_attn_b_Wv"), f("L0_attn_b_Wo")
    wg, wu, wd = f("L0_ffn_w_gate"), f("L0_ffn_w_up"), f("L0_ffn_w_down")

    pos = np.arange(TT, dtype=np.float64)
    invf = 1.0 / (10000.0 ** (np.arange(64, dtype=np.float64) / 64.0))
    ang = pos[None, :] * invf[:, None]
    cosv = np.cos(ang); sinv = np.sin(ang)
    csc = np.ascontiguousarray(np.concatenate([cosv, cosv], 0).astype(ml_dtypes.bfloat16))
    css = np.ascontiguousarray(np.concatenate([-sinv, sinv], 0).astype(ml_dtypes.bfloat16))
    pswap = np.zeros((128, 128), np.float32)
    for jj in range(128):
        pswap[(jj + 64) % 128, jj] = 1.0
    pswap = np.ascontiguousarray(pswap.astype(ml_dtypes.bfloat16))
    ku = np.arange(128)[:, None]
    uu = np.arange(896)[None, :]
    maskc = np.ascontiguousarray(np.where(uu >= ku + 384, 0.0, -1e30).astype(ml_dtypes.bfloat16))

    wg_b = _blocks_img(np.ascontiguousarray(wg.T))
    wu_b = _blocks_img(np.ascontiguousarray(wu.T))
    wdT = np.ascontiguousarray(wd.T)
    wd_b = np.stack([_img(np.ascontiguousarray(wdT[:, oc * 128:(oc + 1) * 128]))
                     for oc in range(16)])
    woa_b = _blocks_img(woa)
    wob_b = _blocks_img(np.ascontiguousarray(wob.T))
    wqbT = np.ascontiguousarray(wqb.T); wkbT = np.ascontiguousarray(wkb.T)
    wvbT = np.ascontiguousarray(wvb.T)

    in_maps = []
    for c in range(8):
        b, p = c // 2, c % 2
        tok = slice(p * T, (p + 1) * T)
        hh = slice(p * 1024, (p + 1) * 1024)
        habs = np.arange(p * 8, p * 8 + 8)
        slopes = (2.0 ** (-8.0 * (habs + 1) / 16.0))
        ali = np.empty((128, NH * 32), np.float32)
        kb = np.arange(32)[None, :]
        ii = np.arange(128)[:, None]
        for j in range(NH):
            ali[:, j * 32:(j + 1) * 32] = slopes[j] * (kb * 128 + ii - (TT - 1))
        qq = np.arange(TT, dtype=np.float64)
        wal_t = np.ascontiguousarray(
            (slopes[:, None] * ((TT - 1) - qq[None, :]) / SCALE).astype(ml_dtypes.bfloat16)
        ).reshape(NH, 1, TT)
        m = {
            "x0": np.ascontiguousarray(x[b].T[:, tok]),
            "wqa": _blocks_img(np.ascontiguousarray(wqa[:, hh])),
            "wka": _blocks_img(np.ascontiguousarray(wka[:, hh])),
            "wva": _img(np.ascontiguousarray(wva[:, hh])),
            "woa": woa_b,
            "bqa": np.ascontiguousarray(bqa[hh].reshape(8, 128).T),
            "bka": np.ascontiguousarray(bka[hh].reshape(8, 128).T),
            "bva": np.ascontiguousarray(bva[hh].reshape(1, 1024)),
            "boa": np.ascontiguousarray(boa.reshape(16, 128).T),
            "wqb": _blocks_img(np.ascontiguousarray(wqbT[:, hh])),
            "wkb": _blocks_img(np.ascontiguousarray(wkbT[:, hh])),
            "wvb": _img(np.ascontiguousarray(wvbT[:, hh])),
            "wob": wob_b,
            "wg": wg_b, "wu": wu_b, "wd": wd_b,
            "alibi": np.ascontiguousarray(ali),
            "csc": csc, "css": css, "pswap": pswap, "maskc": maskc,
            "colsel": np.array([[p * T]], np.int32),
            "wal": wal_t,
        }
        in_maps.append(m)
    return in_maps


def kernel(**inputs):
    in_maps = _prep_inputs(inputs)
    if "nc" not in _NC_CACHE:
        _NC_CACHE["nc"] = build_nc()
    nc = _NC_CACHE["nc"]
    res = run_bass_kernel_spmd(nc, in_maps, core_ids=list(range(8)))
    out = np.empty((4, TT, D), np.float32)
    for c in range(8):
        b, p = c // 2, c % 2
        out[b, p * T:(p + 1) * T, :] = np.asarray(res.results[c]["out"]).T
    return out


# version 10
# speedup vs baseline: 1.0981x; 1.0981x over previous
import math
import os
from contextlib import ExitStack

import numpy as np
import ml_dtypes

import concourse.bass as bass
import concourse.mybir as mybir
from concourse.bass_utils import run_bass_kernel_spmd

F32 = mybir.dt.float32
BF16 = mybir.dt.bfloat16
I32 = mybir.dt.int32
AF = mybir.ActivationFunctionType
ALU = mybir.AluOpType
AX = mybir.AxisListType

D = 2048
T = 2048
TT = 4096
NH = 8
HD = 128
NDC = D // 128
NTC = T // 512
NTTC = TT // 512
SCALE = 1.0 / math.sqrt(HD)
EPS = 1e-5
N_SUB = int(os.environ.get("MK_NSUB", "4"))

ENGINES = ("t", "a", "v", "g", "y")


class Plan:
    def __init__(self):
        self.q = {e: [] for e in ENGINES}
        self.semv = {}
        self.seen = {e: {} for e in ENGINES}

    def val(self, sem):
        return self.semv.get(sem.name, 0)

    def inc(self, sem, n):
        v = self.val(sem) + n
        self.semv[sem.name] = v
        return v

    def wait(self, e, sem, v=None):
        if v is None:
            v = self.val(sem)
        if v <= 0 or self.seen[e].get(sem.name, -1) >= v:
            return
        self.seen[e][sem.name] = v
        self.q[e].append(lambda eng, s=sem, vv=v: eng.wait_ge(s, vv))

    def op(self, e, fn, inc=None, n=1):
        if inc is not None:
            self.inc(inc, n)
        self.q[e].append(
            lambda eng, f=fn, s=inc, nn=n: (f(eng).then_inc(s, nn) if s is not None else f(eng))
        )

    def dma(self, e, sem, out, in_, accum=False):
        if accum:
            self.op(e, lambda eng, o=out, i=in_: eng.dma_start(out=o, in_=i, accum_op=ALU.add),
                    inc=sem, n=16)
        else:
            self.op(e, lambda eng, o=out, i=in_: eng.dma_start(out=o, in_=i), inc=sem, n=16)
        return self.val(sem)


def build_nc():
    nc = bass.Bass("TRN2", target_bir_lowering=False)

    din = {}
    def inp(name, shape, dt=F32):
        din[name] = nc.dram_tensor(name, list(shape), dt, kind="ExternalInput").ap()
        return din[name]

    x0 = inp("x0", (D, T))
    inp("wqa", (NH, 128, NDC * 128)); inp("wka", (NH, 128, NDC * 128))
    inp("wva", (128, NDC * NH * 128))
    inp("woa", (16, 128, NDC * 128))
    inp("bqa", (128, NH)); inp("bka", (128, NH))
    inp("bva", (1, NH * 128)); inp("boa", (128, 16))
    inp("wqb", (NH, 128, NDC * 128)); inp("wkb", (NH, 128, NDC * 128))
    inp("wvb", (128, NDC * NH * 128))
    inp("wob", (16, 128, NDC * 128))
    inp("wg", (64, 128, NDC * 128)); inp("wu", (64, 128, NDC * 128))
    inp("wd", (16, 128, 64 * 128))
    inp("alibi", (128, NH * 32))
    inp("csc", (128, TT), BF16); inp("css", (128, TT), BF16)
    inp("pswap", (128, 128), BF16)
    inp("maskc", (128, 896), BF16)
    inp("colsel", (1, 1), I32)
    inp("wal", (NH, 1, TT), BF16)

    out_t = nc.dram_tensor("out", [D, T], F32, kind="ExternalOutput").ap()

    y_scr = nc.dram_tensor("y_scr", [D, T], BF16).ap()
    y_gth = nc.dram_tensor("y_gth", [2 * D, T], BF16).ap()
    q_scr = nc.dram_tensor("q_scr", [NH * 128, TT], BF16).ap()
    k_scr = nc.dram_tensor("k_scr", [NH * 128, TT], BF16).ap()
    v_scr = nc.dram_tensor("v_scr", [TT, NH * 128], BF16).ap()
    c_scr = nc.dram_tensor("c_scr", [NH * 128, TT], BF16).ap()
    c_gth = nc.dram_tensor("c_gth", [D, TT], BF16).ap()
    x_scr = [nc.dram_tensor(f"x{i}_scr", [D, T], F32).ap() for i in (1, 2, 3)]

    es = ExitStack()
    sb = lambda name, shape, dt: es.enter_context(nc.sbuf_tensor(name, list(shape), dt))

    ar1 = sb("ar1", (128, 32768), BF16)
    ar2 = sb("ar2", (128, 32768), BF16)
    wst = sb("wst", (128, 6144), BF16)
    ones_c = sb("ones_c", (128, 1), BF16)
    ones_rb = sb("ones_rb", (1, 128), BF16)
    ones_rf = sb("ones_rf", (1, 128), F32)
    m_row = sb("m_row", (1, T), F32)
    q_row = sb("q_row", (1, T), F32)
    r_row = sb("r_row", (1, T), F32)
    _ = r_row
    mrep = sb("mrep", (128, 512), F32)
    rrep = sb("rrep", (128, 512), F32)
    xst = sb("xst", (128, 2048), F32)
    sqst = sb("sqst", (128, 1024), BF16)
    fst = sb("fst", (128, 1024), F32)
    fst2 = sb("fst2", (128, 1024), F32)
    btl = sb("btl", (128, 6 * 512), BF16)
    ptl = sb("ptl", (128, 2048), BF16)
    sgt = sb("sgt", (128, 2048), BF16)
    biassl = sb("biassl", (128, 64), F32)
    mcol = sb("mcol", (128, 2), F32)
    alib = sb("alib", (128, NH * 32), F32)
    maskb = sb("maskb", (128, 896), BF16)
    pswb = sb("pswb", (128, 128), BF16)
    bqb = sb("bqb", (128, NH), F32)
    bkb = sb("bkb", (128, NH), F32)
    bvb = sb("bvb", (1, NH * 128), BF16)
    bob = sb("bob", (128, 16), F32)
    colsb = sb("colsb", (1, 1), I32)
    cells = sb("cells", (1, 64), F32)
    rowt = r_row[0:1, 1536:2048]
    psA = es.enter_context(nc.psum_tensor("psA", [128, 2048], F32))
    psB = es.enter_context(nc.psum_tensor("psB", [128, 2048], F32))

    dW = es.enter_context(nc.semaphore("dW"))
    dY = es.enter_context(nc.semaphore("dY"))
    dS = es.enter_context(nc.semaphore("dS"))
    cT = es.enter_context(nc.semaphore("cT"))
    cA = es.enter_context(nc.semaphore("cA"))
    cV = es.enter_context(nc.semaphore("cV"))
    cc = es.enter_context(nc.semaphore("ccs"))
    ini = es.enter_context(nc.semaphore("ini"))

    P = Plan()

    rreg = nc.alloc_register(mybir.EngineType.SP, "rctx")
    rtmp = nc.alloc_register(mybir.EngineType.SP, "rtmp")

    def sl512(t_, i):
        return t_[:, i * 512:(i + 1) * 512]

    P.op("g", lambda g: g.memset(ones_c[:, :], 1.0), inc=ini)
    P.op("g", lambda g: g.memset(ones_rb[:, :], 1.0), inc=ini)
    P.op("g", lambda g: g.memset(ones_rf[:, :], 1.0), inc=ini)
    for e_ in ("t", "a", "v", "y"):
        P.wait(e_, ini)
    P.dma("g", dW, maskb[:, :], din["maskc"])
    P.dma("g", dW, pswb[:, :], din["pswap"])
    P.dma("g", dW, colsb[:, :], din["colsel"])
    P.dma("g", dW, bqb[:, :], din["bqa"])
    P.dma("g", dW, bkb[:, :], din["bka"])
    P.dma("g", dW, bvb[:, :], din["bva"])
    P.dma("g", dW, bob[:, :], din["boa"])
    v_colsel = P.val(dW)
    P.wait("y", dW, v_colsel)
    P.q["y"].append(lambda sy: sy.reg_load(rreg, colsb[0:1, 0:1]))

    def barrier_dma(e):
        for s in (dW, dY, dS):
            P.wait(e, s)
        P.wait(e, cc)

    a1v = ar1[:, :].rearrange("p (dc t) -> p dc t", dc=NDC)

    def stats(x_ap):
        sum_r = psA[0:1, 0:512]
        sq_r = psA[0:1, 512:1024]
        P.wait("v", cT)
        P.wait("y", dW)
        P.wait("a", dW)
        P.wait("a", dY)
        slot_cv = [0] * 4
        slot_ca = [0] * 4
        sq_ct = [0, 0]
        row_cv = 0
        for tc in range(NTC):
            if tc > 0:
                P.wait("t", cV, row_cv)
            for dc in range(NDC):
                k = tc * NDC + dc
                i = k % 4
                qe, qs = ("y", dY) if k % 2 == 0 else ("a", dS)
                P.wait(qe, cV, slot_cv[i])
                P.wait(qe, cA, slot_ca[i])
                vx = P.dma(qe, qs, sl512(xst, i),
                           x_ap[dc * 128:(dc + 1) * 128, tc * 512:(tc + 1) * 512])
                P.wait("v", qs, vx)
                P.op("v", lambda v, i=i, dc=dc, tc=tc: v.tensor_copy(
                    a1v[:, dc, tc * 512:(tc + 1) * 512], sl512(xst, i)), inc=cV)
                slot_cv[i] = P.val(cV)
                P.wait("a", qs, vx)
                P.wait("a", cT, sq_ct[k % 2])
                P.op("a", lambda a, i=i, k=k: a.activation(
                    sl512(sqst, k % 2), sl512(xst, i), AF.Square), inc=cA)
                slot_ca[i] = P.val(cA)
                P.wait("t", cV, slot_cv[i])
                P.wait("t", cA, slot_ca[i])
                P.op("t", lambda t, dc=dc, tc=tc: t.matmul(
                    sum_r, ones_c[:, :], a1v[:, dc, tc * 512:(tc + 1) * 512],
                    start=(dc == 0), stop=(dc == NDC - 1)), inc=cT)
                P.op("t", lambda t, k=k, dc=dc: t.matmul(
                    sq_r, ones_c[:, :], sl512(sqst, k % 2),
                    start=(dc == 0), stop=(dc == NDC - 1)), inc=cT)
                sq_ct[k % 2] = P.val(cT)
            P.wait("v", cT)
            P.op("v", lambda v, tc=tc: v.tensor_scalar(
                m_row[0:1, tc * 512:(tc + 1) * 512], sum_r, 1.0 / D, None, ALU.mult), inc=cV)
            P.op("v", lambda v, tc=tc: v.tensor_scalar(
                q_row[0:1, tc * 512:(tc + 1) * 512], sq_r, 1.0 / D, None, ALU.mult), inc=cV)
            row_cv = P.val(cV)
        P.op("v", lambda v: v.tensor_tensor(r_row[:, :], m_row[:, :], m_row[:, :], ALU.mult), inc=cV)
        P.wait("v", cV)
        P.op("v", lambda v: v.tensor_tensor(q_row[:, :], q_row[:, :], r_row[:, :], ALU.subtract), inc=cV)
        P.wait("v", cV)
        P.op("v", lambda v: v.tensor_scalar(q_row[:, :], q_row[:, :], float(EPS), None, ALU.add), inc=cV)
        P.wait("a", cV)
        P.op("a", lambda a: a.activation(q_row[:, :], q_row[:, :], AF.Ln), inc=cA)
        P.wait("a", cA)
        P.op("a", lambda a: a.activation(r_row[:, :], q_row[:, :], AF.Exp, scale=-0.5), inc=cA)
        return slot_cv, slot_ca

    def replicate_tc(tc):
        P.wait("t", cA)
        P.wait("t", cV)
        P.op("t", lambda t, tc=tc: t.matmul(
            psB[:, 0:512], ones_rf[:, :], m_row[0:1, tc * 512:(tc + 1) * 512],
            start=True, stop=True), inc=cT)
        P.op("t", lambda t, tc=tc: t.matmul(
            psB[:, 512:1024], ones_rf[:, :], r_row[0:1, tc * 512:(tc + 1) * 512],
            start=True, stop=True), inc=cT)
        P.wait("a", cT)
        P.wait("a", cV)
        P.op("a", lambda a: a.copy(mrep[:, :], psB[:, 0:512]), inc=cA)
        P.op("a", lambda a: a.copy(rrep[:, :], psB[:, 512:1024]), inc=cA)
        return P.val(cA)

    def y_produce(x_ap, dst_dram, st_cv, st_ca):
        slot_cv = list(st_cv)
        slot_ca = list(st_ca)
        for tc in range(NTC):
            v_rep = replicate_tc(tc)
            for dc in range(NDC):
                k = tc * NDC + dc
                i = k % 4
                qe, qs = ("y", dY) if k % 2 == 0 else ("a", dS)
                P.wait(qe, cV, slot_cv[i])
                P.wait(qe, cA, slot_ca[i])
                slot_ca[i] = 0
                vx = P.dma(qe, qs, sl512(xst, i),
                           x_ap[dc * 128:(dc + 1) * 128, tc * 512:(tc + 1) * 512])
                P.wait("v", qs, vx)
                P.wait("v", cA, v_rep)
                P.op("v", lambda v, i=i, k=k, dc=dc, tc=tc: v.tensor_tensor(
                    sl512(fst, k % 2), sl512(xst, i), mrep[:, :], ALU.subtract), inc=cV)
                P.op("v", lambda v, k=k, dc=dc, tc=tc: v.tensor_tensor(
                    a1v[:, dc, tc * 512:(tc + 1) * 512], sl512(fst, k % 2), rrep[:, :],
                    ALU.mult), inc=cV)
                slot_cv[i] = P.val(cV)
                if dst_dram is not None:
                    P.wait(qe, cV, slot_cv[i])
                    P.dma(qe, qs, dst_dram[dc * 128:(dc + 1) * 128, tc * 512:(tc + 1) * 512],
                          a1v[:, dc, tc * 512:(tc + 1) * 512])

    wslot = [wst[:, i * 2048:(i + 1) * 2048] for i in range(3)]

    def wload(slot, src_ap):
        return P.dma("g", dW, slot, src_ap)

    def attn_sublayer(x_prev, x_next, is_b):
        wq_i, wk_i, wv_i, wo_i = (
            (din["wqb"], din["wkb"], din["wvb"], din["wob"]) if is_b else
            (din["wqa"], din["wka"], din["wva"], din["woa"]))

        if is_b:
            P.op("g", lambda g: g.memset(alib[:, :], 0.0), inc=ini)
        else:
            P.dma("g", dW, alib[:, :], din["alibi"])

        st_cv, st_ca = stats(x_prev)
        y_produce(x_prev, y_scr, st_cv, st_ca)
        v_ywr = P.val(dY)
        v_ysr = P.val(dS)
        P.wait("g", dY, v_ywr)
        P.wait("g", dS, v_ysr)
        P.op("g", lambda g: g.collective_compute(
            "AllGather", ALU.bypass,
            replica_groups=[[0, 1], [2, 3], [4, 5], [6, 7]],
            ins=[y_scr], outs=[y_gth]), inc=cc)
        v_cc = P.val(cc)

        wv_all = ar2[:, 0:16384].rearrange("p (dc e) -> p dc e", dc=NDC)
        qraw_st = [ar2[:, 16384 + i * 512:16384 + (i + 1) * 512] for i in range(2)]
        csc_sb = ar2[:, 20480:24576]
        css_sb = ar2[:, 24576:28672]
        P.wait("g", cT)
        if is_b:
            P.dma("g", dW, csc_sb, din["csc"])
            P.dma("g", dW, css_sb, din["css"])
        v_wv = P.dma("g", dW, wv_all, wv_i.rearrange("p (dc e) -> p dc e", dc=NDC))
        P.op("g", lambda g: g.memset(cells[:, :], 0.0), inc=ini)
        v_cells0 = P.val(ini)
        P.wait("v", ini, v_cells0)

        yslab = ar1[:, :].rearrange("p (dc t) -> p dc t", dc=NDC)

        def proj_evac_qk(quad, oc, sl, tq, which, i):
            tile = sl512(btl, i % 6)
            col = slice(sl * 2048 + tq * 512, sl * 2048 + (tq + 1) * 512)
            scr = k_scr if which == "k" else q_scr
            bias_sb = bkb if which == "k" else bqb
            if not is_b:
                P.wait("a", cT)
                P.wait("a", dY)
                P.op("a", lambda a, quad=quad, tq=tq, tile=tile, oc=oc, bias_sb=bias_sb: a.add(
                    tile, sl512(quad, tq), bias_sb[:, oc:oc + 1]), inc=cA)
                va = P.val(cA)
                P.wait("y", cA, va)
                P.dma("y", dY, scr[oc * 128:(oc + 1) * 128, col], tile)
            else:
                qraw = qraw_st[i % 2]
                P.wait("a", cT)
                P.wait("a", cV)
                P.op("a", lambda a, quad=quad, tq=tq, qraw=qraw: a.copy(
                    qraw, sl512(quad, tq)), inc=cA)
                va = P.val(cA)
                P.wait("t", cA, va)
                P.op("t", lambda t, quad=quad, tq=tq, qraw=qraw: t.matmul(
                    sl512(quad, tq), pswb[:, :], qraw, start=True, stop=True), inc=cT)
                vt = P.val(cT)
                P.wait("v", cA, va)
                P.wait("v", cA)
                P.wait("v", dY)
                P.wait("v", dS)
                P.op("v", lambda v, qraw=qraw, c=col: v.tensor_tensor(
                    sl512(xst, 0), qraw, csc_sb[:, c], ALU.mult), inc=cV)
                P.wait("v", cT, vt)
                P.op("v", lambda v, quad=quad, tq=tq, c=col: v.tensor_tensor(
                    sl512(xst, 1), sl512(quad, tq), css_sb[:, c], ALU.mult), inc=cV)
                P.wait("v", cV)
                P.op("v", lambda v, tile=tile: v.tensor_tensor(
                    tile, sl512(xst, 0), sl512(xst, 1), ALU.add), inc=cV)
                vv = P.val(cV)
                P.wait("y", cV, vv)
                P.dma("y", dY, scr[oc * 128:(oc + 1) * 128, col], tile)
            cidx = (0 if which == "q" else 8) + oc
            P.wait("v", cA)
            P.wait("v", cV)
            P.op("v", lambda v, i=i, tile=tile: v.tensor_tensor(
                sl512(sqst, i % 2), tile, tile, ALU.mult), inc=cV)
            vsq = P.val(cV)
            P.wait("t", cV, vsq)
            P.op("t", lambda t, quad=quad, tq=tq, i=i: t.matmul(
                quad[0:1, tq * 512:(tq + 1) * 512], ones_c[:, :], sl512(sqst, i % 2),
                start=True, stop=True), inc=cT)
            vnm = P.val(cT)
            P.wait("v", cT, vnm)
            P.op("v", lambda v, quad=quad, tq=tq: v.reduce_max(
                rowt[0:1, 0:1], quad[0:1, tq * 512:(tq + 1) * 512], axis=AX.X), inc=cV)
            P.wait("v", cV)
            P.op("v", lambda v, cidx=cidx: v.tensor_tensor(
                cells[0:1, cidx:cidx + 1], cells[0:1, cidx:cidx + 1], rowt[0:1, 0:1],
                ALU.max), inc=cV)

        wfree = [P.val(cT)] * 3
        ldi = [0]
        bank_a = [P.val(cA), P.val(cA)]
        bank_v = [P.val(cV), P.val(cV)]
        for sl in range(2):
            P.wait("y", cc, v_cc)
            P.wait("y", cT)
            src = y_gth[sl * D:(sl + 1) * D, :].rearrange("(dc p) t -> p dc t", p=128)
            for half in range(2):
                P.dma("y", dY, yslab[:, half * 8:(half + 1) * 8, :],
                      src[:, half * 8:(half + 1) * 8, :])
            vy = P.val(dY)
            P.wait("t", dY, vy)

            for which in ("q", "k"):
                w_in = wq_i if which == "q" else wk_i
                for oc in range(NH):
                    si = ldi[0] % 3
                    ldi[0] += 1
                    s_ = wslot[si]
                    P.wait("g", cT, wfree[si])
                    vw = wload(s_, w_in[oc])
                    bk = oc % 2
                    quad = psA if bk == 0 else psB
                    P.wait("t", dW, vw)
                    P.wait("t", cA, bank_a[bk])
                    P.wait("t", cV, bank_v[bk])
                    for dc in range(NDC):
                        for tq in range(4):
                            P.op("t", lambda t, quad=quad, s_=s_, dc=dc, tq=tq: t.matmul(
                                sl512(quad, tq), s_[:, dc * 128:(dc + 1) * 128],
                                yslab[:, dc, tq * 512:(tq + 1) * 512],
                                start=(dc == 0), stop=(dc == NDC - 1)),
                                inc=(cT if (dc == NDC - 1 and tq == 3) else None))
                    wfree[si] = P.val(cT)
                    for tq in range(4):
                        proj_evac_qk(quad, oc, sl, tq, which, oc * 4 + tq)
                    bank_a[bk] = P.val(cA)
                    bank_v[bk] = P.val(cV)

            P.wait("t", dW, v_wv)
            for tcb in range(16):
                bk = tcb % 2
                quad = psA if bk == 0 else psB
                P.wait("t", cA, bank_a[bk])
                P.wait("t", cV, bank_v[bk])
                last_dc_stop = is_b
                for dc in range(NDC):
                    for ec in range(2):
                        P.op("t", lambda t, quad=quad, dc=dc, ec=ec, tcb=tcb, lds=last_dc_stop: t.matmul(
                            sl512(quad, ec), yslab[:, dc, tcb * 128:(tcb + 1) * 128],
                            wv_all[:, dc, ec * 512:(ec + 1) * 512],
                            start=(dc == 0),
                            stop=(dc == NDC - 1 and lds)),
                            inc=(cT if (dc == NDC - 1 and ec == 1 and last_dc_stop) else None))
                if not is_b:
                    for ec in range(2):
                        P.op("t", lambda t, quad=quad, ec=ec: t.matmul(
                            sl512(quad, ec), ones_rb[:, :],
                            bvb[0:1, ec * 512:(ec + 1) * 512],
                            start=False, stop=True), inc=(cT if ec == 1 else None))
                vv_ = P.val(cT)
                P.wait("a", cT, vv_)
                for ec in range(2):
                    ti = (tcb * 2 + ec) % 6
                    P.wait("a", dY)
                    P.op("a", lambda a, quad=quad, ec=ec, ti=ti: a.copy(
                        sl512(btl, ti), sl512(quad, ec)), inc=cA)
                    va = P.val(cA)
                    P.wait("y", cA, va)
                    P.dma("y", dY,
                          v_scr[(sl * 2048 + tcb * 128):(sl * 2048 + (tcb + 1) * 128),
                                ec * 512:(ec + 1) * 512], sl512(btl, ti))
                bank_a[bk] = P.val(cA)

        P.wait("v", cT)
        P.wait("v", cV)
        P.op("v", lambda v: v.tensor_tensor(
            cells[0:1, 16:24], cells[0:1, 0:8], cells[0:1, 8:16], ALU.add), inc=cV)
        P.wait("v", cV)
        P.op("v", lambda v: v.tensor_scalar(
            cells[0:1, 16:24], cells[0:1, 16:24], SCALE / 2.0, 8.0, ALU.mult, ALU.add), inc=cV)
        v_M = P.val(cV)

        kh = [ar2[:, 0:4096], ar2[:, 4096:8192]]
        qh = [ar2[:, 8192:12288], ar2[:, 12288:16384]]
        vh = [ar2[:, 16384:20480], ar2[:, 20480:24576]]
        wal = [ar2[0:1, 24576:28672], ar2[0:1, 28672:32768]]

        def head_loads(h):
            i = h % 2
            P.wait("a", cT)
            P.dma("a", dS, kh[i], k_scr[h * 128:(h + 1) * 128, :])
            P.dma("a", dS, qh[i], q_scr[h * 128:(h + 1) * 128, :])
            if not is_b:
                P.wait("a", cV)
                P.dma("a", dS, wal[i], din["wal"][h])
            P.wait("y", cT)
            P.wait("y", cV)
            vsrc = v_scr[:, :].rearrange("(kb p) e -> p kb e", p=128)
            for piece in range(4):
                P.dma("y", dY,
                      vh[i].rearrange("p (kb e) -> p kb e", kb=32)[:, piece * 8:(piece + 1) * 8, :],
                      vsrc[:, piece * 8:(piece + 1) * 8, (h * 128):((h + 1) * 128)])
            return P.val(dS), P.val(dY)

        def build_bias_slab(h):
            P.wait("t", cV, v_M)
            P.wait("t", cA)
            P.op("t", lambda t, h=h: t.matmul(
                psB[:, 1024:1025], ones_rf[:, :], cells[0:1, 16 + h:17 + h],
                start=True, stop=True), inc=cT)
            vt = P.val(cT)
            P.wait("a", cT, vt)
            P.wait("a", cV)
            P.op("a", lambda a, h=h: a.copy(mcol[:, h % 2:h % 2 + 1], psB[:, 1024:1025]), inc=cA)
            va = P.val(cA)
            P.wait("v", cA, va)
            P.op("v", lambda v, h=h: v.tensor_scalar(
                biassl[:, (h % 2) * 32:(h % 2) * 32 + 32], alib[:, h * 32:(h + 1) * 32],
                mcol[:, h % 2:h % 2 + 1], None, ALU.subtract), inc=cV)
            return P.val(cV)

        sems = [None] * (NH + 1)
        bs_v = [None] * (NH + 1)
        sems[0] = head_loads(0)
        bs_v[0] = build_bias_slab(0)
        KBc = [0]
        exp_v = {}
        psb_free = [P.val(cV)]
        for h in range(NH):
            if h + 1 < NH:
                sems[h + 1] = head_loads(h + 1)
                bs_v[h + 1] = build_bias_slab(h + 1)
            vS, vY = sems[h]
            P.wait("t", dS, vS)
            P.wait("t", dY, vY)
            i = h % 2
            for qc in range(NTTC):
                nkb = 4 * (qc + 1)
                kg0 = KBc[0]

                def u_scores(kb, kg, i=i, h=h, qc=qc):
                    sl = sl512(psA, kg % 4)
                    P.wait("t", cA, exp_v.get(kg - 4, 0))
                    P.op("t", lambda t, sl=sl, kb=kb, i=i, qc=qc: t.matmul(
                        sl, kh[i][:, kb * 128:(kb + 1) * 128],
                        qh[i][:, qc * 512:(qc + 1) * 512],
                        start=True, stop=is_b), inc=(cT if is_b else None))
                    if not is_b:
                        P.op("t", lambda t, sl=sl, i=i, qc=qc: t.matmul(
                            sl, ones_rb[:, :], wal[i][0:1, qc * 512:(qc + 1) * 512],
                            start=False, stop=True), inc=cT)
                    vsc = P.val(cT)
                    dj = kb - 4 * qc
                    vmask = None
                    if dj >= 0:
                        P.wait("v", cT, vsc)
                        P.op("v", lambda v, sl=sl, dj=dj: v.tensor_tensor(
                            sl, sl, maskb[:, 384 - 128 * dj:896 - 128 * dj],
                            ALU.add), inc=cV)
                        vmask = P.val(cV)
                    P.wait("a", cT, vsc)
                    if vmask is not None:
                        P.wait("a", cV, vmask)
                    P.wait("a", cV, bs_v[h])
                    P.op("a", lambda a, sl=sl, kg=kg, kb=kb, h=h: a.activation(
                        sl512(ptl, kg % 4), sl,
                        AF.Exp, bias=biassl[:, (h % 2) * 32 + kb:(h % 2) * 32 + kb + 1],
                        scale=SCALE), inc=cA)
                    exp_v[kg] = P.val(cA)

                def u_sums_av(kb, kg, first, i=i):
                    P.wait("t", cA, exp_v[kg])
                    if first:
                        P.wait("t", cV, psb_free[0])
                    P.op("t", lambda t, kg=kg, first=first: t.matmul(
                        psB[0:1, 512:1024], ones_c[:, :], sl512(ptl, kg % 4),
                        start=first, stop=True), inc=None)
                    P.op("t", lambda t, kg=kg, kb=kb, i=i, first=first: t.matmul(
                        psB[:, 0:512], vh[i][:, kb * 128:(kb + 1) * 128],
                        sl512(ptl, kg % 4),
                        start=first, stop=True), inc=cT)

                for kb in range(nkb):
                    u_scores(kb, kg0 + kb)
                    if kb >= 2:
                        u_sums_av(kb - 2, kg0 + kb - 2, kb - 2 == 0)
                for kb in range(max(0, nkb - 2), nkb):
                    u_sums_av(kb, kg0 + kb, kb == 0)
                KBc[0] = kg0 + nkb

                v_avdone = P.val(cT)
                P.wait("v", cT, v_avdone)
                P.op("v", lambda v: v.reciprocal(rowt[:, :], psB[0:1, 512:1024]), inc=cV)
                vrec = P.val(cV)
                P.wait("t", cV, vrec)
                P.op("t", lambda t: t.matmul(
                    psB[:, 1024:1536], ones_rf[:, :], rowt[0:1, :], start=True, stop=True), inc=cT)
                vrep = P.val(cT)
                P.wait("a", cT, vrep)
                P.wait("a", cV)
                P.op("a", lambda a, qc=qc: a.copy(sl512(fst2, qc % 2), psB[:, 1024:1536]), inc=cA)
                vrepc = P.val(cA)
                ctile = sl512(btl, qc % 6)
                P.wait("v", cA, vrepc)
                P.wait("v", dY)
                P.op("v", lambda v, qc=qc, ctile=ctile: v.tensor_tensor(
                    ctile, psB[:, 0:512], sl512(fst2, qc % 2), ALU.mult), inc=cV)
                vctx = P.val(cV)
                psb_free[0] = vctx
                P.wait("y", cV, vctx)
                P.dma("y", dY, c_scr[h * 128:(h + 1) * 128, qc * 512:(qc + 1) * 512], ctile)

        v_cwr = P.val(dY)
        P.wait("g", dY, v_cwr)
        P.wait("g", dS)
        P.op("g", lambda g: g.collective_compute(
            "AllGather", ALU.bypass,
            replica_groups=[[0, 1], [2, 3], [4, 5], [6, 7]],
            ins=[c_scr], outs=[c_gth]), inc=cc)
        v_cc2 = P.val(cc)

        ctxb = ar1[:, :].rearrange("p (dc t) -> p dc t", dc=NDC)
        P.wait("y", cc, v_cc2)
        P.wait("y", cT)
        for piece in range(4):
            def dma_piece(sy, piece=piece):
                sy.reg_add(rtmp, rreg, piece * 4 * 128 * TT)
                return sy.dma_start(
                    out=ctxb[:, piece * 4:(piece + 1) * 4, :],
                    in_=bass.AP(c_gth.tensor, rtmp, [[TT, 128], [128 * TT, 4], [1, T]]))
            P.op("y", dma_piece, inc=dY, n=16)
        v_ctxld = P.val(dY)
        P.wait("t", dY, v_ctxld)

        P.wait("t", cA)
        bank_v = [P.val(cV), P.val(cV)]
        for oc in range(16):
            si = ldi[0] % 3
            ldi[0] += 1
            s_ = wslot[si]
            P.wait("g", cT, wfree[si])
            vw = wload(s_, wo_i[oc])
            bk = oc % 2
            quad = psA if bk == 0 else psB
            P.wait("t", dW, vw)
            P.wait("t", cV, bank_v[bk])
            for dc in range(NDC):
                for tq in range(4):
                    P.op("t", lambda t, quad=quad, s_=s_, dc=dc, tq=tq: t.matmul(
                        sl512(quad, tq), s_[:, dc * 128:(dc + 1) * 128],
                        ctxb[:, dc, tq * 512:(tq + 1) * 512],
                        start=(dc == 0), stop=(dc == NDC - 1)),
                        inc=(cT if (dc == NDC - 1 and tq == 3) else None))
            vq = P.val(cT)
            wfree[si] = vq
            for tq in range(4):
                i = (oc * 4 + tq) % 2
                P.wait("y", cV)
                vx = P.dma("y", dY, sl512(xst, i),
                           x_prev[oc * 128:(oc + 1) * 128, tq * 512:(tq + 1) * 512])
                P.wait("v", cT, vq)
                P.wait("v", dY, vx)
                if is_b:
                    P.op("v", lambda v, quad=quad, tq=tq, i=i: v.tensor_tensor(
                        sl512(fst, i), sl512(quad, tq), sl512(xst, i), ALU.add), inc=cV)
                else:
                    P.op("v", lambda v, quad=quad, tq=tq, i=i, oc=oc: v.scalar_tensor_tensor(
                        sl512(fst, i), sl512(quad, tq), bob[:, oc:oc + 1],
                        sl512(xst, i), ALU.add, ALU.add), inc=cV)
                vv = P.val(cV)
                P.wait("y", cV, vv)
                P.dma("y", dY, x_next[oc * 128:(oc + 1) * 128, tq * 512:(tq + 1) * 512],
                      sl512(fst, i))
            bank_v[bk] = P.val(cV)

    def ffn_sublayer(x_prev, x_next):
        st_cv, st_ca = stats(x_prev)
        y_produce(x_prev, None, st_cv, st_ca)
        y2 = ar1[:, :].rearrange("p (dc t) -> p dc t", dc=NDC)
        g_sl = ar2[:, :].rearrange("p (hc t) -> p hc t", hc=16)
        v_y2 = P.val(cV)
        P.wait("t", cV, v_y2)

        ff_wfree = [P.val(cT)] * 3
        fli = [0]
        for slab in range(4):
            for hc in range(16):
                habs = slab * 16 + hc
                gi = fli[0] % 3
                fli[0] += 1
                ui = fli[0] % 3
                fli[0] += 1
                sg_ = wslot[gi]
                su_ = wslot[ui]
                P.wait("g", cT, ff_wfree[gi])
                vwg = wload(sg_, din["wg"][habs])
                P.wait("g", cT, ff_wfree[ui])
                vwu = wload(su_, din["wu"][habs])
                P.wait("t", dW, vwg)
                P.wait("t", cA)
                for dc in range(NDC):
                    for tq in range(4):
                        P.op("t", lambda t, s_=sg_, dc=dc, tq=tq: t.matmul(
                            sl512(psA, tq), s_[:, dc * 128:(dc + 1) * 128],
                            y2[:, dc, tq * 512:(tq + 1) * 512],
                            start=(dc == 0), stop=(dc == NDC - 1)),
                            inc=(cT if (dc == NDC - 1 and tq == 3) else None))
                vg = P.val(cT)
                ff_wfree[gi] = vg
                P.wait("t", dW, vwu)
                P.wait("t", cV)
                for dc in range(NDC):
                    for tq in range(4):
                        P.op("t", lambda t, s_=su_, dc=dc, tq=tq: t.matmul(
                            sl512(psB, tq), s_[:, dc * 128:(dc + 1) * 128],
                            y2[:, dc, tq * 512:(tq + 1) * 512],
                            start=(dc == 0), stop=(dc == NDC - 1)),
                            inc=(cT if (dc == NDC - 1 and tq == 3) else None))
                vu = P.val(cT)
                ff_wfree[ui] = vu
                P.wait("a", cT, vg)
                P.wait("a", cV)
                P.op("a", lambda a: a.activation(sgt[:, :], psA[:, :], AF.Silu), inc=cA)
                vsilu = P.val(cA)
                P.wait("v", cA, vsilu)
                P.wait("v", cT, vu)
                P.op("v", lambda v, hc=hc: v.tensor_tensor(
                    g_sl[:, hc, :], sgt[:, :], psB[:, :], ALU.mult), inc=cV)
            v_g = P.val(cV)

            dn_a = [P.val(cA), P.val(cA)]
            dn_v = [P.val(cV), P.val(cV)]
            dn_w = []

            def dn_load(oc2, slab=slab, dn_w=dn_w):
                si2 = fli[0] % 3
                fli[0] += 1
                s2 = wslot[si2]
                P.wait("g", cT, ff_wfree[si2])
                vw2 = wload(s2, din["wd"][oc2][:, slab * 2048:(slab + 1) * 2048])
                dn_w.append((si2, s2, vw2))

            for oc in range(16):
                dn_load(oc)
                si, s_, vw = dn_w[oc]
                bk = oc % 2
                quad = psA if bk == 0 else psB
                P.wait("t", dW, vw)
                P.wait("t", cV, v_g)
                P.wait("t", cA)
                P.wait("t", cV)
                for hc in range(16):
                    for tq in range(4):
                        P.op("t", lambda t, quad=quad, s_=s_, hc=hc, tq=tq: t.matmul(
                            sl512(quad, tq), s_[:, hc * 128:(hc + 1) * 128],
                            g_sl[:, hc, tq * 512:(tq + 1) * 512],
                            start=(hc == 0), stop=(hc == 15)),
                            inc=(cT if (hc == 15 and tq == 3) else None))
                vdn = P.val(cT)
                ff_wfree[si] = vdn
                for tq in range(4):
                    i = (oc * 4 + tq) % 2
                    dst = x_next[oc * 128:(oc + 1) * 128, tq * 512:(tq + 1) * 512]
                    if slab == 0:
                        P.wait("y", cV)
                        vx = P.dma("y", dY, sl512(fst2, i),
                                   x_prev[oc * 128:(oc + 1) * 128, tq * 512:(tq + 1) * 512])
                        P.wait("v", cT, vdn)
                        P.wait("v", dY, vx)
                        P.op("v", lambda v, quad=quad, tq=tq, i=i: v.tensor_tensor(
                            sl512(fst, i), sl512(quad, tq), sl512(fst2, i), ALU.add), inc=cV)
                        vv = P.val(cV)
                        P.wait("y", cV, vv)
                        P.dma("y", dY, dst, sl512(fst, i))
                    else:
                        P.wait("a", cT, vdn)
                        P.wait("a", dW)
                        P.wait("a", cV)
                        P.op("a", lambda a, quad=quad, tq=tq, i=i: a.copy(
                            sl512(fst, i), sl512(quad, tq)), inc=cA)
                        va = P.val(cA)
                        P.wait("g", cA, va)
                        P.wait("g", dY)
                        P.dma("g", dW, dst, sl512(fst, i), accum=True)
                dn_a[bk] = P.val(cA)
                dn_v[bk] = P.val(cV)

    seq = [
        ("attn", x0, x_scr[0], False),
        ("ffn", x_scr[0], x_scr[1], None),
        ("attn", x_scr[1], x_scr[2], True),
        ("ffn", x_scr[2], out_t, None),
    ][:N_SUB]
    if N_SUB < 4:
        kind, xp, _, fl = seq[-1]
        seq[-1] = (kind, xp, out_t, fl)

    for kind, xp, xn, fl in seq:
        for e in ("y", "a", "g"):
            barrier_dma(e)
        if kind == "attn":
            attn_sublayer(xp, xn, fl)
        else:
            ffn_sublayer(xp, xn)

    for s in (dW, dY, dS):
        P.wait("y", s)
    P.wait("y", cc)

    if os.environ.get("MK_DEBUG") == "1":
        dbg_specs = [
            ("dbg_y", y_scr, [D, T], BF16), ("dbg_ygth", y_gth, [2 * D, T], BF16),
            ("dbg_q", q_scr, [NH * 128, TT], BF16), ("dbg_k", k_scr, [NH * 128, TT], BF16),
            ("dbg_v", v_scr, [TT, NH * 128], BF16), ("dbg_c", c_scr, [NH * 128, TT], BF16),
            ("dbg_cgth", c_gth, [D, TT], BF16), ("dbg_x1", x_scr[0], [D, T], F32),
            ("dbg_x2", x_scr[1], [D, T], F32), ("dbg_x3", x_scr[2], [D, T], F32),
        ]
        for nm, src, shp, dt in dbg_specs:
            dap = nc.dram_tensor(nm, shp, dt, kind="ExternalOutput").ap()
            P.dma("y", dY, dap, src)
        ar1d = nc.dram_tensor("dbg_ar1", [128, 32768], BF16, kind="ExternalOutput").ap()
        P.dma("y", dY, ar1d, ar1[:, :])
        P.wait("y", dY)

    with nc.Block() as block:
        @block.tensor
        def _(t):
            for f in P.q["t"]:
                f(t)

        @block.scalar
        def _(a):
            for f in P.q["a"]:
                f(a)

        @block.vector
        def _(v):
            for f in P.q["v"]:
                f(v)

        @block.gpsimd
        def _(g):
            for f in P.q["g"]:
                f(g)

        @block.sync
        def _(sy):
            for f in P.q["y"]:
                f(sy)

    es.close()
    return nc


_NC_CACHE = {}


def _blocks_img(w):
    O = w.shape[1]
    nb = O // 128
    out = np.empty((nb, 128, NDC * 128), np.float32)
    for oc in range(nb):
        blk = w[:, oc * 128:(oc + 1) * 128].reshape(NDC, 128, 128)
        out[oc] = blk.transpose(1, 0, 2).reshape(128, NDC * 128)
    return np.ascontiguousarray(out)


def _img(w):
    K, N = w.shape
    return np.ascontiguousarray(
        w.reshape(K // 128, 128, N).transpose(1, 0, 2).reshape(128, (K // 128) * N))


def _prep_inputs(inputs):
    f = lambda k: np.asarray(inputs[k], np.float32)
    x = f("L0_attn_a_x")
    wqa, wka, wva, woa = f("L0_attn_a_Wq"), f("L0_attn_a_Wk"), f("L0_attn_a_Wv"), f("L0_attn_a_Wo")
    bqa, bka, bva, boa = f("L0_attn_a_bq"), f("L0_attn_a_bk"), f("L0_attn_a_bv"), f("L0_attn_a_bo")
    wqb, wkb, wvb, wob = f("L0_attn_b_Wq"), f("L0_attn_b_Wk"), f("L0_attn_b_Wv"), f("L0_attn_b_Wo")
    wg, wu, wd = f("L0_ffn_w_gate"), f("L0_ffn_w_up"), f("L0_ffn_w_down")

    pos = np.arange(TT, dtype=np.float64)
    invf = 1.0 / (10000.0 ** (np.arange(64, dtype=np.float64) / 64.0))
    ang = pos[None, :] * invf[:, None]
    cosv = np.cos(ang); sinv = np.sin(ang)
    csc = np.ascontiguousarray(np.concatenate([cosv, cosv], 0).astype(ml_dtypes.bfloat16))
    css = np.ascontiguousarray(np.concatenate([-sinv, sinv], 0).astype(ml_dtypes.bfloat16))
    pswap = np.zeros((128, 128), np.float32)
    for jj in range(128):
        pswap[(jj + 64) % 128, jj] = 1.0
    pswap = np.ascontiguousarray(pswap.astype(ml_dtypes.bfloat16))
    ku = np.arange(128)[:, None]
    uu = np.arange(896)[None, :]
    maskc = np.ascontiguousarray(np.where(uu >= ku + 384, 0.0, -1e30).astype(ml_dtypes.bfloat16))

    wg_b = _blocks_img(np.ascontiguousarray(wg.T))
    wu_b = _blocks_img(np.ascontiguousarray(wu.T))
    wdT = np.ascontiguousarray(wd.T)
    wd_b = np.stack([_img(np.ascontiguousarray(wdT[:, oc * 128:(oc + 1) * 128]))
                     for oc in range(16)])
    woa_b = _blocks_img(woa)
    wob_b = _blocks_img(np.ascontiguousarray(wob.T))
    wqbT = np.ascontiguousarray(wqb.T); wkbT = np.ascontiguousarray(wkb.T)
    wvbT = np.ascontiguousarray(wvb.T)

    in_maps = []
    for c in range(8):
        b, p = c // 2, c % 2
        tok = slice(p * T, (p + 1) * T)
        hh = slice(p * 1024, (p + 1) * 1024)
        habs = np.arange(p * 8, p * 8 + 8)
        slopes = (2.0 ** (-8.0 * (habs + 1) / 16.0))
        ali = np.empty((128, NH * 32), np.float32)
        kb = np.arange(32)[None, :]
        ii = np.arange(128)[:, None]
        for j in range(NH):
            ali[:, j * 32:(j + 1) * 32] = slopes[j] * (kb * 128 + ii - (TT - 1))
        qq = np.arange(TT, dtype=np.float64)
        wal_t = np.ascontiguousarray(
            (slopes[:, None] * ((TT - 1) - qq[None, :]) / SCALE).astype(ml_dtypes.bfloat16)
        ).reshape(NH, 1, TT)
        m = {
            "x0": np.ascontiguousarray(x[b].T[:, tok]),
            "wqa": _blocks_img(np.ascontiguousarray(wqa[:, hh])),
            "wka": _blocks_img(np.ascontiguousarray(wka[:, hh])),
            "wva": _img(np.ascontiguousarray(wva[:, hh])),
            "woa": woa_b,
            "bqa": np.ascontiguousarray(bqa[hh].reshape(8, 128).T),
            "bka": np.ascontiguousarray(bka[hh].reshape(8, 128).T),
            "bva": np.ascontiguousarray(bva[hh].reshape(1, 1024)),
            "boa": np.ascontiguousarray(boa.reshape(16, 128).T),
            "wqb": _blocks_img(np.ascontiguousarray(wqbT[:, hh])),
            "wkb": _blocks_img(np.ascontiguousarray(wkbT[:, hh])),
            "wvb": _img(np.ascontiguousarray(wvbT[:, hh])),
            "wob": wob_b,
            "wg": wg_b, "wu": wu_b, "wd": wd_b,
            "alibi": np.ascontiguousarray(ali),
            "csc": csc, "css": css, "pswap": pswap, "maskc": maskc,
            "colsel": np.array([[p * T]], np.int32),
            "wal": wal_t,
        }
        in_maps.append(m)
    return in_maps


def kernel(**inputs):
    in_maps = _prep_inputs(inputs)
    if "nc" not in _NC_CACHE:
        _NC_CACHE["nc"] = build_nc()
    nc = _NC_CACHE["nc"]
    res = run_bass_kernel_spmd(nc, in_maps, core_ids=list(range(8)))
    out = np.empty((4, TT, D), np.float32)
    for c in range(8):
        b, p = c // 2, c % 2
        out[b, p * T:(p + 1) * T, :] = np.asarray(res.results[c]["out"]).T
    return out


# version 11
# speedup vs baseline: 1.1938x; 1.0871x over previous
import math
import os
from contextlib import ExitStack

import numpy as np
import ml_dtypes

import concourse.bass as bass
import concourse.mybir as mybir
from concourse.bass_utils import run_bass_kernel_spmd

F32 = mybir.dt.float32
BF16 = mybir.dt.bfloat16
I32 = mybir.dt.int32
AF = mybir.ActivationFunctionType
ALU = mybir.AluOpType
AX = mybir.AxisListType

D = 2048
T = 2048
TT = 4096
NH = 8
HD = 128
NDC = D // 128
NTC = T // 512
NTTC = TT // 512
SCALE = 1.0 / math.sqrt(HD)
EPS = 1e-5
N_SUB = int(os.environ.get("MK_NSUB", "4"))

ENGINES = ("t", "a", "v", "g", "y")


class Plan:
    def __init__(self):
        self.q = {e: [] for e in ENGINES}
        self.semv = {}
        self.seen = {e: {} for e in ENGINES}

    def val(self, sem):
        return self.semv.get(sem.name, 0)

    def inc(self, sem, n):
        v = self.val(sem) + n
        self.semv[sem.name] = v
        return v

    def wait(self, e, sem, v=None):
        if v is None:
            v = self.val(sem)
        if v <= 0 or self.seen[e].get(sem.name, -1) >= v:
            return
        self.seen[e][sem.name] = v
        self.q[e].append(lambda eng, s=sem, vv=v: eng.wait_ge(s, vv))

    def op(self, e, fn, inc=None, n=1):
        if inc is not None:
            self.inc(inc, n)
        self.q[e].append(
            lambda eng, f=fn, s=inc, nn=n: (f(eng).then_inc(s, nn) if s is not None else f(eng))
        )

    def dma(self, e, sem, out, in_, accum=False):
        if accum:
            self.op(e, lambda eng, o=out, i=in_: eng.dma_start(out=o, in_=i, accum_op=ALU.add),
                    inc=sem, n=16)
        else:
            self.op(e, lambda eng, o=out, i=in_: eng.dma_start(out=o, in_=i), inc=sem, n=16)
        return self.val(sem)


def build_nc():
    nc = bass.Bass("TRN2", target_bir_lowering=False)

    din = {}
    def inp(name, shape, dt=F32):
        din[name] = nc.dram_tensor(name, list(shape), dt, kind="ExternalInput").ap()
        return din[name]

    x0 = inp("x0", (D, T))
    inp("wqa", (NH, 128, NDC * 128)); inp("wka", (NH, 128, NDC * 128))
    inp("wva", (128, NDC * NH * 128))
    inp("woa", (16, 128, NDC * 128))
    inp("bqa", (128, NH)); inp("bka", (128, NH))
    inp("bva", (1, NH * 128)); inp("boa", (128, 16))
    inp("wqb", (NH, 128, NDC * 128)); inp("wkb", (NH, 128, NDC * 128))
    inp("wvb", (128, NDC * NH * 128))
    inp("wob", (16, 128, NDC * 128))
    inp("wg", (64, 128, NDC * 128)); inp("wu", (64, 128, NDC * 128))
    inp("wd", (16, 128, 64 * 128))
    inp("alibi", (128, NH * 32))
    inp("csc", (128, TT), BF16); inp("css", (128, TT), BF16)
    inp("pswap", (128, 128), BF16)
    inp("maskc", (128, 896), BF16)
    inp("colsel", (1, 1), I32)
    inp("wal", (NH, 1, TT), BF16)

    out_t = nc.dram_tensor("out", [D, T], F32, kind="ExternalOutput").ap()

    y_scr = nc.dram_tensor("y_scr", [D, T], BF16).ap()
    y_gth = nc.dram_tensor("y_gth", [2 * D, T], BF16).ap()
    q_scr = nc.dram_tensor("q_scr", [NH * 128, TT], BF16).ap()
    k_scr = nc.dram_tensor("k_scr", [NH * 128, TT], BF16).ap()
    v_scr = nc.dram_tensor("v_scr", [TT, NH * 128], BF16).ap()
    c_scr = nc.dram_tensor("c_scr", [NH * 128, TT], BF16).ap()
    c_gth = nc.dram_tensor("c_gth", [D, TT], BF16).ap()
    x_scr = [nc.dram_tensor(f"x{i}_scr", [D, T], F32).ap() for i in (1, 2, 3)]

    es = ExitStack()
    sb = lambda name, shape, dt: es.enter_context(nc.sbuf_tensor(name, list(shape), dt))

    ar1 = sb("ar1", (128, 32768), BF16)
    ar2 = sb("ar2", (128, 32768), BF16)
    wst = sb("wst", (128, 6144), BF16)
    ones_c = sb("ones_c", (128, 1), BF16)
    ones_rb = sb("ones_rb", (1, 128), BF16)
    ones_rf = sb("ones_rf", (1, 128), F32)
    m_row = sb("m_row", (1, T), F32)
    q_row = sb("q_row", (1, T), F32)
    r_row = sb("r_row", (1, T), F32)
    _ = r_row
    mrep = sb("mrep", (128, 512), F32)
    rrep = sb("rrep", (128, 512), F32)
    xst = sb("xst", (128, 2048), F32)
    sqst = sb("sqst", (128, 1024), BF16)
    fst = sb("fst", (128, 1024), F32)
    fst2 = sb("fst2", (128, 1024), F32)
    btl = sb("btl", (128, 6 * 512), BF16)
    ptl = sb("ptl", (128, 2048), BF16)
    sgt = sb("sgt", (128, 2048), BF16)
    biassl = sb("biassl", (128, 64), F32)
    mcol = sb("mcol", (128, 2), F32)
    alib = sb("alib", (128, NH * 32), F32)
    maskb = sb("maskb", (128, 896), BF16)
    pswb = sb("pswb", (128, 128), BF16)
    bqb = sb("bqb", (128, NH), F32)
    bkb = sb("bkb", (128, NH), F32)
    bvb = sb("bvb", (1, NH * 128), BF16)
    bob = sb("bob", (128, 16), F32)
    colsb = sb("colsb", (1, 1), I32)
    cells = sb("cells", (1, 64), F32)
    rowt = r_row[0:1, 1536:2048]
    psA = es.enter_context(nc.psum_tensor("psA", [128, 2048], F32))
    psB = es.enter_context(nc.psum_tensor("psB", [128, 2048], F32))

    dW = es.enter_context(nc.semaphore("dW"))
    dY = es.enter_context(nc.semaphore("dY"))
    dS = es.enter_context(nc.semaphore("dS"))
    cT = es.enter_context(nc.semaphore("cT"))
    cA = es.enter_context(nc.semaphore("cA"))
    cV = es.enter_context(nc.semaphore("cV"))
    cc = es.enter_context(nc.semaphore("ccs"))
    ini = es.enter_context(nc.semaphore("ini"))

    P = Plan()

    rreg = nc.alloc_register(mybir.EngineType.SP, "rctx")
    rtmp = nc.alloc_register(mybir.EngineType.SP, "rtmp")

    def sl512(t_, i):
        return t_[:, i * 512:(i + 1) * 512]

    P.op("g", lambda g: g.memset(ones_c[:, :], 1.0), inc=ini)
    P.op("g", lambda g: g.memset(ones_rb[:, :], 1.0), inc=ini)
    P.op("g", lambda g: g.memset(ones_rf[:, :], 1.0), inc=ini)
    for e_ in ("t", "a", "v", "y"):
        P.wait(e_, ini)
    P.dma("g", dW, maskb[:, :], din["maskc"])
    P.dma("g", dW, pswb[:, :], din["pswap"])
    P.dma("g", dW, colsb[:, :], din["colsel"])
    P.dma("g", dW, bqb[:, :], din["bqa"])
    P.dma("g", dW, bkb[:, :], din["bka"])
    P.dma("g", dW, bvb[:, :], din["bva"])
    P.dma("g", dW, bob[:, :], din["boa"])
    v_colsel = P.val(dW)
    P.wait("y", dW, v_colsel)
    P.q["y"].append(lambda sy: sy.reg_load(rreg, colsb[0:1, 0:1]))

    def barrier_dma(e):
        for s in (dW, dY, dS):
            P.wait(e, s)
        P.wait(e, cc)

    a1v = ar1[:, :].rearrange("p (dc t) -> p dc t", dc=NDC)

    def stats(x_ap):
        sum_r = psA[0:1, 0:512]
        sq_r = psA[0:1, 512:1024]
        P.wait("v", cT)
        P.wait("y", dW)
        P.wait("a", dW)
        P.wait("a", dY)
        slot_cv = [0] * 4
        slot_ca = [0] * 4
        sq_ct = [0, 0]
        row_cv = 0
        for tc in range(NTC):
            if tc > 0:
                P.wait("t", cV, row_cv)
            for dc in range(NDC):
                k = tc * NDC + dc
                i = k % 4
                qe, qs = ("y", dY) if k % 2 == 0 else ("a", dS)
                P.wait(qe, cV, slot_cv[i])
                P.wait(qe, cA, slot_ca[i])
                vx = P.dma(qe, qs, sl512(xst, i),
                           x_ap[dc * 128:(dc + 1) * 128, tc * 512:(tc + 1) * 512])
                P.wait("v", qs, vx)
                P.op("v", lambda v, i=i, dc=dc, tc=tc: v.tensor_copy(
                    a1v[:, dc, tc * 512:(tc + 1) * 512], sl512(xst, i)), inc=cV)
                slot_cv[i] = P.val(cV)
                P.wait("a", qs, vx)
                P.wait("a", cT, sq_ct[k % 2])
                P.op("a", lambda a, i=i, k=k: a.activation(
                    sl512(sqst, k % 2), sl512(xst, i), AF.Square), inc=cA)
                slot_ca[i] = P.val(cA)
                P.wait("t", cV, slot_cv[i])
                P.wait("t", cA, slot_ca[i])
                P.op("t", lambda t, dc=dc, tc=tc: t.matmul(
                    sum_r, ones_c[:, :], a1v[:, dc, tc * 512:(tc + 1) * 512],
                    start=(dc == 0), stop=(dc == NDC - 1)), inc=cT)
                P.op("t", lambda t, k=k, dc=dc: t.matmul(
                    sq_r, ones_c[:, :], sl512(sqst, k % 2),
                    start=(dc == 0), stop=(dc == NDC - 1)), inc=cT)
                sq_ct[k % 2] = P.val(cT)
            P.wait("v", cT)
            P.op("v", lambda v, tc=tc: v.tensor_scalar(
                m_row[0:1, tc * 512:(tc + 1) * 512], sum_r, 1.0 / D, None, ALU.mult), inc=cV)
            P.op("v", lambda v, tc=tc: v.tensor_scalar(
                q_row[0:1, tc * 512:(tc + 1) * 512], sq_r, 1.0 / D, None, ALU.mult), inc=cV)
            row_cv = P.val(cV)
        P.op("v", lambda v: v.tensor_tensor(r_row[:, :], m_row[:, :], m_row[:, :], ALU.mult), inc=cV)
        P.wait("v", cV)
        P.op("v", lambda v: v.tensor_tensor(q_row[:, :], q_row[:, :], r_row[:, :], ALU.subtract), inc=cV)
        P.wait("v", cV)
        P.op("v", lambda v: v.tensor_scalar(q_row[:, :], q_row[:, :], float(EPS), None, ALU.add), inc=cV)
        P.wait("a", cV)
        P.op("a", lambda a: a.activation(q_row[:, :], q_row[:, :], AF.Ln), inc=cA)
        P.wait("a", cA)
        P.op("a", lambda a: a.activation(r_row[:, :], q_row[:, :], AF.Exp, scale=-0.5), inc=cA)
        return slot_cv, slot_ca

    def replicate_tc(tc):
        P.wait("t", cA)
        P.wait("t", cV)
        P.op("t", lambda t, tc=tc: t.matmul(
            psB[:, 0:512], ones_rf[:, :], m_row[0:1, tc * 512:(tc + 1) * 512],
            start=True, stop=True), inc=cT)
        P.op("t", lambda t, tc=tc: t.matmul(
            psB[:, 512:1024], ones_rf[:, :], r_row[0:1, tc * 512:(tc + 1) * 512],
            start=True, stop=True), inc=cT)
        P.wait("a", cT)
        P.wait("a", cV)
        P.op("a", lambda a: a.copy(mrep[:, :], psB[:, 0:512]), inc=cA)
        P.op("a", lambda a: a.copy(rrep[:, :], psB[:, 512:1024]), inc=cA)
        return P.val(cA)

    def y_produce(x_ap, dst_dram, st_cv, st_ca):
        slot_cv = list(st_cv)
        slot_ca = list(st_ca)
        for tc in range(NTC):
            v_rep = replicate_tc(tc)
            for dc in range(NDC):
                k = tc * NDC + dc
                i = k % 4
                qe, qs = ("y", dY) if k % 2 == 0 else ("a", dS)
                P.wait(qe, cV, slot_cv[i])
                P.wait(qe, cA, slot_ca[i])
                slot_ca[i] = 0
                vx = P.dma(qe, qs, sl512(xst, i),
                           x_ap[dc * 128:(dc + 1) * 128, tc * 512:(tc + 1) * 512])
                P.wait("v", qs, vx)
                P.wait("v", cA, v_rep)
                P.op("v", lambda v, i=i, k=k, dc=dc, tc=tc: v.tensor_tensor(
                    sl512(fst, k % 2), sl512(xst, i), mrep[:, :], ALU.subtract), inc=cV)
                P.op("v", lambda v, k=k, dc=dc, tc=tc: v.tensor_tensor(
                    a1v[:, dc, tc * 512:(tc + 1) * 512], sl512(fst, k % 2), rrep[:, :],
                    ALU.mult), inc=cV)
                slot_cv[i] = P.val(cV)
                if dst_dram is not None:
                    P.wait(qe, cV, slot_cv[i])
                    P.dma(qe, qs, dst_dram[dc * 128:(dc + 1) * 128, tc * 512:(tc + 1) * 512],
                          a1v[:, dc, tc * 512:(tc + 1) * 512])

    wslot = [wst[:, i * 2048:(i + 1) * 2048] for i in range(3)]

    def wload(slot, src_ap):
        return P.dma("g", dW, slot, src_ap)

    def attn_sublayer(x_prev, x_next, is_b):
        wq_i, wk_i, wv_i, wo_i = (
            (din["wqb"], din["wkb"], din["wvb"], din["wob"]) if is_b else
            (din["wqa"], din["wka"], din["wva"], din["woa"]))

        if is_b:
            P.op("g", lambda g: g.memset(alib[:, :], 0.0), inc=ini)
        else:
            P.dma("g", dW, alib[:, :], din["alibi"])

        st_cv, st_ca = stats(x_prev)
        y_produce(x_prev, y_scr, st_cv, st_ca)
        v_ywr = P.val(dY)
        v_ysr = P.val(dS)
        P.wait("g", dY, v_ywr)
        P.wait("g", dS, v_ysr)
        P.op("g", lambda g: g.collective_compute(
            "AllGather", ALU.bypass,
            replica_groups=[[0, 1], [2, 3], [4, 5], [6, 7]],
            ins=[y_scr], outs=[y_gth]), inc=cc)
        v_cc = P.val(cc)

        wv_all = ar2[:, 0:16384].rearrange("p (dc e) -> p dc e", dc=NDC)
        qraw_st = [ar2[:, 16384 + i * 512:16384 + (i + 1) * 512] for i in range(2)]
        csc_sb = ar2[:, 20480:24576]
        css_sb = ar2[:, 24576:28672]
        P.wait("g", cT)
        if is_b:
            P.dma("g", dW, csc_sb, din["csc"])
            P.dma("g", dW, css_sb, din["css"])
        v_wv = P.dma("g", dW, wv_all, wv_i.rearrange("p (dc e) -> p dc e", dc=NDC))
        P.op("g", lambda g: g.memset(cells[:, :], 0.0), inc=ini)
        v_cells0 = P.val(ini)
        P.wait("v", ini, v_cells0)

        yslab = ar1[:, :].rearrange("p (dc t) -> p dc t", dc=NDC)

        def proj_evac_qk(quad, oc, sl, tq, which, i):
            tile = sl512(btl, i % 6)
            col = slice(sl * 2048 + tq * 512, sl * 2048 + (tq + 1) * 512)
            scr = k_scr if which == "k" else q_scr
            bias_sb = bkb if which == "k" else bqb
            if not is_b:
                P.wait("a", cT)
                P.wait("a", dY)
                P.op("a", lambda a, quad=quad, tq=tq, tile=tile, oc=oc, bias_sb=bias_sb: a.add(
                    tile, sl512(quad, tq), bias_sb[:, oc:oc + 1]), inc=cA)
                va = P.val(cA)
                P.wait("y", cA, va)
                P.dma("y", dY, scr[oc * 128:(oc + 1) * 128, col], tile)
            else:
                qraw = qraw_st[i % 2]
                P.wait("a", cT)
                P.wait("a", cV)
                P.op("a", lambda a, quad=quad, tq=tq, qraw=qraw: a.copy(
                    qraw, sl512(quad, tq)), inc=cA)
                va = P.val(cA)
                P.wait("t", cA, va)
                P.op("t", lambda t, quad=quad, tq=tq, qraw=qraw: t.matmul(
                    sl512(quad, tq), pswb[:, :], qraw, start=True, stop=True), inc=cT)
                vt = P.val(cT)
                P.wait("v", cA, va)
                P.wait("v", cA)
                P.wait("v", dY)
                P.wait("v", dS)
                P.op("v", lambda v, qraw=qraw, c=col: v.tensor_tensor(
                    sl512(xst, 0), qraw, csc_sb[:, c], ALU.mult), inc=cV)
                P.wait("v", cT, vt)
                P.op("v", lambda v, quad=quad, tq=tq, c=col: v.tensor_tensor(
                    sl512(xst, 1), sl512(quad, tq), css_sb[:, c], ALU.mult), inc=cV)
                P.wait("v", cV)
                P.op("v", lambda v, tile=tile: v.tensor_tensor(
                    tile, sl512(xst, 0), sl512(xst, 1), ALU.add), inc=cV)
                vv = P.val(cV)
                P.wait("y", cV, vv)
                P.dma("y", dY, scr[oc * 128:(oc + 1) * 128, col], tile)
            cidx = (0 if which == "q" else 8) + oc
            P.wait("v", cA)
            P.wait("v", cV)
            P.op("v", lambda v, i=i, tile=tile: v.tensor_tensor(
                sl512(sqst, i % 2), tile, tile, ALU.mult), inc=cV)
            vsq = P.val(cV)
            P.wait("t", cV, vsq)
            P.op("t", lambda t, quad=quad, tq=tq, i=i: t.matmul(
                quad[0:1, tq * 512:(tq + 1) * 512], ones_c[:, :], sl512(sqst, i % 2),
                start=True, stop=True), inc=cT)
            vnm = P.val(cT)
            P.wait("v", cT, vnm)
            P.op("v", lambda v, quad=quad, tq=tq: v.reduce_max(
                rowt[0:1, 0:1], quad[0:1, tq * 512:(tq + 1) * 512], axis=AX.X), inc=cV)
            P.wait("v", cV)
            P.op("v", lambda v, cidx=cidx: v.tensor_tensor(
                cells[0:1, cidx:cidx + 1], cells[0:1, cidx:cidx + 1], rowt[0:1, 0:1],
                ALU.max), inc=cV)

        wfree = [P.val(cT)] * 3
        ldi = [0]
        bank_a = [P.val(cA), P.val(cA)]
        bank_v = [P.val(cV), P.val(cV)]
        for sl in range(2):
            P.wait("y", cc, v_cc)
            P.wait("y", cT)
            src = y_gth[sl * D:(sl + 1) * D, :].rearrange("(dc p) t -> p dc t", p=128)
            for half in range(2):
                P.dma("y", dY, yslab[:, half * 8:(half + 1) * 8, :],
                      src[:, half * 8:(half + 1) * 8, :])
            vy = P.val(dY)
            P.wait("t", dY, vy)

            for which in ("q", "k"):
                w_in = wq_i if which == "q" else wk_i
                for oc in range(NH):
                    si = ldi[0] % 3
                    ldi[0] += 1
                    s_ = wslot[si]
                    P.wait("g", cT, wfree[si])
                    vw = wload(s_, w_in[oc])
                    bk = oc % 2
                    quad = psA if bk == 0 else psB
                    P.wait("t", dW, vw)
                    P.wait("t", cA, bank_a[bk])
                    P.wait("t", cV, bank_v[bk])
                    for dc in range(NDC):
                        for tq in range(4):
                            P.op("t", lambda t, quad=quad, s_=s_, dc=dc, tq=tq: t.matmul(
                                sl512(quad, tq), s_[:, dc * 128:(dc + 1) * 128],
                                yslab[:, dc, tq * 512:(tq + 1) * 512],
                                start=(dc == 0), stop=(dc == NDC - 1)),
                                inc=(cT if (dc == NDC - 1 and tq == 3) else None))
                    wfree[si] = P.val(cT)
                    for tq in range(4):
                        proj_evac_qk(quad, oc, sl, tq, which, oc * 4 + tq)
                    bank_a[bk] = P.val(cA)
                    bank_v[bk] = P.val(cV)

            P.wait("t", dW, v_wv)
            for tcb in range(16):
                bk = tcb % 2
                quad = psA if bk == 0 else psB
                P.wait("t", cA, bank_a[bk])
                P.wait("t", cV, bank_v[bk])
                last_dc_stop = is_b
                for dc in range(NDC):
                    for ec in range(2):
                        P.op("t", lambda t, quad=quad, dc=dc, ec=ec, tcb=tcb, lds=last_dc_stop: t.matmul(
                            sl512(quad, ec), yslab[:, dc, tcb * 128:(tcb + 1) * 128],
                            wv_all[:, dc, ec * 512:(ec + 1) * 512],
                            start=(dc == 0),
                            stop=(dc == NDC - 1 and lds)),
                            inc=(cT if (dc == NDC - 1 and ec == 1 and last_dc_stop) else None))
                if not is_b:
                    for ec in range(2):
                        P.op("t", lambda t, quad=quad, ec=ec: t.matmul(
                            sl512(quad, ec), ones_rb[:, :],
                            bvb[0:1, ec * 512:(ec + 1) * 512],
                            start=False, stop=True), inc=(cT if ec == 1 else None))
                vv_ = P.val(cT)
                P.wait("a", cT, vv_)
                for ec in range(2):
                    ti = (tcb * 2 + ec) % 6
                    P.wait("a", dY)
                    P.op("a", lambda a, quad=quad, ec=ec, ti=ti: a.copy(
                        sl512(btl, ti), sl512(quad, ec)), inc=cA)
                    va = P.val(cA)
                    P.wait("y", cA, va)
                    P.dma("y", dY,
                          v_scr[(sl * 2048 + tcb * 128):(sl * 2048 + (tcb + 1) * 128),
                                ec * 512:(ec + 1) * 512], sl512(btl, ti))
                bank_a[bk] = P.val(cA)

        P.wait("v", cT)
        P.wait("v", cV)
        P.op("v", lambda v: v.tensor_tensor(
            cells[0:1, 16:24], cells[0:1, 0:8], cells[0:1, 8:16], ALU.add), inc=cV)
        P.wait("v", cV)
        P.op("v", lambda v: v.tensor_scalar(
            cells[0:1, 16:24], cells[0:1, 16:24], SCALE / 2.0, 8.0, ALU.mult, ALU.add), inc=cV)
        v_M = P.val(cV)

        kh = [ar2[:, 0:4096], ar2[:, 4096:8192]]
        qh = [ar2[:, 8192:12288], ar2[:, 12288:16384]]
        vh = [ar2[:, 16384:20480], ar2[:, 20480:24576]]
        wal = [ar2[0:1, 24576:28672], ar2[0:1, 28672:32768]]

        def head_loads(h):
            i = h % 2
            P.wait("a", cT)
            P.dma("a", dS, kh[i], k_scr[h * 128:(h + 1) * 128, :])
            P.dma("a", dS, qh[i], q_scr[h * 128:(h + 1) * 128, :])
            if not is_b:
                P.wait("a", cV)
                P.dma("a", dS, wal[i], din["wal"][h])
            P.wait("y", cT)
            P.wait("y", cV)
            vsrc = v_scr[:, :].rearrange("(kb p) e -> p kb e", p=128)
            for piece in range(4):
                P.dma("y", dY,
                      vh[i].rearrange("p (kb e) -> p kb e", kb=32)[:, piece * 8:(piece + 1) * 8, :],
                      vsrc[:, piece * 8:(piece + 1) * 8, (h * 128):((h + 1) * 128)])
            return P.val(dS), P.val(dY)

        def build_bias_slab(h):
            P.wait("t", cV, v_M)
            P.wait("t", cA)
            P.op("t", lambda t, h=h: t.matmul(
                psB[:, 1024:1025], ones_rf[:, :], cells[0:1, 16 + h:17 + h],
                start=True, stop=True), inc=cT)
            vt = P.val(cT)
            P.wait("a", cT, vt)
            P.wait("a", cV)
            P.op("a", lambda a, h=h: a.copy(mcol[:, h % 2:h % 2 + 1], psB[:, 1024:1025]), inc=cA)
            va = P.val(cA)
            P.wait("v", cA, va)
            P.op("v", lambda v, h=h: v.tensor_scalar(
                biassl[:, (h % 2) * 32:(h % 2) * 32 + 32], alib[:, h * 32:(h + 1) * 32],
                mcol[:, h % 2:h % 2 + 1], None, ALU.subtract), inc=cV)
            return P.val(cV)

        sems = [None] * (NH + 1)
        bs_v = [None] * (NH + 1)
        sems[0] = head_loads(0)
        bs_v[0] = build_bias_slab(0)
        KBc = [0]
        exp_v = {}
        psb_free = [P.val(cV)]
        wrslot = [sgt[:, 0:512], sgt[:, 512:1024]]
        wr_ca = [0, 0]
        wr_cv = [0, 0]
        wr_mmca = [P.val(cA)]
        for h in range(NH):
            if h + 1 < NH:
                sems[h + 1] = head_loads(h + 1)
                bs_v[h + 1] = build_bias_slab(h + 1)
            vS, vY = sems[h]
            P.wait("t", dS, vS)
            P.wait("t", dY, vY)
            i = h % 2
            for qc in range(NTTC):
                nkb = 4 * (qc + 1)
                kg0 = KBc[0]
                if not is_b:
                    P.wait("t", cA, wr_mmca[0])
                    P.op("t", lambda t, i=i, qc=qc: t.matmul(
                        psB[:, 1536:2048], ones_rb[:, :],
                        wal[i][0:1, qc * 512:(qc + 1) * 512],
                        start=True, stop=True), inc=cT)
                    vwm = P.val(cT)
                    P.wait("a", cT, vwm)
                    P.wait("a", cV, wr_cv[qc % 2])
                    P.op("a", lambda a, qc=qc: a.copy(wrslot[qc % 2], psB[:, 1536:2048]), inc=cA)
                    wr_ca[qc % 2] = P.val(cA)
                    wr_mmca[0] = wr_ca[qc % 2]

                def u_scores(kb, kg, i=i, h=h, qc=qc):
                    sl = sl512(psA, kg % 4)
                    P.wait("t", cA, exp_v.get(kg - 4, 0))
                    P.op("t", lambda t, sl=sl, kb=kb, i=i, qc=qc: t.matmul(
                        sl, kh[i][:, kb * 128:(kb + 1) * 128],
                        qh[i][:, qc * 512:(qc + 1) * 512],
                        start=True, stop=True), inc=cT)
                    vsc = P.val(cT)
                    dj = kb - 4 * qc
                    vdve = None
                    if not is_b:
                        P.wait("v", cT, vsc)
                        P.wait("v", cA, wr_ca[qc % 2])
                        P.op("v", lambda v, sl=sl, qc=qc: v.tensor_tensor(
                            sl, sl, wrslot[qc % 2], ALU.add), inc=cV)
                        vdve = P.val(cV)
                    if dj >= 0:
                        P.wait("v", cT, vsc)
                        P.op("v", lambda v, sl=sl, dj=dj: v.tensor_tensor(
                            sl, sl, maskb[:, 384 - 128 * dj:896 - 128 * dj],
                            ALU.add), inc=cV)
                        vdve = P.val(cV)
                    P.wait("a", cT, vsc)
                    if vdve is not None:
                        P.wait("a", cV, vdve)
                    P.wait("a", cV, bs_v[h])
                    P.op("a", lambda a, sl=sl, kg=kg, kb=kb, h=h: a.activation(
                        sl512(ptl, kg % 4), sl,
                        AF.Exp, bias=biassl[:, (h % 2) * 32 + kb:(h % 2) * 32 + kb + 1],
                        scale=SCALE), inc=cA)
                    exp_v[kg] = P.val(cA)

                def u_sums_av(kb, kg, first, i=i):
                    P.wait("t", cA, exp_v[kg])
                    if first:
                        P.wait("t", cV, psb_free[0])
                    P.op("t", lambda t, kg=kg, first=first: t.matmul(
                        psB[0:1, 512:1024], ones_c[:, :], sl512(ptl, kg % 4),
                        start=first, stop=True), inc=None)
                    P.op("t", lambda t, kg=kg, kb=kb, i=i, first=first: t.matmul(
                        psB[:, 0:512], vh[i][:, kb * 128:(kb + 1) * 128],
                        sl512(ptl, kg % 4),
                        start=first, stop=True), inc=cT)

                for kb in range(nkb):
                    u_scores(kb, kg0 + kb)
                    if kb >= 2:
                        u_sums_av(kb - 2, kg0 + kb - 2, kb - 2 == 0)
                for kb in range(max(0, nkb - 2), nkb):
                    u_sums_av(kb, kg0 + kb, kb == 0)
                KBc[0] = kg0 + nkb
                if not is_b:
                    wr_cv[qc % 2] = P.val(cV)

                v_avdone = P.val(cT)
                P.wait("v", cT, v_avdone)
                P.op("v", lambda v: v.reciprocal(rowt[:, :], psB[0:1, 512:1024]), inc=cV)
                vrec = P.val(cV)
                P.wait("t", cV, vrec)
                P.op("t", lambda t: t.matmul(
                    psB[:, 1024:1536], ones_rf[:, :], rowt[0:1, :], start=True, stop=True), inc=cT)
                vrep = P.val(cT)
                P.wait("a", cT, vrep)
                P.wait("a", cV)
                P.op("a", lambda a, qc=qc: a.copy(sl512(fst2, qc % 2), psB[:, 1024:1536]), inc=cA)
                vrepc = P.val(cA)
                ctile = sl512(btl, qc % 6)
                P.wait("v", cA, vrepc)
                P.wait("v", dY)
                P.op("v", lambda v, qc=qc, ctile=ctile: v.tensor_tensor(
                    ctile, psB[:, 0:512], sl512(fst2, qc % 2), ALU.mult), inc=cV)
                vctx = P.val(cV)
                psb_free[0] = vctx
                P.wait("y", cV, vctx)
                P.dma("y", dY, c_scr[h * 128:(h + 1) * 128, qc * 512:(qc + 1) * 512], ctile)

        v_cwr = P.val(dY)
        P.wait("g", dY, v_cwr)
        P.wait("g", dS)
        P.op("g", lambda g: g.collective_compute(
            "AllGather", ALU.bypass,
            replica_groups=[[0, 1], [2, 3], [4, 5], [6, 7]],
            ins=[c_scr], outs=[c_gth]), inc=cc)
        v_cc2 = P.val(cc)

        ctxb = ar1[:, :].rearrange("p (dc t) -> p dc t", dc=NDC)
        P.wait("y", cc, v_cc2)
        P.wait("y", cT)
        for piece in range(4):
            def dma_piece(sy, piece=piece):
                sy.reg_add(rtmp, rreg, piece * 4 * 128 * TT)
                return sy.dma_start(
                    out=ctxb[:, piece * 4:(piece + 1) * 4, :],
                    in_=bass.AP(c_gth.tensor, rtmp, [[TT, 128], [128 * TT, 4], [1, T]]))
            P.op("y", dma_piece, inc=dY, n=16)
        v_ctxld = P.val(dY)
        P.wait("t", dY, v_ctxld)

        P.wait("t", cA)
        bank_v = [P.val(cV), P.val(cV)]
        for oc in range(16):
            si = ldi[0] % 3
            ldi[0] += 1
            s_ = wslot[si]
            P.wait("g", cT, wfree[si])
            vw = wload(s_, wo_i[oc])
            bk = oc % 2
            quad = psA if bk == 0 else psB
            P.wait("t", dW, vw)
            P.wait("t", cV, bank_v[bk])
            for dc in range(NDC):
                for tq in range(4):
                    P.op("t", lambda t, quad=quad, s_=s_, dc=dc, tq=tq: t.matmul(
                        sl512(quad, tq), s_[:, dc * 128:(dc + 1) * 128],
                        ctxb[:, dc, tq * 512:(tq + 1) * 512],
                        start=(dc == 0), stop=(dc == NDC - 1)),
                        inc=(cT if (dc == NDC - 1 and tq == 3) else None))
            vq = P.val(cT)
            wfree[si] = vq
            for tq in range(4):
                i = (oc * 4 + tq) % 2
                P.wait("y", cV)
                vx = P.dma("y", dY, sl512(xst, i),
                           x_prev[oc * 128:(oc + 1) * 128, tq * 512:(tq + 1) * 512])
                P.wait("v", cT, vq)
                P.wait("v", dY, vx)
                if is_b:
                    P.op("v", lambda v, quad=quad, tq=tq, i=i: v.tensor_tensor(
                        sl512(fst, i), sl512(quad, tq), sl512(xst, i), ALU.add), inc=cV)
                else:
                    P.op("v", lambda v, quad=quad, tq=tq, i=i, oc=oc: v.scalar_tensor_tensor(
                        sl512(fst, i), sl512(quad, tq), bob[:, oc:oc + 1],
                        sl512(xst, i), ALU.add, ALU.add), inc=cV)
                vv = P.val(cV)
                P.wait("y", cV, vv)
                P.dma("y", dY, x_next[oc * 128:(oc + 1) * 128, tq * 512:(tq + 1) * 512],
                      sl512(fst, i))
            bank_v[bk] = P.val(cV)

    def ffn_sublayer(x_prev, x_next):
        st_cv, st_ca = stats(x_prev)
        y_produce(x_prev, None, st_cv, st_ca)
        y2 = ar1[:, :].rearrange("p (dc t) -> p dc t", dc=NDC)
        g_sl = ar2[:, :].rearrange("p (hc t) -> p hc t", hc=16)
        v_y2 = P.val(cV)
        P.wait("t", cV, v_y2)

        ff_wfree = [P.val(cT)] * 3
        fli = [0]
        for slab in range(4):
            for hc in range(16):
                habs = slab * 16 + hc
                gi = fli[0] % 3
                fli[0] += 1
                ui = fli[0] % 3
                fli[0] += 1
                sg_ = wslot[gi]
                su_ = wslot[ui]
                P.wait("g", cT, ff_wfree[gi])
                vwg = wload(sg_, din["wg"][habs])
                P.wait("g", cT, ff_wfree[ui])
                vwu = wload(su_, din["wu"][habs])
                P.wait("t", dW, vwg)
                P.wait("t", cA)
                for dc in range(NDC):
                    for tq in range(4):
                        P.op("t", lambda t, s_=sg_, dc=dc, tq=tq: t.matmul(
                            sl512(psA, tq), s_[:, dc * 128:(dc + 1) * 128],
                            y2[:, dc, tq * 512:(tq + 1) * 512],
                            start=(dc == 0), stop=(dc == NDC - 1)),
                            inc=(cT if (dc == NDC - 1 and tq == 3) else None))
                vg = P.val(cT)
                ff_wfree[gi] = vg
                P.wait("t", dW, vwu)
                P.wait("t", cV)
                for dc in range(NDC):
                    for tq in range(4):
                        P.op("t", lambda t, s_=su_, dc=dc, tq=tq: t.matmul(
                            sl512(psB, tq), s_[:, dc * 128:(dc + 1) * 128],
                            y2[:, dc, tq * 512:(tq + 1) * 512],
                            start=(dc == 0), stop=(dc == NDC - 1)),
                            inc=(cT if (dc == NDC - 1 and tq == 3) else None))
                vu = P.val(cT)
                ff_wfree[ui] = vu
                P.wait("a", cT, vg)
                P.wait("a", cV)
                P.op("a", lambda a: a.activation(sgt[:, :], psA[:, :], AF.Silu), inc=cA)
                vsilu = P.val(cA)
                P.wait("v", cA, vsilu)
                P.wait("v", cT, vu)
                P.op("v", lambda v, hc=hc: v.tensor_tensor(
                    g_sl[:, hc, :], sgt[:, :], psB[:, :], ALU.mult), inc=cV)
            v_g = P.val(cV)

            dn_a = [P.val(cA), P.val(cA)]
            dn_v = [P.val(cV), P.val(cV)]
            dn_w = []

            def dn_load(oc2, slab=slab, dn_w=dn_w):
                si2 = fli[0] % 3
                fli[0] += 1
                s2 = wslot[si2]
                P.wait("g", cT, ff_wfree[si2])
                vw2 = wload(s2, din["wd"][oc2][:, slab * 2048:(slab + 1) * 2048])
                dn_w.append((si2, s2, vw2))

            for oc in range(16):
                dn_load(oc)
                si, s_, vw = dn_w[oc]
                bk = oc % 2
                quad = psA if bk == 0 else psB
                P.wait("t", dW, vw)
                P.wait("t", cV, v_g)
                P.wait("t", cA)
                P.wait("t", cV)
                for hc in range(16):
                    for tq in range(4):
                        P.op("t", lambda t, quad=quad, s_=s_, hc=hc, tq=tq: t.matmul(
                            sl512(quad, tq), s_[:, hc * 128:(hc + 1) * 128],
                            g_sl[:, hc, tq * 512:(tq + 1) * 512],
                            start=(hc == 0), stop=(hc == 15)),
                            inc=(cT if (hc == 15 and tq == 3) else None))
                vdn = P.val(cT)
                ff_wfree[si] = vdn
                for tq in range(4):
                    i = (oc * 4 + tq) % 2
                    dst = x_next[oc * 128:(oc + 1) * 128, tq * 512:(tq + 1) * 512]
                    if slab == 0:
                        P.wait("y", cV)
                        vx = P.dma("y", dY, sl512(fst2, i),
                                   x_prev[oc * 128:(oc + 1) * 128, tq * 512:(tq + 1) * 512])
                        P.wait("v", cT, vdn)
                        P.wait("v", dY, vx)
                        P.op("v", lambda v, quad=quad, tq=tq, i=i: v.tensor_tensor(
                            sl512(fst, i), sl512(quad, tq), sl512(fst2, i), ALU.add), inc=cV)
                        vv = P.val(cV)
                        P.wait("y", cV, vv)
                        P.dma("y", dY, dst, sl512(fst, i))
                    else:
                        P.wait("a", cT, vdn)
                        P.wait("a", dW)
                        P.wait("a", cV)
                        P.op("a", lambda a, quad=quad, tq=tq, i=i: a.copy(
                            sl512(fst, i), sl512(quad, tq)), inc=cA)
                        va = P.val(cA)
                        P.wait("g", cA, va)
                        P.wait("g", dY)
                        P.dma("g", dW, dst, sl512(fst, i), accum=True)
                dn_a[bk] = P.val(cA)
                dn_v[bk] = P.val(cV)

    seq = [
        ("attn", x0, x_scr[0], False),
        ("ffn", x_scr[0], x_scr[1], None),
        ("attn", x_scr[1], x_scr[2], True),
        ("ffn", x_scr[2], out_t, None),
    ][:N_SUB]
    if N_SUB < 4:
        kind, xp, _, fl = seq[-1]
        seq[-1] = (kind, xp, out_t, fl)

    for kind, xp, xn, fl in seq:
        for e in ("y", "a", "g"):
            barrier_dma(e)
        if kind == "attn":
            attn_sublayer(xp, xn, fl)
        else:
            ffn_sublayer(xp, xn)

    for s in (dW, dY, dS):
        P.wait("y", s)
    P.wait("y", cc)

    if os.environ.get("MK_DEBUG") == "1":
        dbg_specs = [
            ("dbg_y", y_scr, [D, T], BF16), ("dbg_ygth", y_gth, [2 * D, T], BF16),
            ("dbg_q", q_scr, [NH * 128, TT], BF16), ("dbg_k", k_scr, [NH * 128, TT], BF16),
            ("dbg_v", v_scr, [TT, NH * 128], BF16), ("dbg_c", c_scr, [NH * 128, TT], BF16),
            ("dbg_cgth", c_gth, [D, TT], BF16), ("dbg_x1", x_scr[0], [D, T], F32),
            ("dbg_x2", x_scr[1], [D, T], F32), ("dbg_x3", x_scr[2], [D, T], F32),
        ]
        for nm, src, shp, dt in dbg_specs:
            dap = nc.dram_tensor(nm, shp, dt, kind="ExternalOutput").ap()
            P.dma("y", dY, dap, src)
        ar1d = nc.dram_tensor("dbg_ar1", [128, 32768], BF16, kind="ExternalOutput").ap()
        P.dma("y", dY, ar1d, ar1[:, :])
        P.wait("y", dY)

    with nc.Block() as block:
        @block.tensor
        def _(t):
            for f in P.q["t"]:
                f(t)

        @block.scalar
        def _(a):
            for f in P.q["a"]:
                f(a)

        @block.vector
        def _(v):
            for f in P.q["v"]:
                f(v)

        @block.gpsimd
        def _(g):
            for f in P.q["g"]:
                f(g)

        @block.sync
        def _(sy):
            for f in P.q["y"]:
                f(sy)

    es.close()
    return nc


_NC_CACHE = {}


def _blocks_img(w):
    O = w.shape[1]
    nb = O // 128
    out = np.empty((nb, 128, NDC * 128), np.float32)
    for oc in range(nb):
        blk = w[:, oc * 128:(oc + 1) * 128].reshape(NDC, 128, 128)
        out[oc] = blk.transpose(1, 0, 2).reshape(128, NDC * 128)
    return np.ascontiguousarray(out)


def _img(w):
    K, N = w.shape
    return np.ascontiguousarray(
        w.reshape(K // 128, 128, N).transpose(1, 0, 2).reshape(128, (K // 128) * N))


def _prep_inputs(inputs):
    f = lambda k: np.asarray(inputs[k], np.float32)
    x = f("L0_attn_a_x")
    wqa, wka, wva, woa = f("L0_attn_a_Wq"), f("L0_attn_a_Wk"), f("L0_attn_a_Wv"), f("L0_attn_a_Wo")
    bqa, bka, bva, boa = f("L0_attn_a_bq"), f("L0_attn_a_bk"), f("L0_attn_a_bv"), f("L0_attn_a_bo")
    wqb, wkb, wvb, wob = f("L0_attn_b_Wq"), f("L0_attn_b_Wk"), f("L0_attn_b_Wv"), f("L0_attn_b_Wo")
    wg, wu, wd = f("L0_ffn_w_gate"), f("L0_ffn_w_up"), f("L0_ffn_w_down")

    pos = np.arange(TT, dtype=np.float64)
    invf = 1.0 / (10000.0 ** (np.arange(64, dtype=np.float64) / 64.0))
    ang = pos[None, :] * invf[:, None]
    cosv = np.cos(ang); sinv = np.sin(ang)
    csc = np.ascontiguousarray(np.concatenate([cosv, cosv], 0).astype(ml_dtypes.bfloat16))
    css = np.ascontiguousarray(np.concatenate([-sinv, sinv], 0).astype(ml_dtypes.bfloat16))
    pswap = np.zeros((128, 128), np.float32)
    for jj in range(128):
        pswap[(jj + 64) % 128, jj] = 1.0
    pswap = np.ascontiguousarray(pswap.astype(ml_dtypes.bfloat16))
    ku = np.arange(128)[:, None]
    uu = np.arange(896)[None, :]
    maskc = np.ascontiguousarray(np.where(uu >= ku + 384, 0.0, -1e30).astype(ml_dtypes.bfloat16))

    wg_b = _blocks_img(np.ascontiguousarray(wg.T))
    wu_b = _blocks_img(np.ascontiguousarray(wu.T))
    wdT = np.ascontiguousarray(wd.T)
    wd_b = np.stack([_img(np.ascontiguousarray(wdT[:, oc * 128:(oc + 1) * 128]))
                     for oc in range(16)])
    woa_b = _blocks_img(woa)
    wob_b = _blocks_img(np.ascontiguousarray(wob.T))
    wqbT = np.ascontiguousarray(wqb.T); wkbT = np.ascontiguousarray(wkb.T)
    wvbT = np.ascontiguousarray(wvb.T)

    in_maps = []
    for c in range(8):
        b, p = c // 2, c % 2
        tok = slice(p * T, (p + 1) * T)
        hh = slice(p * 1024, (p + 1) * 1024)
        habs = np.arange(p * 8, p * 8 + 8)
        slopes = (2.0 ** (-8.0 * (habs + 1) / 16.0))
        ali = np.empty((128, NH * 32), np.float32)
        kb = np.arange(32)[None, :]
        ii = np.arange(128)[:, None]
        for j in range(NH):
            ali[:, j * 32:(j + 1) * 32] = slopes[j] * (kb * 128 + ii - (TT - 1))
        qq = np.arange(TT, dtype=np.float64)
        wal_t = np.ascontiguousarray(
            (slopes[:, None] * ((TT - 1) - qq[None, :]) / SCALE).astype(ml_dtypes.bfloat16)
        ).reshape(NH, 1, TT)
        m = {
            "x0": np.ascontiguousarray(x[b].T[:, tok]),
            "wqa": _blocks_img(np.ascontiguousarray(wqa[:, hh])),
            "wka": _blocks_img(np.ascontiguousarray(wka[:, hh])),
            "wva": _img(np.ascontiguousarray(wva[:, hh])),
            "woa": woa_b,
            "bqa": np.ascontiguousarray(bqa[hh].reshape(8, 128).T),
            "bka": np.ascontiguousarray(bka[hh].reshape(8, 128).T),
            "bva": np.ascontiguousarray(bva[hh].reshape(1, 1024)),
            "boa": np.ascontiguousarray(boa.reshape(16, 128).T),
            "wqb": _blocks_img(np.ascontiguousarray(wqbT[:, hh])),
            "wkb": _blocks_img(np.ascontiguousarray(wkbT[:, hh])),
            "wvb": _img(np.ascontiguousarray(wvbT[:, hh])),
            "wob": wob_b,
            "wg": wg_b, "wu": wu_b, "wd": wd_b,
            "alibi": np.ascontiguousarray(ali),
            "csc": csc, "css": css, "pswap": pswap, "maskc": maskc,
            "colsel": np.array([[p * T]], np.int32),
            "wal": wal_t,
        }
        in_maps.append(m)
    return in_maps


def kernel(**inputs):
    in_maps = _prep_inputs(inputs)
    if "nc" not in _NC_CACHE:
        _NC_CACHE["nc"] = build_nc()
    nc = _NC_CACHE["nc"]
    res = run_bass_kernel_spmd(nc, in_maps, core_ids=list(range(8)))
    out = np.empty((4, TT, D), np.float32)
    for c in range(8):
        b, p = c // 2, c % 2
        out[b, p * T:(p + 1) * T, :] = np.asarray(res.results[c]["out"]).T
    return out


# version 12
# speedup vs baseline: 1.2221x; 1.0237x over previous
import math
import os
from contextlib import ExitStack

import numpy as np
import ml_dtypes

import concourse.bass as bass
import concourse.mybir as mybir
from concourse.bass_utils import run_bass_kernel_spmd

F32 = mybir.dt.float32
BF16 = mybir.dt.bfloat16
I32 = mybir.dt.int32
AF = mybir.ActivationFunctionType
ALU = mybir.AluOpType
AX = mybir.AxisListType

D = 2048
T = 2048
TT = 4096
NH = 8
HD = 128
NDC = D // 128
NTC = T // 512
NTTC = TT // 512
SCALE = 1.0 / math.sqrt(HD)
EPS = 1e-5
N_SUB = int(os.environ.get("MK_NSUB", "4"))

ENGINES = ("t", "a", "v", "g", "y")


class Plan:
    def __init__(self):
        self.q = {e: [] for e in ENGINES}
        self.semv = {}
        self.seen = {e: {} for e in ENGINES}

    def val(self, sem):
        return self.semv.get(sem.name, 0)

    def inc(self, sem, n):
        v = self.val(sem) + n
        self.semv[sem.name] = v
        return v

    def wait(self, e, sem, v=None):
        if v is None:
            v = self.val(sem)
        if v <= 0 or self.seen[e].get(sem.name, -1) >= v:
            return
        self.seen[e][sem.name] = v
        self.q[e].append(lambda eng, s=sem, vv=v: eng.wait_ge(s, vv))

    def op(self, e, fn, inc=None, n=1):
        if inc is not None:
            self.inc(inc, n)
        self.q[e].append(
            lambda eng, f=fn, s=inc, nn=n: (f(eng).then_inc(s, nn) if s is not None else f(eng))
        )

    def dma(self, e, sem, out, in_, accum=False):
        if accum:
            self.op(e, lambda eng, o=out, i=in_: eng.dma_start(out=o, in_=i, accum_op=ALU.add),
                    inc=sem, n=16)
        else:
            self.op(e, lambda eng, o=out, i=in_: eng.dma_start(out=o, in_=i), inc=sem, n=16)
        return self.val(sem)


def build_nc():
    nc = bass.Bass("TRN2", target_bir_lowering=False)

    din = {}
    def inp(name, shape, dt=F32):
        din[name] = nc.dram_tensor(name, list(shape), dt, kind="ExternalInput").ap()
        return din[name]

    x0 = inp("x0", (D, T))
    inp("wqa", (NH, 128, NDC * 128)); inp("wka", (NH, 128, NDC * 128))
    inp("wva", (128, NDC * NH * 128))
    inp("woa", (16, 128, NDC * 128))
    inp("bqa", (128, NH)); inp("bka", (128, NH))
    inp("bva", (1, NH * 128)); inp("boa", (128, 16))
    inp("wqb", (NH, 128, NDC * 128)); inp("wkb", (NH, 128, NDC * 128))
    inp("wvb", (128, NDC * NH * 128))
    inp("wob", (16, 128, NDC * 128))
    inp("wg", (64, 128, NDC * 128)); inp("wu", (64, 128, NDC * 128))
    inp("wd", (16, 128, 64 * 128))
    inp("alibi", (128, NH * 32))
    inp("csc", (128, TT), BF16); inp("css", (128, TT), BF16)
    inp("pswap", (128, 128), BF16)
    inp("maskc", (128, 896), BF16)
    inp("colsel", (1, 1), I32)
    inp("wal", (NH, 1, TT), BF16)

    out_t = nc.dram_tensor("out", [D, T], F32, kind="ExternalOutput").ap()

    y_scr = nc.dram_tensor("y_scr", [D, T], BF16).ap()
    y_gth = nc.dram_tensor("y_gth", [2 * D, T], BF16).ap()
    q_scr = nc.dram_tensor("q_scr", [NH * 128, TT], BF16).ap()
    k_scr = nc.dram_tensor("k_scr", [NH * 128, TT], BF16).ap()
    v_scr = nc.dram_tensor("v_scr", [TT, NH * 128], BF16).ap()
    c_scr = nc.dram_tensor("c_scr", [NH * 128, TT], BF16).ap()
    c_gth = nc.dram_tensor("c_gth", [D, TT], BF16).ap()
    x_scr = [nc.dram_tensor(f"x{i}_scr", [D, T], F32).ap() for i in (1, 2, 3)]

    es = ExitStack()
    sb = lambda name, shape, dt: es.enter_context(nc.sbuf_tensor(name, list(shape), dt))

    ar1 = sb("ar1", (128, 32768), BF16)
    ar2 = sb("ar2", (128, 32768), BF16)
    wst = sb("wst", (128, 6144), BF16)
    ones_c = sb("ones_c", (128, 1), BF16)
    ones_rb = sb("ones_rb", (1, 128), BF16)
    ones_rf = sb("ones_rf", (1, 128), F32)
    m_row = sb("m_row", (1, T), F32)
    q_row = sb("q_row", (1, T), F32)
    r_row = sb("r_row", (1, T), F32)
    _ = r_row
    mrep = sb("mrep", (128, 512), F32)
    rrep = sb("rrep", (128, 512), F32)
    xst = sb("xst", (128, 2048), F32)
    sqst = sb("sqst", (128, 1024), BF16)
    fst = sb("fst", (128, 1024), F32)
    fst2 = sb("fst2", (128, 1024), F32)
    btl = sb("btl", (128, 6 * 512), BF16)
    ptl = sb("ptl", (128, 2048), BF16)
    sgt = sb("sgt", (128, 2048), BF16)
    biassl = sb("biassl", (128, 64), F32)
    mcol = sb("mcol", (128, 2), F32)
    alib = sb("alib", (128, NH * 32), F32)
    maskb = sb("maskb", (128, 896), BF16)
    pswb = sb("pswb", (128, 128), BF16)
    bqb = sb("bqb", (128, NH), F32)
    bkb = sb("bkb", (128, NH), F32)
    bvb = sb("bvb", (1, NH * 128), BF16)
    bob = sb("bob", (128, 16), F32)
    colsb = sb("colsb", (1, 1), I32)
    cells = sb("cells", (1, 64), F32)
    rowt = r_row[0:1, 1536:2048]
    psA = es.enter_context(nc.psum_tensor("psA", [128, 2048], F32))
    psB = es.enter_context(nc.psum_tensor("psB", [128, 2048], F32))

    dW = es.enter_context(nc.semaphore("dW"))
    dY = es.enter_context(nc.semaphore("dY"))
    dS = es.enter_context(nc.semaphore("dS"))
    cT = es.enter_context(nc.semaphore("cT"))
    cA = es.enter_context(nc.semaphore("cA"))
    cV = es.enter_context(nc.semaphore("cV"))
    cc = es.enter_context(nc.semaphore("ccs"))
    ini = es.enter_context(nc.semaphore("ini"))

    P = Plan()

    rreg = nc.alloc_register(mybir.EngineType.SP, "rctx")
    rtmp = nc.alloc_register(mybir.EngineType.SP, "rtmp")

    def sl512(t_, i):
        return t_[:, i * 512:(i + 1) * 512]

    P.op("g", lambda g: g.memset(ones_c[:, :], 1.0), inc=ini)
    P.op("g", lambda g: g.memset(ones_rb[:, :], 1.0), inc=ini)
    P.op("g", lambda g: g.memset(ones_rf[:, :], 1.0), inc=ini)
    for e_ in ("t", "a", "v", "y"):
        P.wait(e_, ini)
    P.dma("g", dW, maskb[:, :], din["maskc"])
    P.dma("g", dW, pswb[:, :], din["pswap"])
    P.dma("g", dW, colsb[:, :], din["colsel"])
    P.dma("g", dW, bqb[:, :], din["bqa"])
    P.dma("g", dW, bkb[:, :], din["bka"])
    P.dma("g", dW, bvb[:, :], din["bva"])
    P.dma("g", dW, bob[:, :], din["boa"])
    v_colsel = P.val(dW)
    P.wait("y", dW, v_colsel)
    P.q["y"].append(lambda sy: sy.reg_load(rreg, colsb[0:1, 0:1]))

    def barrier_dma(e):
        for s in (dW, dY, dS):
            P.wait(e, s)
        P.wait(e, cc)

    a1v = ar1[:, :].rearrange("p (dc t) -> p dc t", dc=NDC)

    def stats(x_ap):
        sum_r = psA[0:1, 0:512]
        sq_r = psA[0:1, 512:1024]
        P.wait("v", cT)
        P.wait("y", dW)
        P.wait("a", dW)
        P.wait("a", dY)
        slot_cv = [0] * 4
        slot_ca = [0] * 4
        sq_ct = [0, 0]
        row_cv = 0
        for tc in range(NTC):
            if tc > 0:
                P.wait("t", cV, row_cv)
            for dc in range(NDC):
                k = tc * NDC + dc
                i = k % 4
                qe, qs = ("y", dY) if k % 2 == 0 else ("a", dS)
                P.wait(qe, cV, slot_cv[i])
                P.wait(qe, cA, slot_ca[i])
                vx = P.dma(qe, qs, sl512(xst, i),
                           x_ap[dc * 128:(dc + 1) * 128, tc * 512:(tc + 1) * 512])
                P.wait("v", qs, vx)
                P.op("v", lambda v, i=i, dc=dc, tc=tc: v.tensor_copy(
                    a1v[:, dc, tc * 512:(tc + 1) * 512], sl512(xst, i)), inc=cV)
                slot_cv[i] = P.val(cV)
                P.wait("a", qs, vx)
                P.wait("a", cT, sq_ct[k % 2])
                P.op("a", lambda a, i=i, k=k: a.activation(
                    sl512(sqst, k % 2), sl512(xst, i), AF.Square), inc=cA)
                slot_ca[i] = P.val(cA)
                P.wait("t", cV, slot_cv[i])
                P.wait("t", cA, slot_ca[i])
                P.op("t", lambda t, dc=dc, tc=tc: t.matmul(
                    sum_r, ones_c[:, :], a1v[:, dc, tc * 512:(tc + 1) * 512],
                    start=(dc == 0), stop=(dc == NDC - 1)), inc=cT)
                P.op("t", lambda t, k=k, dc=dc: t.matmul(
                    sq_r, ones_c[:, :], sl512(sqst, k % 2),
                    start=(dc == 0), stop=(dc == NDC - 1)), inc=cT)
                sq_ct[k % 2] = P.val(cT)
            P.wait("v", cT)
            P.op("v", lambda v, tc=tc: v.tensor_scalar(
                m_row[0:1, tc * 512:(tc + 1) * 512], sum_r, 1.0 / D, None, ALU.mult), inc=cV)
            P.op("v", lambda v, tc=tc: v.tensor_scalar(
                q_row[0:1, tc * 512:(tc + 1) * 512], sq_r, 1.0 / D, None, ALU.mult), inc=cV)
            row_cv = P.val(cV)
        P.op("v", lambda v: v.tensor_tensor(r_row[:, :], m_row[:, :], m_row[:, :], ALU.mult), inc=cV)
        P.wait("v", cV)
        P.op("v", lambda v: v.tensor_tensor(q_row[:, :], q_row[:, :], r_row[:, :], ALU.subtract), inc=cV)
        P.wait("v", cV)
        P.op("v", lambda v: v.tensor_scalar(q_row[:, :], q_row[:, :], float(EPS), None, ALU.add), inc=cV)
        P.wait("a", cV)
        P.op("a", lambda a: a.activation(q_row[:, :], q_row[:, :], AF.Ln), inc=cA)
        P.wait("a", cA)
        P.op("a", lambda a: a.activation(r_row[:, :], q_row[:, :], AF.Exp, scale=-0.5), inc=cA)
        return slot_cv, slot_ca

    def replicate_tc(tc):
        P.wait("t", cA)
        P.wait("t", cV)
        P.op("t", lambda t, tc=tc: t.matmul(
            psB[:, 0:512], ones_rf[:, :], m_row[0:1, tc * 512:(tc + 1) * 512],
            start=True, stop=True), inc=cT)
        P.op("t", lambda t, tc=tc: t.matmul(
            psB[:, 512:1024], ones_rf[:, :], r_row[0:1, tc * 512:(tc + 1) * 512],
            start=True, stop=True), inc=cT)
        P.wait("a", cT)
        P.wait("a", cV)
        P.op("a", lambda a: a.copy(mrep[:, :], psB[:, 0:512]), inc=cA)
        P.op("a", lambda a: a.copy(rrep[:, :], psB[:, 512:1024]), inc=cA)
        return P.val(cA)

    def y_produce(x_ap, dst_dram, st_cv, st_ca):
        slot_cv = list(st_cv)
        slot_ca = list(st_ca)
        for tc in range(NTC):
            v_rep = replicate_tc(tc)
            for dc in range(NDC):
                k = tc * NDC + dc
                i = k % 4
                qe, qs = ("y", dY) if k % 2 == 0 else ("a", dS)
                P.wait(qe, cV, slot_cv[i])
                P.wait(qe, cA, slot_ca[i])
                slot_ca[i] = 0
                vx = P.dma(qe, qs, sl512(xst, i),
                           x_ap[dc * 128:(dc + 1) * 128, tc * 512:(tc + 1) * 512])
                P.wait("v", qs, vx)
                P.wait("v", cA, v_rep)
                P.op("v", lambda v, i=i, k=k, dc=dc, tc=tc: v.tensor_tensor(
                    sl512(fst, k % 2), sl512(xst, i), mrep[:, :], ALU.subtract), inc=cV)
                P.op("v", lambda v, k=k, dc=dc, tc=tc: v.tensor_tensor(
                    a1v[:, dc, tc * 512:(tc + 1) * 512], sl512(fst, k % 2), rrep[:, :],
                    ALU.mult), inc=cV)
                slot_cv[i] = P.val(cV)
                if dst_dram is not None:
                    P.wait(qe, cV, slot_cv[i])
                    P.dma(qe, qs, dst_dram[dc * 128:(dc + 1) * 128, tc * 512:(tc + 1) * 512],
                          a1v[:, dc, tc * 512:(tc + 1) * 512])

    wslot = [wst[:, i * 2048:(i + 1) * 2048] for i in range(3)]

    def wload(slot, src_ap):
        return P.dma("g", dW, slot, src_ap)

    def attn_sublayer(x_prev, x_next, is_b):
        wq_i, wk_i, wv_i, wo_i = (
            (din["wqb"], din["wkb"], din["wvb"], din["wob"]) if is_b else
            (din["wqa"], din["wka"], din["wva"], din["woa"]))

        if is_b:
            P.op("g", lambda g: g.memset(alib[:, :], 0.0), inc=ini)
        else:
            P.dma("g", dW, alib[:, :], din["alibi"])

        st_cv, st_ca = stats(x_prev)
        y_produce(x_prev, y_scr, st_cv, st_ca)
        v_ywr = P.val(dY)
        v_ysr = P.val(dS)
        P.wait("g", dY, v_ywr)
        P.wait("g", dS, v_ysr)
        P.op("g", lambda g: g.collective_compute(
            "AllGather", ALU.bypass,
            replica_groups=[[0, 1], [2, 3], [4, 5], [6, 7]],
            ins=[y_scr], outs=[y_gth]), inc=cc)
        v_cc = P.val(cc)

        wv_all = ar2[:, 0:16384].rearrange("p (dc e) -> p dc e", dc=NDC)
        qraw_st = [ar2[:, 16384 + i * 512:16384 + (i + 1) * 512] for i in range(2)]
        csc_sb = ar2[:, 20480:24576]
        css_sb = ar2[:, 24576:28672]
        P.wait("g", cT)
        if is_b:
            P.dma("g", dW, csc_sb, din["csc"])
            P.dma("g", dW, css_sb, din["css"])
        v_wv = P.dma("g", dW, wv_all, wv_i.rearrange("p (dc e) -> p dc e", dc=NDC))
        P.op("g", lambda g: g.memset(cells[:, :], 0.0), inc=ini)
        v_cells0 = P.val(ini)
        P.wait("v", ini, v_cells0)

        yslab = ar1[:, :].rearrange("p (dc t) -> p dc t", dc=NDC)

        def proj_evac_qk(quad, oc, sl, tq, which, i):
            tile = sl512(btl, i % 6)
            col = slice(sl * 2048 + tq * 512, sl * 2048 + (tq + 1) * 512)
            scr = k_scr if which == "k" else q_scr
            bias_sb = bkb if which == "k" else bqb
            if not is_b:
                P.wait("a", cT)
                P.wait("a", dY)
                P.op("a", lambda a, quad=quad, tq=tq, tile=tile, oc=oc, bias_sb=bias_sb: a.add(
                    tile, sl512(quad, tq), bias_sb[:, oc:oc + 1]), inc=cA)
                va = P.val(cA)
                P.wait("y", cA, va)
                P.dma("y", dY, scr[oc * 128:(oc + 1) * 128, col], tile)
            else:
                qraw = qraw_st[i % 2]
                P.wait("a", cT)
                P.wait("a", cV)
                P.op("a", lambda a, quad=quad, tq=tq, qraw=qraw: a.copy(
                    qraw, sl512(quad, tq)), inc=cA)
                va = P.val(cA)
                P.wait("t", cA, va)
                P.op("t", lambda t, quad=quad, tq=tq, qraw=qraw: t.matmul(
                    sl512(quad, tq), pswb[:, :], qraw, start=True, stop=True), inc=cT)
                vt = P.val(cT)
                P.wait("v", cA, va)
                P.wait("v", cA)
                P.wait("v", dY)
                P.wait("v", dS)
                P.op("v", lambda v, qraw=qraw, c=col: v.tensor_tensor(
                    sl512(xst, 0), qraw, csc_sb[:, c], ALU.mult), inc=cV)
                P.wait("v", cT, vt)
                P.op("v", lambda v, quad=quad, tq=tq, c=col: v.tensor_tensor(
                    sl512(xst, 1), sl512(quad, tq), css_sb[:, c], ALU.mult), inc=cV)
                P.wait("v", cV)
                P.op("v", lambda v, tile=tile: v.tensor_tensor(
                    tile, sl512(xst, 0), sl512(xst, 1), ALU.add), inc=cV)
                vv = P.val(cV)
                P.wait("y", cV, vv)
                P.dma("y", dY, scr[oc * 128:(oc + 1) * 128, col], tile)
            cidx = (0 if which == "q" else 8) + oc
            P.wait("v", cA)
            P.wait("v", cV)
            P.op("v", lambda v, i=i, tile=tile: v.tensor_tensor(
                sl512(sqst, i % 2), tile, tile, ALU.mult), inc=cV)
            vsq = P.val(cV)
            P.wait("t", cV, vsq)
            P.op("t", lambda t, quad=quad, tq=tq, i=i: t.matmul(
                quad[0:1, tq * 512:(tq + 1) * 512], ones_c[:, :], sl512(sqst, i % 2),
                start=True, stop=True), inc=cT)
            vnm = P.val(cT)
            P.wait("v", cT, vnm)
            P.op("v", lambda v, quad=quad, tq=tq: v.reduce_max(
                rowt[0:1, 0:1], quad[0:1, tq * 512:(tq + 1) * 512], axis=AX.X), inc=cV)
            P.wait("v", cV)
            P.op("v", lambda v, cidx=cidx: v.tensor_tensor(
                cells[0:1, cidx:cidx + 1], cells[0:1, cidx:cidx + 1], rowt[0:1, 0:1],
                ALU.max), inc=cV)

        wfree = [P.val(cT)] * 3
        ldi = [0]
        bank_a = [P.val(cA), P.val(cA)]
        bank_v = [P.val(cV), P.val(cV)]
        for sl in range(2):
            P.wait("y", cc, v_cc)
            P.wait("y", cT)
            src = y_gth[sl * D:(sl + 1) * D, :].rearrange("(dc p) t -> p dc t", p=128)
            for half in range(2):
                P.dma("y", dY, yslab[:, half * 8:(half + 1) * 8, :],
                      src[:, half * 8:(half + 1) * 8, :])
            vy = P.val(dY)
            P.wait("t", dY, vy)

            for which in ("q", "k"):
                w_in = wq_i if which == "q" else wk_i
                for oc in range(NH):
                    si = ldi[0] % 3
                    ldi[0] += 1
                    s_ = wslot[si]
                    P.wait("g", cT, wfree[si])
                    vw = wload(s_, w_in[oc])
                    bk = oc % 2
                    quad = psA if bk == 0 else psB
                    P.wait("t", dW, vw)
                    P.wait("t", cA, bank_a[bk])
                    P.wait("t", cV, bank_v[bk])
                    for dc in range(NDC):
                        for tq in range(4):
                            P.op("t", lambda t, quad=quad, s_=s_, dc=dc, tq=tq: t.matmul(
                                sl512(quad, tq), s_[:, dc * 128:(dc + 1) * 128],
                                yslab[:, dc, tq * 512:(tq + 1) * 512],
                                start=(dc == 0), stop=(dc == NDC - 1)),
                                inc=(cT if (dc == NDC - 1 and tq == 3) else None))
                    wfree[si] = P.val(cT)
                    for tq in range(4):
                        proj_evac_qk(quad, oc, sl, tq, which, oc * 4 + tq)
                    bank_a[bk] = P.val(cA)
                    bank_v[bk] = P.val(cV)

            P.wait("t", dW, v_wv)
            for tcb in range(16):
                bk = tcb % 2
                quad = psA if bk == 0 else psB
                P.wait("t", cA, bank_a[bk])
                P.wait("t", cV, bank_v[bk])
                last_dc_stop = is_b
                for dc in range(NDC):
                    for ec in range(2):
                        P.op("t", lambda t, quad=quad, dc=dc, ec=ec, tcb=tcb, lds=last_dc_stop: t.matmul(
                            sl512(quad, ec), yslab[:, dc, tcb * 128:(tcb + 1) * 128],
                            wv_all[:, dc, ec * 512:(ec + 1) * 512],
                            start=(dc == 0),
                            stop=(dc == NDC - 1 and lds)),
                            inc=(cT if (dc == NDC - 1 and ec == 1 and last_dc_stop) else None))
                if not is_b:
                    for ec in range(2):
                        P.op("t", lambda t, quad=quad, ec=ec: t.matmul(
                            sl512(quad, ec), ones_rb[:, :],
                            bvb[0:1, ec * 512:(ec + 1) * 512],
                            start=False, stop=True), inc=(cT if ec == 1 else None))
                vv_ = P.val(cT)
                P.wait("a", cT, vv_)
                for ec in range(2):
                    ti = (tcb * 2 + ec) % 6
                    P.wait("a", dY)
                    P.op("a", lambda a, quad=quad, ec=ec, ti=ti: a.copy(
                        sl512(btl, ti), sl512(quad, ec)), inc=cA)
                    va = P.val(cA)
                    P.wait("y", cA, va)
                    P.dma("y", dY,
                          v_scr[(sl * 2048 + tcb * 128):(sl * 2048 + (tcb + 1) * 128),
                                ec * 512:(ec + 1) * 512], sl512(btl, ti))
                bank_a[bk] = P.val(cA)

        P.wait("v", cT)
        P.wait("v", cV)
        P.op("v", lambda v: v.tensor_tensor(
            cells[0:1, 16:24], cells[0:1, 0:8], cells[0:1, 8:16], ALU.add), inc=cV)
        P.wait("v", cV)
        P.op("v", lambda v: v.tensor_scalar(
            cells[0:1, 16:24], cells[0:1, 16:24], SCALE / 2.0, 8.0, ALU.mult, ALU.add), inc=cV)
        v_M = P.val(cV)

        kh = [ar2[:, 0:4096], ar2[:, 4096:8192]]
        qh = [ar2[:, 8192:12288], ar2[:, 12288:16384]]
        vh = [ar2[:, 16384:20480], ar2[:, 20480:24576]]
        wal = [ar2[0:1, 24576:28672], ar2[0:1, 28672:32768]]

        def head_loads(h):
            i = h % 2
            P.wait("a", cT)
            P.dma("a", dS, kh[i], k_scr[h * 128:(h + 1) * 128, :])
            P.dma("a", dS, qh[i], q_scr[h * 128:(h + 1) * 128, :])
            if not is_b:
                P.wait("a", cV)
                P.dma("a", dS, wal[i], din["wal"][h])
            P.wait("y", cT)
            P.wait("y", cV)
            vsrc = v_scr[:, :].rearrange("(kb p) e -> p kb e", p=128)
            for piece in range(4):
                P.dma("y", dY,
                      vh[i].rearrange("p (kb e) -> p kb e", kb=32)[:, piece * 8:(piece + 1) * 8, :],
                      vsrc[:, piece * 8:(piece + 1) * 8, (h * 128):((h + 1) * 128)])
            return P.val(dS), P.val(dY)

        def build_bias_slab(h):
            P.wait("t", cV, v_M)
            P.wait("t", cA)
            P.op("t", lambda t, h=h: t.matmul(
                psB[:, 1024:1025], ones_rf[:, :], cells[0:1, 16 + h:17 + h],
                start=True, stop=True), inc=cT)
            vt = P.val(cT)
            P.wait("a", cT, vt)
            P.wait("a", cV)
            P.op("a", lambda a, h=h: a.copy(mcol[:, h % 2:h % 2 + 1], psB[:, 1024:1025]), inc=cA)
            va = P.val(cA)
            P.wait("v", cA, va)
            P.op("v", lambda v, h=h: v.tensor_scalar(
                biassl[:, (h % 2) * 32:(h % 2) * 32 + 32], alib[:, h * 32:(h + 1) * 32],
                mcol[:, h % 2:h % 2 + 1], None, ALU.subtract), inc=cV)
            return P.val(cV)

        sems = [None] * (NH + 1)
        bs_v = [None] * (NH + 1)
        sems[0] = head_loads(0)
        bs_v[0] = build_bias_slab(0)
        KBc = [0]
        exp_v = {}
        psb_free = [P.val(cV)]
        wrslot = [sgt[:, 0:512], sgt[:, 512:1024]]
        wr_ca = [0, 0]
        wr_cv = [0, 0]
        wr_mmca = [P.val(cA)]
        for h in range(NH):
            if h + 1 < NH:
                sems[h + 1] = head_loads(h + 1)
                bs_v[h + 1] = build_bias_slab(h + 1)
            vS, vY = sems[h]
            P.wait("t", dS, vS)
            P.wait("t", dY, vY)
            i = h % 2
            for qc in range(NTTC):
                nkb = 4 * (qc + 1)
                kg0 = KBc[0]
                if not is_b:
                    P.wait("t", cA, wr_mmca[0])
                    P.op("t", lambda t, i=i, qc=qc: t.matmul(
                        psB[:, 1536:2048], ones_rb[:, :],
                        wal[i][0:1, qc * 512:(qc + 1) * 512],
                        start=True, stop=True), inc=cT)
                    vwm = P.val(cT)
                    P.wait("a", cT, vwm)
                    P.wait("a", cV, wr_cv[qc % 2])
                    P.op("a", lambda a, qc=qc: a.copy(wrslot[qc % 2], psB[:, 1536:2048]), inc=cA)
                    wr_ca[qc % 2] = P.val(cA)
                    wr_mmca[0] = wr_ca[qc % 2]

                def u_scores(kb, kg, i=i, h=h, qc=qc):
                    sl = sl512(psA, kg % 4)
                    P.wait("t", cA, exp_v.get(kg - 4, 0))
                    P.op("t", lambda t, sl=sl, kb=kb, i=i, qc=qc: t.matmul(
                        sl, kh[i][:, kb * 128:(kb + 1) * 128],
                        qh[i][:, qc * 512:(qc + 1) * 512],
                        start=True, stop=True), inc=cT)
                    vsc = P.val(cT)
                    dj = kb - 4 * qc
                    vdve = None
                    if not is_b:
                        P.wait("v", cT, vsc)
                        P.wait("v", cA, wr_ca[qc % 2])
                        P.op("v", lambda v, sl=sl, qc=qc: v.tensor_tensor(
                            sl, sl, wrslot[qc % 2], ALU.add), inc=cV)
                        vdve = P.val(cV)
                    if dj >= 0:
                        P.wait("v", cT, vsc)
                        P.op("v", lambda v, sl=sl, dj=dj: v.tensor_tensor(
                            sl, sl, maskb[:, 384 - 128 * dj:896 - 128 * dj],
                            ALU.add), inc=cV)
                        vdve = P.val(cV)
                    P.wait("a", cT, vsc)
                    if vdve is not None:
                        P.wait("a", cV, vdve)
                    P.wait("a", cV, bs_v[h])
                    P.op("a", lambda a, sl=sl, kg=kg, kb=kb, h=h: a.activation(
                        sl512(ptl, kg % 4), sl,
                        AF.Exp, bias=biassl[:, (h % 2) * 32 + kb:(h % 2) * 32 + kb + 1],
                        scale=SCALE), inc=cA)
                    exp_v[kg] = P.val(cA)

                def u_sums_av(kb, kg, first, i=i):
                    P.wait("t", cA, exp_v[kg])
                    if first:
                        P.wait("t", cV, psb_free[0])
                    P.op("t", lambda t, kg=kg, first=first: t.matmul(
                        psB[0:1, 512:1024], ones_c[:, :], sl512(ptl, kg % 4),
                        start=first, stop=True), inc=None)
                    P.op("t", lambda t, kg=kg, kb=kb, i=i, first=first: t.matmul(
                        psB[:, 0:512], vh[i][:, kb * 128:(kb + 1) * 128],
                        sl512(ptl, kg % 4),
                        start=first, stop=True), inc=cT)

                for kb in range(nkb):
                    u_scores(kb, kg0 + kb)
                    if kb >= 2:
                        u_sums_av(kb - 2, kg0 + kb - 2, kb - 2 == 0)
                for kb in range(max(0, nkb - 2), nkb):
                    u_sums_av(kb, kg0 + kb, kb == 0)
                KBc[0] = kg0 + nkb
                if not is_b:
                    wr_cv[qc % 2] = P.val(cV)

                v_avdone = P.val(cT)
                P.wait("v", cT, v_avdone)
                P.op("v", lambda v: v.reciprocal(rowt[:, :], psB[0:1, 512:1024]), inc=cV)
                vrec = P.val(cV)
                P.wait("t", cV, vrec)
                P.op("t", lambda t: t.matmul(
                    psB[:, 1024:1536], ones_rf[:, :], rowt[0:1, :], start=True, stop=True), inc=cT)
                vrep = P.val(cT)
                P.wait("a", cT, vrep)
                P.wait("a", cV)
                P.op("a", lambda a, qc=qc: a.copy(sl512(fst2, qc % 2), psB[:, 1024:1536]), inc=cA)
                vrepc = P.val(cA)
                ctile = sl512(btl, qc % 6)
                P.wait("v", cA, vrepc)
                P.wait("v", dY)
                P.op("v", lambda v, qc=qc, ctile=ctile: v.tensor_tensor(
                    ctile, psB[:, 0:512], sl512(fst2, qc % 2), ALU.mult), inc=cV)
                vctx = P.val(cV)
                psb_free[0] = vctx
                P.wait("y", cV, vctx)
                P.dma("y", dY, c_scr[h * 128:(h + 1) * 128, qc * 512:(qc + 1) * 512], ctile)

        v_cwr = P.val(dY)
        P.wait("g", dY, v_cwr)
        P.wait("g", dS)
        P.op("g", lambda g: g.collective_compute(
            "AllGather", ALU.bypass,
            replica_groups=[[0, 1], [2, 3], [4, 5], [6, 7]],
            ins=[c_scr], outs=[c_gth]), inc=cc)
        v_cc2 = P.val(cc)

        ctxb = ar1[:, :].rearrange("p (dc t) -> p dc t", dc=NDC)
        P.wait("y", cc, v_cc2)
        P.wait("y", cT)
        for piece in range(4):
            def dma_piece(sy, piece=piece):
                sy.reg_add(rtmp, rreg, piece * 4 * 128 * TT)
                return sy.dma_start(
                    out=ctxb[:, piece * 4:(piece + 1) * 4, :],
                    in_=bass.AP(c_gth.tensor, rtmp, [[TT, 128], [128 * TT, 4], [1, T]]))
            P.op("y", dma_piece, inc=dY, n=16)
        v_ctxld = P.val(dY)
        P.wait("t", dY, v_ctxld)

        P.wait("t", cA)
        bank_v = [P.val(cV), P.val(cV)]
        for oc in range(16):
            si = ldi[0] % 3
            ldi[0] += 1
            s_ = wslot[si]
            P.wait("g", cT, wfree[si])
            vw = wload(s_, wo_i[oc])
            bk = oc % 2
            quad = psA if bk == 0 else psB
            P.wait("t", dW, vw)
            P.wait("t", cV, bank_v[bk])
            for dc in range(NDC):
                for tq in range(4):
                    P.op("t", lambda t, quad=quad, s_=s_, dc=dc, tq=tq: t.matmul(
                        sl512(quad, tq), s_[:, dc * 128:(dc + 1) * 128],
                        ctxb[:, dc, tq * 512:(tq + 1) * 512],
                        start=(dc == 0), stop=(dc == NDC - 1)),
                        inc=(cT if (dc == NDC - 1 and tq == 3) else None))
            vq = P.val(cT)
            wfree[si] = vq
            for tq in range(4):
                i = (oc * 4 + tq) % 2
                P.wait("y", cV)
                vx = P.dma("y", dY, sl512(xst, i),
                           x_prev[oc * 128:(oc + 1) * 128, tq * 512:(tq + 1) * 512])
                P.wait("v", cT, vq)
                P.wait("v", dY, vx)
                if is_b:
                    P.op("v", lambda v, quad=quad, tq=tq, i=i: v.tensor_tensor(
                        sl512(fst, i), sl512(quad, tq), sl512(xst, i), ALU.add), inc=cV)
                else:
                    P.op("v", lambda v, quad=quad, tq=tq, i=i, oc=oc: v.scalar_tensor_tensor(
                        sl512(fst, i), sl512(quad, tq), bob[:, oc:oc + 1],
                        sl512(xst, i), ALU.add, ALU.add), inc=cV)
                vv = P.val(cV)
                P.wait("y", cV, vv)
                P.dma("y", dY, x_next[oc * 128:(oc + 1) * 128, tq * 512:(tq + 1) * 512],
                      sl512(fst, i))
            bank_v[bk] = P.val(cV)

    def ffn_sublayer(x_prev, x_next):
        st_cv, st_ca = stats(x_prev)
        y_produce(x_prev, None, st_cv, st_ca)
        y2 = ar1[:, :].rearrange("p (dc t) -> p dc t", dc=NDC)
        g_sl = ar2[:, :].rearrange("p (hc t) -> p hc t", hc=16)
        v_y2 = P.val(cV)
        P.wait("t", cV, v_y2)

        ff_wfree = [P.val(cT)] * 3
        fli = [0]
        acc_dw = [P.val(dW)] * 4
        for slab in range(4):
            for hc in range(16):
                habs = slab * 16 + hc
                gi = fli[0] % 3
                fli[0] += 1
                ui = fli[0] % 3
                fli[0] += 1
                sg_ = wslot[gi]
                su_ = wslot[ui]
                P.wait("g", cT, ff_wfree[gi])
                vwg = wload(sg_, din["wg"][habs])
                P.wait("g", cT, ff_wfree[ui])
                vwu = wload(su_, din["wu"][habs])
                P.wait("t", dW, vwg)
                P.wait("t", cA)
                for dc in range(NDC):
                    for tq in range(4):
                        P.op("t", lambda t, s_=sg_, dc=dc, tq=tq: t.matmul(
                            sl512(psA, tq), s_[:, dc * 128:(dc + 1) * 128],
                            y2[:, dc, tq * 512:(tq + 1) * 512],
                            start=(dc == 0), stop=(dc == NDC - 1)),
                            inc=(cT if (dc == NDC - 1 and tq == 3) else None))
                vg = P.val(cT)
                ff_wfree[gi] = vg
                P.wait("t", dW, vwu)
                P.wait("t", cV)
                for dc in range(NDC):
                    for tq in range(4):
                        P.op("t", lambda t, s_=su_, dc=dc, tq=tq: t.matmul(
                            sl512(psB, tq), s_[:, dc * 128:(dc + 1) * 128],
                            y2[:, dc, tq * 512:(tq + 1) * 512],
                            start=(dc == 0), stop=(dc == NDC - 1)),
                            inc=(cT if (dc == NDC - 1 and tq == 3) else None))
                vu = P.val(cT)
                ff_wfree[ui] = vu
                P.wait("a", cT, vg)
                P.wait("a", cV)
                P.op("a", lambda a: a.activation(sgt[:, :], psA[:, :], AF.Silu), inc=cA)
                vsilu = P.val(cA)
                P.wait("v", cA, vsilu)
                P.wait("v", cT, vu)
                P.op("v", lambda v, hc=hc: v.tensor_tensor(
                    g_sl[:, hc, :], sgt[:, :], psB[:, :], ALU.mult), inc=cV)
            v_g = P.val(cV)

            dn_a = [P.val(cA), P.val(cA)]
            dn_v = [P.val(cV), P.val(cV)]
            dn_w = []

            def dn_load(oc2, slab=slab, dn_w=dn_w):
                si2 = fli[0] % 3
                fli[0] += 1
                s2 = wslot[si2]
                P.wait("g", cT, ff_wfree[si2])
                vw2 = wload(s2, din["wd"][oc2][:, slab * 2048:(slab + 1) * 2048])
                dn_w.append((si2, s2, vw2))

            for oc in range(16):
                dn_load(oc)
                si, s_, vw = dn_w[oc]
                bk = oc % 2
                quad = psA if bk == 0 else psB
                P.wait("t", dW, vw)
                P.wait("t", cV, v_g)
                P.wait("t", cA)
                P.wait("t", cV)
                for hc in range(16):
                    for tq in range(4):
                        P.op("t", lambda t, quad=quad, s_=s_, hc=hc, tq=tq: t.matmul(
                            sl512(quad, tq), s_[:, hc * 128:(hc + 1) * 128],
                            g_sl[:, hc, tq * 512:(tq + 1) * 512],
                            start=(hc == 0), stop=(hc == 15)),
                            inc=(cT if (hc == 15 and tq == 3) else None))
                vdn = P.val(cT)
                ff_wfree[si] = vdn
                for tq in range(4):
                    i = (oc * 4 + tq) % 2
                    dst = x_next[oc * 128:(oc + 1) * 128, tq * 512:(tq + 1) * 512]
                    if slab == 0:
                        P.wait("y", cV)
                        vx = P.dma("y", dY, sl512(fst2, i),
                                   x_prev[oc * 128:(oc + 1) * 128, tq * 512:(tq + 1) * 512])
                        P.wait("v", cT, vdn)
                        P.wait("v", dY, vx)
                        P.op("v", lambda v, quad=quad, tq=tq, i=i: v.tensor_tensor(
                            sl512(fst, i), sl512(quad, tq), sl512(fst2, i), ALU.add), inc=cV)
                        vv = P.val(cV)
                        P.wait("y", cV, vv)
                        P.dma("y", dY, dst, sl512(fst, i))
                    else:
                        sl4 = (oc * 4 + tq) % 4
                        ftile = sl512(fst, sl4) if sl4 < 2 else sl512(fst2, sl4 - 2)
                        P.wait("a", cT, vdn)
                        P.wait("a", dW, acc_dw[sl4])
                        P.wait("a", cV)
                        P.op("a", lambda a, quad=quad, tq=tq, ftile=ftile: a.copy(
                            ftile, sl512(quad, tq)), inc=cA)
                        va = P.val(cA)
                        P.wait("g", cA, va)
                        P.wait("g", dY)
                        P.dma("g", dW, dst, ftile, accum=True)
                        acc_dw[sl4] = P.val(dW)
                dn_a[bk] = P.val(cA)
                dn_v[bk] = P.val(cV)

    seq = [
        ("attn", x0, x_scr[0], False),
        ("ffn", x_scr[0], x_scr[1], None),
        ("attn", x_scr[1], x_scr[2], True),
        ("ffn", x_scr[2], out_t, None),
    ][:N_SUB]
    if N_SUB < 4:
        kind, xp, _, fl = seq[-1]
        seq[-1] = (kind, xp, out_t, fl)

    for kind, xp, xn, fl in seq:
        for e in ("y", "a", "g"):
            barrier_dma(e)
        if kind == "attn":
            attn_sublayer(xp, xn, fl)
        else:
            ffn_sublayer(xp, xn)

    for s in (dW, dY, dS):
        P.wait("y", s)
    P.wait("y", cc)

    if os.environ.get("MK_DEBUG") == "1":
        dbg_specs = [
            ("dbg_y", y_scr, [D, T], BF16), ("dbg_ygth", y_gth, [2 * D, T], BF16),
            ("dbg_q", q_scr, [NH * 128, TT], BF16), ("dbg_k", k_scr, [NH * 128, TT], BF16),
            ("dbg_v", v_scr, [TT, NH * 128], BF16), ("dbg_c", c_scr, [NH * 128, TT], BF16),
            ("dbg_cgth", c_gth, [D, TT], BF16), ("dbg_x1", x_scr[0], [D, T], F32),
            ("dbg_x2", x_scr[1], [D, T], F32), ("dbg_x3", x_scr[2], [D, T], F32),
        ]
        for nm, src, shp, dt in dbg_specs:
            dap = nc.dram_tensor(nm, shp, dt, kind="ExternalOutput").ap()
            P.dma("y", dY, dap, src)
        ar1d = nc.dram_tensor("dbg_ar1", [128, 32768], BF16, kind="ExternalOutput").ap()
        P.dma("y", dY, ar1d, ar1[:, :])
        P.wait("y", dY)

    with nc.Block() as block:
        @block.tensor
        def _(t):
            for f in P.q["t"]:
                f(t)

        @block.scalar
        def _(a):
            for f in P.q["a"]:
                f(a)

        @block.vector
        def _(v):
            for f in P.q["v"]:
                f(v)

        @block.gpsimd
        def _(g):
            for f in P.q["g"]:
                f(g)

        @block.sync
        def _(sy):
            for f in P.q["y"]:
                f(sy)

    es.close()
    return nc


_NC_CACHE = {}


def _blocks_img(w):
    O = w.shape[1]
    nb = O // 128
    out = np.empty((nb, 128, NDC * 128), np.float32)
    for oc in range(nb):
        blk = w[:, oc * 128:(oc + 1) * 128].reshape(NDC, 128, 128)
        out[oc] = blk.transpose(1, 0, 2).reshape(128, NDC * 128)
    return np.ascontiguousarray(out)


def _img(w):
    K, N = w.shape
    return np.ascontiguousarray(
        w.reshape(K // 128, 128, N).transpose(1, 0, 2).reshape(128, (K // 128) * N))


def _prep_inputs(inputs):
    f = lambda k: np.asarray(inputs[k], np.float32)
    x = f("L0_attn_a_x")
    wqa, wka, wva, woa = f("L0_attn_a_Wq"), f("L0_attn_a_Wk"), f("L0_attn_a_Wv"), f("L0_attn_a_Wo")
    bqa, bka, bva, boa = f("L0_attn_a_bq"), f("L0_attn_a_bk"), f("L0_attn_a_bv"), f("L0_attn_a_bo")
    wqb, wkb, wvb, wob = f("L0_attn_b_Wq"), f("L0_attn_b_Wk"), f("L0_attn_b_Wv"), f("L0_attn_b_Wo")
    wg, wu, wd = f("L0_ffn_w_gate"), f("L0_ffn_w_up"), f("L0_ffn_w_down")

    pos = np.arange(TT, dtype=np.float64)
    invf = 1.0 / (10000.0 ** (np.arange(64, dtype=np.float64) / 64.0))
    ang = pos[None, :] * invf[:, None]
    cosv = np.cos(ang); sinv = np.sin(ang)
    csc = np.ascontiguousarray(np.concatenate([cosv, cosv], 0).astype(ml_dtypes.bfloat16))
    css = np.ascontiguousarray(np.concatenate([-sinv, sinv], 0).astype(ml_dtypes.bfloat16))
    pswap = np.zeros((128, 128), np.float32)
    for jj in range(128):
        pswap[(jj + 64) % 128, jj] = 1.0
    pswap = np.ascontiguousarray(pswap.astype(ml_dtypes.bfloat16))
    ku = np.arange(128)[:, None]
    uu = np.arange(896)[None, :]
    maskc = np.ascontiguousarray(np.where(uu >= ku + 384, 0.0, -1e30).astype(ml_dtypes.bfloat16))

    wg_b = _blocks_img(np.ascontiguousarray(wg.T))
    wu_b = _blocks_img(np.ascontiguousarray(wu.T))
    wdT = np.ascontiguousarray(wd.T)
    wd_b = np.stack([_img(np.ascontiguousarray(wdT[:, oc * 128:(oc + 1) * 128]))
                     for oc in range(16)])
    woa_b = _blocks_img(woa)
    wob_b = _blocks_img(np.ascontiguousarray(wob.T))
    wqbT = np.ascontiguousarray(wqb.T); wkbT = np.ascontiguousarray(wkb.T)
    wvbT = np.ascontiguousarray(wvb.T)

    in_maps = []
    for c in range(8):
        b, p = c // 2, c % 2
        tok = slice(p * T, (p + 1) * T)
        hh = slice(p * 1024, (p + 1) * 1024)
        habs = np.arange(p * 8, p * 8 + 8)
        slopes = (2.0 ** (-8.0 * (habs + 1) / 16.0))
        ali = np.empty((128, NH * 32), np.float32)
        kb = np.arange(32)[None, :]
        ii = np.arange(128)[:, None]
        for j in range(NH):
            ali[:, j * 32:(j + 1) * 32] = slopes[j] * (kb * 128 + ii - (TT - 1))
        qq = np.arange(TT, dtype=np.float64)
        wal_t = np.ascontiguousarray(
            (slopes[:, None] * ((TT - 1) - qq[None, :]) / SCALE).astype(ml_dtypes.bfloat16)
        ).reshape(NH, 1, TT)
        m = {
            "x0": np.ascontiguousarray(x[b].T[:, tok]),
            "wqa": _blocks_img(np.ascontiguousarray(wqa[:, hh])),
            "wka": _blocks_img(np.ascontiguousarray(wka[:, hh])),
            "wva": _img(np.ascontiguousarray(wva[:, hh])),
            "woa": woa_b,
            "bqa": np.ascontiguousarray(bqa[hh].reshape(8, 128).T),
            "bka": np.ascontiguousarray(bka[hh].reshape(8, 128).T),
            "bva": np.ascontiguousarray(bva[hh].reshape(1, 1024)),
            "boa": np.ascontiguousarray(boa.reshape(16, 128).T),
            "wqb": _blocks_img(np.ascontiguousarray(wqbT[:, hh])),
            "wkb": _blocks_img(np.ascontiguousarray(wkbT[:, hh])),
            "wvb": _img(np.ascontiguousarray(wvbT[:, hh])),
            "wob": wob_b,
            "wg": wg_b, "wu": wu_b, "wd": wd_b,
            "alibi": np.ascontiguousarray(ali),
            "csc": csc, "css": css, "pswap": pswap, "maskc": maskc,
            "colsel": np.array([[p * T]], np.int32),
            "wal": wal_t,
        }
        in_maps.append(m)
    return in_maps


def kernel(**inputs):
    in_maps = _prep_inputs(inputs)
    if "nc" not in _NC_CACHE:
        _NC_CACHE["nc"] = build_nc()
    nc = _NC_CACHE["nc"]
    res = run_bass_kernel_spmd(nc, in_maps, core_ids=list(range(8)))
    out = np.empty((4, TT, D), np.float32)
    for c in range(8):
        b, p = c // 2, c % 2
        out[b, p * T:(p + 1) * T, :] = np.asarray(res.results[c]["out"]).T
    return out


# version 13
# speedup vs baseline: 1.3277x; 1.0864x over previous
import math
import os
from contextlib import ExitStack

import numpy as np
import ml_dtypes

import concourse.bass as bass
import concourse.mybir as mybir
from concourse.bass_utils import run_bass_kernel_spmd

F32 = mybir.dt.float32
BF16 = mybir.dt.bfloat16
I32 = mybir.dt.int32
AF = mybir.ActivationFunctionType
ALU = mybir.AluOpType
AX = mybir.AxisListType

D = 2048
T = 2048
TT = 4096
NH = 8
HD = 128
NDC = D // 128
NTC = T // 512
NTTC = TT // 512
SCALE = 1.0 / math.sqrt(HD)
EPS = 1e-5
N_SUB = int(os.environ.get("MK_NSUB", "4"))

ENGINES = ("t", "a", "v", "g", "y")


class Plan:
    def __init__(self):
        self.q = {e: [] for e in ENGINES}
        self.semv = {}
        self.seen = {e: {} for e in ENGINES}

    def val(self, sem):
        return self.semv.get(sem.name, 0)

    def inc(self, sem, n):
        v = self.val(sem) + n
        self.semv[sem.name] = v
        return v

    def wait(self, e, sem, v=None):
        if v is None:
            v = self.val(sem)
        if v <= 0 or self.seen[e].get(sem.name, -1) >= v:
            return
        self.seen[e][sem.name] = v
        self.q[e].append(lambda eng, s=sem, vv=v: eng.wait_ge(s, vv))

    def op(self, e, fn, inc=None, n=1):
        if inc is not None:
            self.inc(inc, n)
        self.q[e].append(
            lambda eng, f=fn, s=inc, nn=n: (f(eng).then_inc(s, nn) if s is not None else f(eng))
        )

    def dma(self, e, sem, out, in_, accum=False):
        if accum:
            self.op(e, lambda eng, o=out, i=in_: eng.dma_start(out=o, in_=i, accum_op=ALU.add),
                    inc=sem, n=16)
        else:
            self.op(e, lambda eng, o=out, i=in_: eng.dma_start(out=o, in_=i), inc=sem, n=16)
        return self.val(sem)


def build_nc():
    nc = bass.Bass("TRN2", target_bir_lowering=False)

    din = {}
    def inp(name, shape, dt=F32):
        din[name] = nc.dram_tensor(name, list(shape), dt, kind="ExternalInput").ap()
        return din[name]

    x0 = inp("x0", (D, T))
    inp("wqa", (NH, 128, NDC * 128)); inp("wka", (NH, 128, NDC * 128))
    inp("wva", (128, NDC * NH * 128))
    inp("woa", (16, 128, NDC * 128))
    inp("bqa", (128, NH)); inp("bka", (128, NH))
    inp("bva", (1, NH * 128)); inp("boa", (128, 16))
    inp("wqb", (NH, 128, NDC * 128)); inp("wkb", (NH, 128, NDC * 128))
    inp("wvb", (128, NDC * NH * 128))
    inp("wob", (16, 128, NDC * 128))
    inp("wg", (64, 128, NDC * 128)); inp("wu", (64, 128, NDC * 128))
    inp("wd", (16, 128, 64 * 128))
    inp("alibi", (128, NH * 32))
    inp("csc", (128, TT), BF16); inp("css", (128, TT), BF16)
    inp("pswap", (128, 128), BF16)
    inp("maskc", (128, 896), BF16)
    inp("colsel", (1, 1), I32)
    inp("wal", (NH, 1, TT), BF16)

    out_t = nc.dram_tensor("out", [D, T], F32, kind="ExternalOutput").ap()

    y_scr = nc.dram_tensor("y_scr", [D, T], BF16).ap()
    y_gth = nc.dram_tensor("y_gth", [2 * D, T], BF16).ap()
    q_scr = nc.dram_tensor("q_scr", [NH * 128, TT], BF16).ap()
    k_scr = nc.dram_tensor("k_scr", [NH * 128, TT], BF16).ap()
    v_scr = nc.dram_tensor("v_scr", [TT, NH * 128], BF16).ap()
    c_scr = nc.dram_tensor("c_scr", [NH * 128, TT], BF16).ap()
    c_gth = nc.dram_tensor("c_gth", [D, TT], BF16).ap()
    x_scr = [nc.dram_tensor(f"x{i}_scr", [D, T], F32).ap() for i in (1, 2, 3)]

    es = ExitStack()
    sb = lambda name, shape, dt: es.enter_context(nc.sbuf_tensor(name, list(shape), dt))

    ar1 = sb("ar1", (128, 32768), BF16)
    ar2 = sb("ar2", (128, 32768), BF16)
    wst = sb("wst", (128, 6144), BF16)
    ones_c = sb("ones_c", (128, 1), BF16)
    ones_rb = sb("ones_rb", (1, 128), BF16)
    ones_rf = sb("ones_rf", (1, 128), F32)
    m_row = sb("m_row", (1, T), F32)
    q_row = sb("q_row", (1, T), F32)
    r_row = sb("r_row", (1, T), F32)
    _ = r_row
    mrep = sb("mrep", (128, 512), F32)
    rrep = sb("rrep", (128, 512), F32)
    xst = sb("xst", (128, 2048), F32)
    sqst = sb("sqst", (128, 1024), BF16)
    fst = sb("fst", (128, 1024), F32)
    fst2 = sb("fst2", (128, 1024), F32)
    btl = sb("btl", (128, 6 * 512), BF16)
    ptl = sb("ptl", (128, 2048), BF16)
    sgt = sb("sgt", (128, 2048), BF16)
    biassl = sb("biassl", (128, 64), F32)
    mcol = sb("mcol", (128, 2), F32)
    alib = sb("alib", (128, NH * 32), F32)
    maskb = sb("maskb", (128, 896), BF16)
    pswb = sb("pswb", (128, 128), BF16)
    bqb = sb("bqb", (128, NH), F32)
    bkb = sb("bkb", (128, NH), F32)
    bvb = sb("bvb", (1, NH * 128), BF16)
    bob = sb("bob", (128, 16), F32)
    colsb = sb("colsb", (1, 1), I32)
    cells = sb("cells", (1, 64), F32)
    rowt = r_row[0:1, 1536:2048]
    psA = es.enter_context(nc.psum_tensor("psA", [128, 2048], F32))
    psB = es.enter_context(nc.psum_tensor("psB", [128, 2048], F32))

    dW = es.enter_context(nc.semaphore("dW"))
    dY = es.enter_context(nc.semaphore("dY"))
    dS = es.enter_context(nc.semaphore("dS"))
    cT = es.enter_context(nc.semaphore("cT"))
    cA = es.enter_context(nc.semaphore("cA"))
    cV = es.enter_context(nc.semaphore("cV"))
    cc = es.enter_context(nc.semaphore("ccs"))
    ini = es.enter_context(nc.semaphore("ini"))

    P = Plan()

    rreg = nc.alloc_register(mybir.EngineType.SP, "rctx")
    rtmp = nc.alloc_register(mybir.EngineType.SP, "rtmp")

    def sl512(t_, i):
        return t_[:, i * 512:(i + 1) * 512]

    P.op("g", lambda g: g.memset(ones_c[:, :], 1.0), inc=ini)
    P.op("g", lambda g: g.memset(ones_rb[:, :], 1.0), inc=ini)
    P.op("g", lambda g: g.memset(ones_rf[:, :], 1.0), inc=ini)
    for e_ in ("t", "a", "v", "y"):
        P.wait(e_, ini)
    P.dma("g", dW, maskb[:, :], din["maskc"])
    P.dma("g", dW, pswb[:, :], din["pswap"])
    P.dma("g", dW, colsb[:, :], din["colsel"])
    P.dma("g", dW, bqb[:, :], din["bqa"])
    P.dma("g", dW, bkb[:, :], din["bka"])
    P.dma("g", dW, bvb[:, :], din["bva"])
    P.dma("g", dW, bob[:, :], din["boa"])
    v_colsel = P.val(dW)
    P.wait("y", dW, v_colsel)
    P.q["y"].append(lambda sy: sy.reg_load(rreg, colsb[0:1, 0:1]))

    def barrier_dma(e):
        for s in (dW, dY, dS):
            P.wait(e, s)
        P.wait(e, cc)

    a1v = ar1[:, :].rearrange("p (dc t) -> p dc t", dc=NDC)

    def stats(x_ap):
        sum_r = psA[0:1, 0:512]
        sq_r = psA[0:1, 512:1024]
        P.wait("v", cT)
        P.wait("y", dW)
        P.wait("y", dS)
        P.wait("a", dW)
        P.wait("a", dY)
        slot_cv = [0] * 4
        slot_ca = [0] * 4
        sq_ct = [0, 0]
        row_cv = 0
        for tc in range(NTC):
            if tc > 0:
                P.wait("t", cV, row_cv)
            for dc in range(NDC):
                k = tc * NDC + dc
                i = k % 4
                qe, qs = ("y", dY) if k % 2 == 0 else ("a", dS)
                P.wait(qe, cV, slot_cv[i])
                P.wait(qe, cA, slot_ca[i])
                vx = P.dma(qe, qs, sl512(xst, i),
                           x_ap[dc * 128:(dc + 1) * 128, tc * 512:(tc + 1) * 512])
                P.wait("v", qs, vx)
                P.op("v", lambda v, i=i, dc=dc, tc=tc: v.tensor_copy(
                    a1v[:, dc, tc * 512:(tc + 1) * 512], sl512(xst, i)), inc=cV)
                slot_cv[i] = P.val(cV)
                P.wait("a", qs, vx)
                P.wait("a", cT, sq_ct[k % 2])
                P.op("a", lambda a, i=i, k=k: a.activation(
                    sl512(sqst, k % 2), sl512(xst, i), AF.Square), inc=cA)
                slot_ca[i] = P.val(cA)
                P.wait("t", cV, slot_cv[i])
                P.wait("t", cA, slot_ca[i])
                P.op("t", lambda t, dc=dc, tc=tc: t.matmul(
                    sum_r, ones_c[:, :], a1v[:, dc, tc * 512:(tc + 1) * 512],
                    start=(dc == 0), stop=(dc == NDC - 1)), inc=cT)
                P.op("t", lambda t, k=k, dc=dc: t.matmul(
                    sq_r, ones_c[:, :], sl512(sqst, k % 2),
                    start=(dc == 0), stop=(dc == NDC - 1)), inc=cT)
                sq_ct[k % 2] = P.val(cT)
            P.wait("v", cT)
            P.op("v", lambda v, tc=tc: v.tensor_scalar(
                m_row[0:1, tc * 512:(tc + 1) * 512], sum_r, 1.0 / D, None, ALU.mult), inc=cV)
            P.op("v", lambda v, tc=tc: v.tensor_scalar(
                q_row[0:1, tc * 512:(tc + 1) * 512], sq_r, 1.0 / D, None, ALU.mult), inc=cV)
            row_cv = P.val(cV)
        P.op("v", lambda v: v.tensor_tensor(r_row[:, :], m_row[:, :], m_row[:, :], ALU.mult), inc=cV)
        P.wait("v", cV)
        P.op("v", lambda v: v.tensor_tensor(q_row[:, :], q_row[:, :], r_row[:, :], ALU.subtract), inc=cV)
        P.wait("v", cV)
        P.op("v", lambda v: v.tensor_scalar(q_row[:, :], q_row[:, :], float(EPS), None, ALU.add), inc=cV)
        P.wait("a", cV)
        P.op("a", lambda a: a.activation(q_row[:, :], q_row[:, :], AF.Ln), inc=cA)
        P.wait("a", cA)
        P.op("a", lambda a: a.activation(r_row[:, :], q_row[:, :], AF.Exp, scale=-0.5), inc=cA)
        return slot_cv, slot_ca

    def replicate_tc(tc):
        P.wait("t", cA)
        P.wait("t", cV)
        P.op("t", lambda t, tc=tc: t.matmul(
            psB[:, 0:512], ones_rf[:, :], m_row[0:1, tc * 512:(tc + 1) * 512],
            start=True, stop=True), inc=cT)
        P.op("t", lambda t, tc=tc: t.matmul(
            psB[:, 512:1024], ones_rf[:, :], r_row[0:1, tc * 512:(tc + 1) * 512],
            start=True, stop=True), inc=cT)
        P.wait("a", cT)
        P.wait("a", cV)
        P.op("a", lambda a: a.copy(mrep[:, :], psB[:, 0:512]), inc=cA)
        P.op("a", lambda a: a.copy(rrep[:, :], psB[:, 512:1024]), inc=cA)
        return P.val(cA)

    def y_produce(x_ap, dst_dram, st_cv, st_ca):
        slot_cv = list(st_cv)
        slot_ca = list(st_ca)
        for tc in range(NTC):
            v_rep = replicate_tc(tc)
            for dc in range(NDC):
                k = tc * NDC + dc
                i = k % 4
                qe, qs = ("y", dY) if k % 2 == 0 else ("a", dS)
                P.wait(qe, cV, slot_cv[i])
                P.wait(qe, cA, slot_ca[i])
                slot_ca[i] = 0
                vx = P.dma(qe, qs, sl512(xst, i),
                           x_ap[dc * 128:(dc + 1) * 128, tc * 512:(tc + 1) * 512])
                P.wait("v", qs, vx)
                P.wait("v", cA, v_rep)
                P.op("v", lambda v, i=i, k=k, dc=dc, tc=tc: v.tensor_tensor(
                    sl512(fst, k % 2), sl512(xst, i), mrep[:, :], ALU.subtract), inc=cV)
                P.op("v", lambda v, k=k, dc=dc, tc=tc: v.tensor_tensor(
                    a1v[:, dc, tc * 512:(tc + 1) * 512], sl512(fst, k % 2), rrep[:, :],
                    ALU.mult), inc=cV)
                slot_cv[i] = P.val(cV)
                if dst_dram is not None:
                    P.wait(qe, cV, slot_cv[i])
                    P.dma(qe, qs, dst_dram[dc * 128:(dc + 1) * 128, tc * 512:(tc + 1) * 512],
                          a1v[:, dc, tc * 512:(tc + 1) * 512])

    wslot = [wst[:, i * 2048:(i + 1) * 2048] for i in range(3)]

    def wload(slot, src_ap):
        return P.dma("g", dW, slot, src_ap)

    def attn_sublayer(x_prev, x_next, is_b):
        wq_i, wk_i, wv_i, wo_i = (
            (din["wqb"], din["wkb"], din["wvb"], din["wob"]) if is_b else
            (din["wqa"], din["wka"], din["wva"], din["woa"]))

        if is_b:
            P.op("g", lambda g: g.memset(alib[:, :], 0.0), inc=ini)
        else:
            P.dma("g", dW, alib[:, :], din["alibi"])

        st_cv, st_ca = stats(x_prev)
        y_produce(x_prev, y_scr, st_cv, st_ca)
        v_ywr = P.val(dY)
        v_ysr = P.val(dS)
        P.wait("g", dY, v_ywr)
        P.wait("g", dS, v_ysr)
        P.op("g", lambda g: g.collective_compute(
            "AllGather", ALU.bypass,
            replica_groups=[[0, 1], [2, 3], [4, 5], [6, 7]],
            ins=[y_scr], outs=[y_gth]), inc=cc)
        v_cc = P.val(cc)

        wv_all = ar2[:, 0:16384].rearrange("p (dc e) -> p dc e", dc=NDC)
        qraw_st = [ar2[:, 16384 + i * 512:16384 + (i + 1) * 512] for i in range(2)]
        csc_sb = ar2[:, 20480:24576]
        css_sb = ar2[:, 24576:28672]
        P.wait("g", cT)
        if is_b:
            P.dma("g", dW, csc_sb, din["csc"])
            P.dma("g", dW, css_sb, din["css"])
        v_wv = P.dma("g", dW, wv_all, wv_i.rearrange("p (dc e) -> p dc e", dc=NDC))
        P.op("g", lambda g: g.memset(cells[:, :], 0.0), inc=ini)
        v_cells0 = P.val(ini)
        P.wait("v", ini, v_cells0)

        yslab = ar1[:, :].rearrange("p (dc t) -> p dc t", dc=NDC)

        def proj_evac_qk(quad, oc, sl, tq, which, i):
            tile = sl512(btl, i % 6)
            col = slice(sl * 2048 + tq * 512, sl * 2048 + (tq + 1) * 512)
            scr = k_scr if which == "k" else q_scr
            bias_sb = bkb if which == "k" else bqb
            if not is_b:
                P.wait("a", cT)
                P.wait("a", dY)
                P.op("a", lambda a, quad=quad, tq=tq, tile=tile, oc=oc, bias_sb=bias_sb: a.add(
                    tile, sl512(quad, tq), bias_sb[:, oc:oc + 1]), inc=cA)
                va = P.val(cA)
                P.wait("y", cA, va)
                P.dma("y", dY, scr[oc * 128:(oc + 1) * 128, col], tile)
            else:
                qraw = qraw_st[i % 2]
                P.wait("a", cT)
                P.wait("a", cV)
                P.op("a", lambda a, quad=quad, tq=tq, qraw=qraw: a.copy(
                    qraw, sl512(quad, tq)), inc=cA)
                va = P.val(cA)
                P.wait("t", cA, va)
                P.op("t", lambda t, quad=quad, tq=tq, qraw=qraw: t.matmul(
                    sl512(quad, tq), pswb[:, :], qraw, start=True, stop=True), inc=cT)
                vt = P.val(cT)
                P.wait("v", cA, va)
                P.wait("v", cA)
                P.wait("v", dY)
                P.wait("v", dS)
                P.op("v", lambda v, qraw=qraw, c=col: v.tensor_tensor(
                    sl512(xst, 0), qraw, csc_sb[:, c], ALU.mult), inc=cV)
                P.wait("v", cT, vt)
                P.op("v", lambda v, quad=quad, tq=tq, c=col: v.tensor_tensor(
                    sl512(xst, 1), sl512(quad, tq), css_sb[:, c], ALU.mult), inc=cV)
                P.wait("v", cV)
                P.op("v", lambda v, tile=tile: v.tensor_tensor(
                    tile, sl512(xst, 0), sl512(xst, 1), ALU.add), inc=cV)
                vv = P.val(cV)
                P.wait("y", cV, vv)
                P.dma("y", dY, scr[oc * 128:(oc + 1) * 128, col], tile)
            cidx = (0 if which == "q" else 8) + oc
            P.wait("v", cA)
            P.wait("v", cV)
            P.op("v", lambda v, i=i, tile=tile: v.tensor_tensor(
                sl512(sqst, i % 2), tile, tile, ALU.mult), inc=cV)
            vsq = P.val(cV)
            P.wait("t", cV, vsq)
            P.op("t", lambda t, quad=quad, tq=tq, i=i: t.matmul(
                quad[0:1, tq * 512:(tq + 1) * 512], ones_c[:, :], sl512(sqst, i % 2),
                start=True, stop=True), inc=cT)
            vnm = P.val(cT)
            P.wait("v", cT, vnm)
            P.op("v", lambda v, quad=quad, tq=tq: v.reduce_max(
                rowt[0:1, 0:1], quad[0:1, tq * 512:(tq + 1) * 512], axis=AX.X), inc=cV)
            P.wait("v", cV)
            P.op("v", lambda v, cidx=cidx: v.tensor_tensor(
                cells[0:1, cidx:cidx + 1], cells[0:1, cidx:cidx + 1], rowt[0:1, 0:1],
                ALU.max), inc=cV)

        wfree = [P.val(cT)] * 3
        ldi = [0]
        bank_a = [P.val(cA), P.val(cA)]
        bank_v = [P.val(cV), P.val(cV)]
        for sl in range(2):
            P.wait("y", cc, v_cc)
            P.wait("y", cT)
            src = y_gth[sl * D:(sl + 1) * D, :].rearrange("(dc p) t -> p dc t", p=128)
            for half in range(2):
                P.dma("y", dY, yslab[:, half * 8:(half + 1) * 8, :],
                      src[:, half * 8:(half + 1) * 8, :])
            vy = P.val(dY)
            P.wait("t", dY, vy)

            for which in ("q", "k"):
                w_in = wq_i if which == "q" else wk_i
                for oc in range(NH):
                    si = ldi[0] % 3
                    ldi[0] += 1
                    s_ = wslot[si]
                    P.wait("g", cT, wfree[si])
                    vw = wload(s_, w_in[oc])
                    bk = oc % 2
                    quad = psA if bk == 0 else psB
                    P.wait("t", dW, vw)
                    P.wait("t", cA, bank_a[bk])
                    P.wait("t", cV, bank_v[bk])
                    for dc in range(NDC):
                        for tq in range(4):
                            P.op("t", lambda t, quad=quad, s_=s_, dc=dc, tq=tq: t.matmul(
                                sl512(quad, tq), s_[:, dc * 128:(dc + 1) * 128],
                                yslab[:, dc, tq * 512:(tq + 1) * 512],
                                start=(dc == 0), stop=(dc == NDC - 1)),
                                inc=(cT if (dc == NDC - 1 and tq == 3) else None))
                    wfree[si] = P.val(cT)
                    for tq in range(4):
                        proj_evac_qk(quad, oc, sl, tq, which, oc * 4 + tq)
                    bank_a[bk] = P.val(cA)
                    bank_v[bk] = P.val(cV)

            P.wait("t", dW, v_wv)
            for tcb in range(16):
                bk = tcb % 2
                quad = psA if bk == 0 else psB
                P.wait("t", cA, bank_a[bk])
                P.wait("t", cV, bank_v[bk])
                last_dc_stop = is_b
                for dc in range(NDC):
                    for ec in range(2):
                        P.op("t", lambda t, quad=quad, dc=dc, ec=ec, tcb=tcb, lds=last_dc_stop: t.matmul(
                            sl512(quad, ec), yslab[:, dc, tcb * 128:(tcb + 1) * 128],
                            wv_all[:, dc, ec * 512:(ec + 1) * 512],
                            start=(dc == 0),
                            stop=(dc == NDC - 1 and lds)),
                            inc=(cT if (dc == NDC - 1 and ec == 1 and last_dc_stop) else None))
                if not is_b:
                    for ec in range(2):
                        P.op("t", lambda t, quad=quad, ec=ec: t.matmul(
                            sl512(quad, ec), ones_rb[:, :],
                            bvb[0:1, ec * 512:(ec + 1) * 512],
                            start=False, stop=True), inc=(cT if ec == 1 else None))
                vv_ = P.val(cT)
                P.wait("a", cT, vv_)
                for ec in range(2):
                    ti = (tcb * 2 + ec) % 6
                    P.wait("a", dY)
                    P.op("a", lambda a, quad=quad, ec=ec, ti=ti: a.copy(
                        sl512(btl, ti), sl512(quad, ec)), inc=cA)
                    va = P.val(cA)
                    P.wait("y", cA, va)
                    P.dma("y", dY,
                          v_scr[(sl * 2048 + tcb * 128):(sl * 2048 + (tcb + 1) * 128),
                                ec * 512:(ec + 1) * 512], sl512(btl, ti))
                bank_a[bk] = P.val(cA)

        P.wait("v", cT)
        P.wait("v", cV)
        P.op("v", lambda v: v.tensor_tensor(
            cells[0:1, 16:24], cells[0:1, 0:8], cells[0:1, 8:16], ALU.add), inc=cV)
        P.wait("v", cV)
        P.op("v", lambda v: v.tensor_scalar(
            cells[0:1, 16:24], cells[0:1, 16:24], SCALE / 2.0, 8.0, ALU.mult, ALU.add), inc=cV)
        v_M = P.val(cV)

        kh = [ar2[:, 0:4096], ar2[:, 4096:8192]]
        qh = [ar2[:, 8192:12288], ar2[:, 12288:16384]]
        vh = [ar2[:, 16384:20480], ar2[:, 20480:24576]]
        wal = [ar2[0:1, 24576:28672], ar2[0:1, 28672:32768]]

        def head_loads(h):
            i = h % 2
            P.wait("a", cT)
            P.dma("a", dS, kh[i], k_scr[h * 128:(h + 1) * 128, :])
            P.dma("a", dS, qh[i], q_scr[h * 128:(h + 1) * 128, :])
            if not is_b:
                P.wait("a", cV)
                P.dma("a", dS, wal[i], din["wal"][h])
            P.wait("y", cT)
            P.wait("y", cV)
            vsrc = v_scr[:, :].rearrange("(kb p) e -> p kb e", p=128)
            for piece in range(4):
                P.dma("y", dY,
                      vh[i].rearrange("p (kb e) -> p kb e", kb=32)[:, piece * 8:(piece + 1) * 8, :],
                      vsrc[:, piece * 8:(piece + 1) * 8, (h * 128):((h + 1) * 128)])
            return P.val(dS), P.val(dY)

        def build_bias_slab(h):
            P.wait("t", cV, v_M)
            P.wait("t", cA)
            P.op("t", lambda t, h=h: t.matmul(
                psB[:, 1024:1025], ones_rf[:, :], cells[0:1, 16 + h:17 + h],
                start=True, stop=True), inc=cT)
            vt = P.val(cT)
            P.wait("a", cT, vt)
            P.wait("a", cV)
            P.op("a", lambda a, h=h: a.copy(mcol[:, h % 2:h % 2 + 1], psB[:, 1024:1025]), inc=cA)
            va = P.val(cA)
            P.wait("v", cA, va)
            P.op("v", lambda v, h=h: v.tensor_scalar(
                biassl[:, (h % 2) * 32:(h % 2) * 32 + 32], alib[:, h * 32:(h + 1) * 32],
                mcol[:, h % 2:h % 2 + 1], None, ALU.subtract), inc=cV)
            return P.val(cV)

        sems = [None] * (NH + 1)
        bs_v = [None] * (NH + 1)
        sems[0] = head_loads(0)
        bs_v[0] = build_bias_slab(0)
        KBc = [0]
        exp_v = {}
        psb_free = [P.val(cV)]
        wrslot = [sgt[:, 0:512], sgt[:, 512:1024]]
        wr_ca = [0, 0]
        wr_cv = [0, 0]
        wr_mmca = [P.val(cA)]
        for h in range(NH):
            if h + 1 < NH:
                sems[h + 1] = head_loads(h + 1)
                bs_v[h + 1] = build_bias_slab(h + 1)
            vS, vY = sems[h]
            P.wait("t", dS, vS)
            P.wait("t", dY, vY)
            i = h % 2
            for qc in range(NTTC):
                nkb = 4 * (qc + 1)
                kg0 = KBc[0]
                if not is_b:
                    P.wait("t", cA, wr_mmca[0])
                    P.op("t", lambda t, i=i, qc=qc: t.matmul(
                        psB[:, 1536:2048], ones_rb[:, :],
                        wal[i][0:1, qc * 512:(qc + 1) * 512],
                        start=True, stop=True), inc=cT)
                    vwm = P.val(cT)
                    P.wait("a", cT, vwm)
                    P.wait("a", cV, wr_cv[qc % 2])
                    P.op("a", lambda a, qc=qc: a.copy(wrslot[qc % 2], psB[:, 1536:2048]), inc=cA)
                    wr_ca[qc % 2] = P.val(cA)
                    wr_mmca[0] = wr_ca[qc % 2]

                def u_scores(kb, kg, i=i, h=h, qc=qc):
                    sl = sl512(psA, kg % 4)
                    P.wait("t", cA, exp_v.get(kg - 4, 0))
                    P.op("t", lambda t, sl=sl, kb=kb, i=i, qc=qc: t.matmul(
                        sl, kh[i][:, kb * 128:(kb + 1) * 128],
                        qh[i][:, qc * 512:(qc + 1) * 512],
                        start=True, stop=True), inc=cT)
                    vsc = P.val(cT)
                    dj = kb - 4 * qc
                    vdve = None
                    if not is_b:
                        P.wait("v", cT, vsc)
                        P.wait("v", cA, wr_ca[qc % 2])
                        P.op("v", lambda v, sl=sl, qc=qc: v.tensor_tensor(
                            sl, sl, wrslot[qc % 2], ALU.add), inc=cV)
                        vdve = P.val(cV)
                    if dj >= 0:
                        P.wait("v", cT, vsc)
                        P.op("v", lambda v, sl=sl, dj=dj: v.tensor_tensor(
                            sl, sl, maskb[:, 384 - 128 * dj:896 - 128 * dj],
                            ALU.add), inc=cV)
                        vdve = P.val(cV)
                    P.wait("a", cT, vsc)
                    if vdve is not None:
                        P.wait("a", cV, vdve)
                    P.wait("a", cV, bs_v[h])
                    P.op("a", lambda a, sl=sl, kg=kg, kb=kb, h=h: a.activation(
                        sl512(ptl, kg % 4), sl,
                        AF.Exp, bias=biassl[:, (h % 2) * 32 + kb:(h % 2) * 32 + kb + 1],
                        scale=SCALE), inc=cA)
                    exp_v[kg] = P.val(cA)

                def u_sums_av(kb, kg, first, i=i):
                    P.wait("t", cA, exp_v[kg])
                    if first:
                        P.wait("t", cV, psb_free[0])
                    P.op("t", lambda t, kg=kg, first=first: t.matmul(
                        psB[0:1, 512:1024], ones_c[:, :], sl512(ptl, kg % 4),
                        start=first, stop=True), inc=None)
                    P.op("t", lambda t, kg=kg, kb=kb, i=i, first=first: t.matmul(
                        psB[:, 0:512], vh[i][:, kb * 128:(kb + 1) * 128],
                        sl512(ptl, kg % 4),
                        start=first, stop=True), inc=cT)

                for kb in range(nkb):
                    u_scores(kb, kg0 + kb)
                    if kb >= 2:
                        u_sums_av(kb - 2, kg0 + kb - 2, kb - 2 == 0)
                for kb in range(max(0, nkb - 2), nkb):
                    u_sums_av(kb, kg0 + kb, kb == 0)
                KBc[0] = kg0 + nkb
                if not is_b:
                    wr_cv[qc % 2] = P.val(cV)

                v_avdone = P.val(cT)
                P.wait("v", cT, v_avdone)
                P.op("v", lambda v: v.reciprocal(rowt[:, :], psB[0:1, 512:1024]), inc=cV)
                vrec = P.val(cV)
                P.wait("t", cV, vrec)
                P.op("t", lambda t: t.matmul(
                    psB[:, 1024:1536], ones_rf[:, :], rowt[0:1, :], start=True, stop=True), inc=cT)
                vrep = P.val(cT)
                P.wait("a", cT, vrep)
                P.wait("a", cV)
                P.op("a", lambda a, qc=qc: a.copy(sl512(fst2, qc % 2), psB[:, 1024:1536]), inc=cA)
                vrepc = P.val(cA)
                ctile = sl512(btl, qc % 6)
                P.wait("v", cA, vrepc)
                P.wait("v", dY)
                P.op("v", lambda v, qc=qc, ctile=ctile: v.tensor_tensor(
                    ctile, psB[:, 0:512], sl512(fst2, qc % 2), ALU.mult), inc=cV)
                vctx = P.val(cV)
                psb_free[0] = vctx
                P.wait("y", cV, vctx)
                P.dma("y", dY, c_scr[h * 128:(h + 1) * 128, qc * 512:(qc + 1) * 512], ctile)

        v_cwr = P.val(dY)
        P.wait("g", dY, v_cwr)
        P.wait("g", dS)
        P.op("g", lambda g: g.collective_compute(
            "AllGather", ALU.bypass,
            replica_groups=[[0, 1], [2, 3], [4, 5], [6, 7]],
            ins=[c_scr], outs=[c_gth]), inc=cc)
        v_cc2 = P.val(cc)

        ctxb = ar1[:, :].rearrange("p (dc t) -> p dc t", dc=NDC)
        P.wait("y", cc, v_cc2)
        P.wait("y", cT)
        for piece in range(4):
            def dma_piece(sy, piece=piece):
                sy.reg_add(rtmp, rreg, piece * 4 * 128 * TT)
                return sy.dma_start(
                    out=ctxb[:, piece * 4:(piece + 1) * 4, :],
                    in_=bass.AP(c_gth.tensor, rtmp, [[TT, 128], [128 * TT, 4], [1, T]]))
            P.op("y", dma_piece, inc=dY, n=16)
        v_ctxld = P.val(dY)
        P.wait("t", dY, v_ctxld)

        P.wait("t", cA)
        bank_v = [P.val(cV), P.val(cV)]
        for oc in range(16):
            si = ldi[0] % 3
            ldi[0] += 1
            s_ = wslot[si]
            P.wait("g", cT, wfree[si])
            vw = wload(s_, wo_i[oc])
            bk = oc % 2
            quad = psA if bk == 0 else psB
            P.wait("t", dW, vw)
            P.wait("t", cV, bank_v[bk])
            for dc in range(NDC):
                for tq in range(4):
                    P.op("t", lambda t, quad=quad, s_=s_, dc=dc, tq=tq: t.matmul(
                        sl512(quad, tq), s_[:, dc * 128:(dc + 1) * 128],
                        ctxb[:, dc, tq * 512:(tq + 1) * 512],
                        start=(dc == 0), stop=(dc == NDC - 1)),
                        inc=(cT if (dc == NDC - 1 and tq == 3) else None))
            vq = P.val(cT)
            wfree[si] = vq
            for tq in range(4):
                i = (oc * 4 + tq) % 2
                P.wait("y", cV)
                vx = P.dma("y", dY, sl512(xst, i),
                           x_prev[oc * 128:(oc + 1) * 128, tq * 512:(tq + 1) * 512])
                P.wait("v", cT, vq)
                P.wait("v", dY, vx)
                if is_b:
                    P.op("v", lambda v, quad=quad, tq=tq, i=i: v.tensor_tensor(
                        sl512(fst, i), sl512(quad, tq), sl512(xst, i), ALU.add), inc=cV)
                else:
                    P.op("v", lambda v, quad=quad, tq=tq, i=i, oc=oc: v.scalar_tensor_tensor(
                        sl512(fst, i), sl512(quad, tq), bob[:, oc:oc + 1],
                        sl512(xst, i), ALU.add, ALU.add), inc=cV)
                vv = P.val(cV)
                P.wait("y", cV, vv)
                P.dma("y", dY, x_next[oc * 128:(oc + 1) * 128, tq * 512:(tq + 1) * 512],
                      sl512(fst, i))
            bank_v[bk] = P.val(cV)

    def ffn_sublayer(x_prev, x_next):
        st_cv, st_ca = stats(x_prev)
        y_produce(x_prev, None, st_cv, st_ca)
        y2 = ar1[:, :].rearrange("p (dc t) -> p dc t", dc=NDC)
        g_sl = ar2[:, :].rearrange("p (hc t) -> p hc t", hc=16)
        v_y2 = P.val(cV)
        P.wait("t", cV, v_y2)

        ff_wfree = [P.val(cT)] * 3
        fli = [0]
        xld_cv = [0] * 4
        for slab in range(4):
            for hc in range(16):
                habs = slab * 16 + hc
                gi = fli[0] % 3
                fli[0] += 1
                ui = fli[0] % 3
                fli[0] += 1
                sg_ = wslot[gi]
                su_ = wslot[ui]
                P.wait("g", cT, ff_wfree[gi])
                vwg = wload(sg_, din["wg"][habs])
                P.wait("g", cT, ff_wfree[ui])
                vwu = wload(su_, din["wu"][habs])
                P.wait("t", dW, vwg)
                P.wait("t", cA)
                for dc in range(NDC):
                    for tq in range(4):
                        P.op("t", lambda t, s_=sg_, dc=dc, tq=tq: t.matmul(
                            sl512(psA, tq), s_[:, dc * 128:(dc + 1) * 128],
                            y2[:, dc, tq * 512:(tq + 1) * 512],
                            start=(dc == 0), stop=(dc == NDC - 1)),
                            inc=(cT if (dc == NDC - 1 and tq == 3) else None))
                vg = P.val(cT)
                ff_wfree[gi] = vg
                P.wait("t", dW, vwu)
                P.wait("t", cV)
                for dc in range(NDC):
                    for tq in range(4):
                        P.op("t", lambda t, s_=su_, dc=dc, tq=tq: t.matmul(
                            sl512(psB, tq), s_[:, dc * 128:(dc + 1) * 128],
                            y2[:, dc, tq * 512:(tq + 1) * 512],
                            start=(dc == 0), stop=(dc == NDC - 1)),
                            inc=(cT if (dc == NDC - 1 and tq == 3) else None))
                vu = P.val(cT)
                ff_wfree[ui] = vu
                P.wait("a", cT, vg)
                P.wait("a", cV)
                P.op("a", lambda a: a.activation(sgt[:, :], psA[:, :], AF.Silu), inc=cA)
                vsilu = P.val(cA)
                P.wait("v", cA, vsilu)
                P.wait("v", cT, vu)
                P.op("v", lambda v, hc=hc: v.tensor_tensor(
                    g_sl[:, hc, :], sgt[:, :], psB[:, :], ALU.mult), inc=cV)
            v_g = P.val(cV)

            dn_a = [P.val(cA), P.val(cA)]
            dn_v = [P.val(cV), P.val(cV)]
            dn_w = []

            def dn_load(oc2, slab=slab, dn_w=dn_w):
                si2 = fli[0] % 3
                fli[0] += 1
                s2 = wslot[si2]
                P.wait("g", cT, ff_wfree[si2])
                vw2 = wload(s2, din["wd"][oc2][:, slab * 2048:(slab + 1) * 2048])
                dn_w.append((si2, s2, vw2))

            for oc in range(16):
                dn_load(oc)
                si, s_, vw = dn_w[oc]
                bk = oc % 2
                quad = psA if bk == 0 else psB
                P.wait("t", dW, vw)
                P.wait("t", cV, v_g)
                P.wait("t", cA)
                P.wait("t", cV)
                for hc in range(16):
                    for tq in range(4):
                        P.op("t", lambda t, quad=quad, s_=s_, hc=hc, tq=tq: t.matmul(
                            sl512(quad, tq), s_[:, hc * 128:(hc + 1) * 128],
                            g_sl[:, hc, tq * 512:(tq + 1) * 512],
                            start=(hc == 0), stop=(hc == 15)),
                            inc=(cT if (hc == 15 and tq == 3) else None))
                vdn = P.val(cT)
                ff_wfree[si] = vdn
                for tq in range(4):
                    u = oc * 4 + tq
                    sl4 = u % 4
                    qe, qs = ("y", dY) if u % 2 == 0 else ("a", dS)
                    dst = x_next[oc * 128:(oc + 1) * 128, tq * 512:(tq + 1) * 512]
                    srcx = (x_prev[oc * 128:(oc + 1) * 128, tq * 512:(tq + 1) * 512]
                            if slab == 0 else dst)
                    ftile = sl512(fst, sl4) if sl4 < 2 else sl512(fst2, sl4 - 2)
                    P.wait(qe, cV, xld_cv[sl4])
                    vx = P.dma(qe, qs, sl512(xst, sl4), srcx)
                    P.wait("v", cT, vdn)
                    P.wait("v", qs, vx)
                    P.op("v", lambda v, quad=quad, tq=tq, ftile=ftile, sl4=sl4: v.tensor_tensor(
                        ftile, sl512(quad, tq), sl512(xst, sl4), ALU.add), inc=cV)
                    xld_cv[sl4] = P.val(cV)
                    P.wait(qe, cV, xld_cv[sl4])
                    P.dma(qe, qs, dst, ftile)
                dn_a[bk] = P.val(cA)
                dn_v[bk] = P.val(cV)

    seq = [
        ("attn", x0, x_scr[0], False),
        ("ffn", x_scr[0], x_scr[1], None),
        ("attn", x_scr[1], x_scr[2], True),
        ("ffn", x_scr[2], out_t, None),
    ][:N_SUB]
    if N_SUB < 4:
        kind, xp, _, fl = seq[-1]
        seq[-1] = (kind, xp, out_t, fl)

    for kind, xp, xn, fl in seq:
        for e in ("y", "a", "g"):
            barrier_dma(e)
        if kind == "attn":
            attn_sublayer(xp, xn, fl)
        else:
            ffn_sublayer(xp, xn)

    for s in (dW, dY, dS):
        P.wait("y", s)
    P.wait("y", cc)

    if os.environ.get("MK_DEBUG") == "1":
        dbg_specs = [
            ("dbg_y", y_scr, [D, T], BF16), ("dbg_ygth", y_gth, [2 * D, T], BF16),
            ("dbg_q", q_scr, [NH * 128, TT], BF16), ("dbg_k", k_scr, [NH * 128, TT], BF16),
            ("dbg_v", v_scr, [TT, NH * 128], BF16), ("dbg_c", c_scr, [NH * 128, TT], BF16),
            ("dbg_cgth", c_gth, [D, TT], BF16), ("dbg_x1", x_scr[0], [D, T], F32),
            ("dbg_x2", x_scr[1], [D, T], F32), ("dbg_x3", x_scr[2], [D, T], F32),
        ]
        for nm, src, shp, dt in dbg_specs:
            dap = nc.dram_tensor(nm, shp, dt, kind="ExternalOutput").ap()
            P.dma("y", dY, dap, src)
        ar1d = nc.dram_tensor("dbg_ar1", [128, 32768], BF16, kind="ExternalOutput").ap()
        P.dma("y", dY, ar1d, ar1[:, :])
        P.wait("y", dY)

    with nc.Block() as block:
        @block.tensor
        def _(t):
            for f in P.q["t"]:
                f(t)

        @block.scalar
        def _(a):
            for f in P.q["a"]:
                f(a)

        @block.vector
        def _(v):
            for f in P.q["v"]:
                f(v)

        @block.gpsimd
        def _(g):
            for f in P.q["g"]:
                f(g)

        @block.sync
        def _(sy):
            for f in P.q["y"]:
                f(sy)

    es.close()
    return nc


_NC_CACHE = {}


def _blocks_img(w):
    O = w.shape[1]
    nb = O // 128
    out = np.empty((nb, 128, NDC * 128), np.float32)
    for oc in range(nb):
        blk = w[:, oc * 128:(oc + 1) * 128].reshape(NDC, 128, 128)
        out[oc] = blk.transpose(1, 0, 2).reshape(128, NDC * 128)
    return np.ascontiguousarray(out)


def _img(w):
    K, N = w.shape
    return np.ascontiguousarray(
        w.reshape(K // 128, 128, N).transpose(1, 0, 2).reshape(128, (K // 128) * N))


def _prep_inputs(inputs):
    f = lambda k: np.asarray(inputs[k], np.float32)
    x = f("L0_attn_a_x")
    wqa, wka, wva, woa = f("L0_attn_a_Wq"), f("L0_attn_a_Wk"), f("L0_attn_a_Wv"), f("L0_attn_a_Wo")
    bqa, bka, bva, boa = f("L0_attn_a_bq"), f("L0_attn_a_bk"), f("L0_attn_a_bv"), f("L0_attn_a_bo")
    wqb, wkb, wvb, wob = f("L0_attn_b_Wq"), f("L0_attn_b_Wk"), f("L0_attn_b_Wv"), f("L0_attn_b_Wo")
    wg, wu, wd = f("L0_ffn_w_gate"), f("L0_ffn_w_up"), f("L0_ffn_w_down")

    pos = np.arange(TT, dtype=np.float64)
    invf = 1.0 / (10000.0 ** (np.arange(64, dtype=np.float64) / 64.0))
    ang = pos[None, :] * invf[:, None]
    cosv = np.cos(ang); sinv = np.sin(ang)
    csc = np.ascontiguousarray(np.concatenate([cosv, cosv], 0).astype(ml_dtypes.bfloat16))
    css = np.ascontiguousarray(np.concatenate([-sinv, sinv], 0).astype(ml_dtypes.bfloat16))
    pswap = np.zeros((128, 128), np.float32)
    for jj in range(128):
        pswap[(jj + 64) % 128, jj] = 1.0
    pswap = np.ascontiguousarray(pswap.astype(ml_dtypes.bfloat16))
    ku = np.arange(128)[:, None]
    uu = np.arange(896)[None, :]
    maskc = np.ascontiguousarray(np.where(uu >= ku + 384, 0.0, -1e30).astype(ml_dtypes.bfloat16))

    wg_b = _blocks_img(np.ascontiguousarray(wg.T))
    wu_b = _blocks_img(np.ascontiguousarray(wu.T))
    wdT = np.ascontiguousarray(wd.T)
    wd_b = np.stack([_img(np.ascontiguousarray(wdT[:, oc * 128:(oc + 1) * 128]))
                     for oc in range(16)])
    woa_b = _blocks_img(woa)
    wob_b = _blocks_img(np.ascontiguousarray(wob.T))
    wqbT = np.ascontiguousarray(wqb.T); wkbT = np.ascontiguousarray(wkb.T)
    wvbT = np.ascontiguousarray(wvb.T)

    in_maps = []
    for c in range(8):
        b, p = c // 2, c % 2
        tok = slice(p * T, (p + 1) * T)
        hh = slice(p * 1024, (p + 1) * 1024)
        habs = np.arange(p * 8, p * 8 + 8)
        slopes = (2.0 ** (-8.0 * (habs + 1) / 16.0))
        ali = np.empty((128, NH * 32), np.float32)
        kb = np.arange(32)[None, :]
        ii = np.arange(128)[:, None]
        for j in range(NH):
            ali[:, j * 32:(j + 1) * 32] = slopes[j] * (kb * 128 + ii - (TT - 1))
        qq = np.arange(TT, dtype=np.float64)
        wal_t = np.ascontiguousarray(
            (slopes[:, None] * ((TT - 1) - qq[None, :]) / SCALE).astype(ml_dtypes.bfloat16)
        ).reshape(NH, 1, TT)
        m = {
            "x0": np.ascontiguousarray(x[b].T[:, tok]),
            "wqa": _blocks_img(np.ascontiguousarray(wqa[:, hh])),
            "wka": _blocks_img(np.ascontiguousarray(wka[:, hh])),
            "wva": _img(np.ascontiguousarray(wva[:, hh])),
            "woa": woa_b,
            "bqa": np.ascontiguousarray(bqa[hh].reshape(8, 128).T),
            "bka": np.ascontiguousarray(bka[hh].reshape(8, 128).T),
            "bva": np.ascontiguousarray(bva[hh].reshape(1, 1024)),
            "boa": np.ascontiguousarray(boa.reshape(16, 128).T),
            "wqb": _blocks_img(np.ascontiguousarray(wqbT[:, hh])),
            "wkb": _blocks_img(np.ascontiguousarray(wkbT[:, hh])),
            "wvb": _img(np.ascontiguousarray(wvbT[:, hh])),
            "wob": wob_b,
            "wg": wg_b, "wu": wu_b, "wd": wd_b,
            "alibi": np.ascontiguousarray(ali),
            "csc": csc, "css": css, "pswap": pswap, "maskc": maskc,
            "colsel": np.array([[p * T]], np.int32),
            "wal": wal_t,
        }
        in_maps.append(m)
    return in_maps


def kernel(**inputs):
    in_maps = _prep_inputs(inputs)
    if "nc" not in _NC_CACHE:
        _NC_CACHE["nc"] = build_nc()
    nc = _NC_CACHE["nc"]
    res = run_bass_kernel_spmd(nc, in_maps, core_ids=list(range(8)))
    out = np.empty((4, TT, D), np.float32)
    for c in range(8):
        b, p = c // 2, c % 2
        out[b, p * T:(p + 1) * T, :] = np.asarray(res.results[c]["out"]).T
    return out
